# version 15
# baseline (speedup 1.0000x reference)
import numpy as np
import ml_dtypes

import concourse.bass as bass
import concourse.mybir as mybir
from concourse.bass_utils import run_bass_kernel_spmd
from concourse.bass_core_rust import IndirectOffsetOnAxis

F32 = mybir.dt.float32
I32 = mybir.dt.int32
BF16 = mybir.dt.bfloat16
AF = mybir.ActivationFunctionType
ALU = mybir.AluOpType
AX = mybir.AxisListType

D = 1024
DCN = 8
FF = 4096
FCN = 32
NE = 8
NH = 16
HD = 64
NEG = -1.0e30
BIG = float(1 << 20)
BIG2 = float(1 << 22)


class Cfg:
    def __init__(self, own_tiles=32, chunk_tiles=8):
        self.OT = own_tiles
        self.CT = chunk_tiles
        assert own_tiles % chunk_tiles == 0
        self.NCHUNK = own_tiles // chunk_tiles
        self.SLOTS = chunk_tiles + 2
        self.SLAB = own_tiles + 2
        self.KC = chunk_tiles * 128 + 64
        self.VT = chunk_tiles + 1



BLOCKNAME = {"pe": "tensor", "dve": "vector", "act": "scalar",
             "pool": "gpsimd", "sp": "sync"}
DMA_ENG = {"Q_sp": "sp", "Q_act": "act", "Q_pool": "pool"}


def _handle(x):
    return x.tensor if isinstance(x, bass.AP) else x


class Prog:
    def __init__(self, nc):
        self.nc = nc
        self.plans = {k: [] for k in BLOCKNAME}
        self.sems = {}
        self.cnt = {}
        self.clock = {e: {} for e in BLOCKNAME}
        self.lastw = {}
        self.readers = {}
        self.qlane = {}
        self.cur_phase = "init"
        self.n_ops = 0
        self._keycache = {}

    def keys(self, x):
        if isinstance(x, str):
            return [x]
        h = _handle(x)
        name = h.name
        got = self._keycache.get(name)
        if got is None:
            if getattr(h, "space", None) is not None and "PSUM" in str(h.space):
                ml = self.nc.lookup_mls(h).memorylocations[0]
                nbytes = int(ml.dims[1])
                b0 = int(ml.bank)
                nb = (nbytes + 2047) // 2048
                got = [f"PSB{b}" for b in range(b0, b0 + nb)]
            else:
                got = [name]
            self._keycache[name] = got
        return got

    def sem(self, name):
        if name not in self.sems:
            self.sems[name] = self.nc.alloc_semaphore(name)
            self.cnt[name] = 0
        return self.sems[name]

    def _wait(self, eng, need):
        plan = self.plans[eng]
        clk = self.clock[eng]
        for s, v in sorted(need.items()):
            if clk.get(s, 0) >= v:
                continue
            plan.append(("w", s, v))
            clk[s] = v

    def emit(self, eng, fns, reads=(), writes=(), dma=None):
        if not isinstance(fns, (list, tuple)):
            fns = [fns]
        rk = [k for r in reads for k in self.keys(r)]
        wk = [k for w in writes for k in self.keys(w)]
        need = {}
        for k in rk:
            lw = self.lastw.get(k)
            if lw:
                need[lw[0]] = max(need.get(lw[0], 0), lw[1])
        for k in wk:
            lw = self.lastw.get(k)
            if lw:
                need[lw[0]] = max(need.get(lw[0], 0), lw[1])
            for s, v in self.readers.get(k, {}).items():
                need[s] = max(need.get(s, 0), v)
        if dma is not None:
            assert len(fns) == 1
            lane = self.qlane[dma] = (self.qlane.get(dma, -1) + 1) % 8
            sname = f"{dma}_{lane}"
            issuer = DMA_ENG[dma]
            self.sem(sname)
            if self.cnt[sname] > 0:
                need[sname] = max(need.get(sname, 0), self.cnt[sname])
        else:
            sname = "P_" + eng
            issuer = eng
            self.sem(sname)
        self._wait(issuer, need)
        plan = self.plans[issuer]
        n = len(fns)
        for i, fn in enumerate(fns):
            if dma is not None:
                self.cnt[sname] += 16
                plan.append(("op", fn, sname, 16))
            else:
                inc = 1 if i == n - 1 else 0
                if inc:
                    self.cnt[sname] += 1
                plan.append(("op", fn, sname, inc))
            self.n_ops += 1
        final = (sname, self.cnt[sname])
        for k in wk:
            self.lastw[k] = final
            self.readers[k] = {}
        for k in rk:
            self.readers.setdefault(k, {})[final[0]] = final[1]

    def claim(self, writes):
        self.emit("pool", lambda g: g.nop(), reads=(), writes=writes)

    def raw_wait(self, eng, sname, val):
        self._wait(eng, {sname: val})

    def phase(self, name):
        self._flush()
        self.cur_phase = name

    def finish(self):
        for q in ("Q_sp", "Q_act", "Q_pool"):
            for lane in range(8):
                s = f"{q}_{lane}"
                if s in self.sems and self.cnt[s] > 0:
                    self.raw_wait("pool", s, self.cnt[s])
        self._flush()

    def _flush(self):
        plans = self.plans
        if not any(plans.values()):
            return
        self.plans = {k: [] for k in BLOCKNAME}
        sems = self.sems

        def mk_replay(plan):
            def replay(eng):
                for item in plan:
                    if item[0] == "w":
                        eng.wait_ge(sems[item[1]], item[2])
                    else:
                        ins = item[1](eng)
                        if item[3]:
                            ins.then_inc(sems[item[2]], item[3])
            return replay

        with self.nc.Block() as block:
            for ename, plan in plans.items():
                if plan:
                    getattr(block, BLOCKNAME[ename])(mk_replay(plan))



def build_program(cfg, debug_stage=None, stop_after=None):
    nc = bass.Bass("TRN2", target_bir_lowering=False, num_swdge_queues=4)
    SLOTS, KC, VT, CT = cfg.SLOTS, cfg.KC, cfg.VT, cfg.CT
    XNC = SLOTS * 128

    slab_rows = cfg.SLAB * 128
    d_x = nc.dram_tensor("x_slab", [slab_rows, D], F32, kind="ExternalInput").ap()
    d_mask = nc.dram_tensor("kvmask", [slab_rows], BF16, kind="ExternalInput").ap()
    d_qw = nc.dram_tensor("q_w", [D, D], F32, kind="ExternalInput").ap()
    d_qb = nc.dram_tensor("q_b", [D], F32, kind="ExternalInput").ap()
    d_kw = nc.dram_tensor("k_w", [D, D], F32, kind="ExternalInput").ap()
    d_kb = nc.dram_tensor("k_b", [D], F32, kind="ExternalInput").ap()
    d_vw = nc.dram_tensor("v_w", [D, D], F32, kind="ExternalInput").ap()
    d_vb = nc.dram_tensor("v_b", [D], F32, kind="ExternalInput").ap()
    d_ow = nc.dram_tensor("o_w", [D, D], F32, kind="ExternalInput").ap()
    d_ob = nc.dram_tensor("o_b", [D], F32, kind="ExternalInput").ap()
    d_w1 = nc.dram_tensor("W1", [NE, D, FF], F32, kind="ExternalInput").ap()
    d_b1 = nc.dram_tensor("b1", [NE, FF], F32, kind="ExternalInput").ap()
    d_w2 = nc.dram_tensor("W2", [NE, FF, D], F32, kind="ExternalInput").ap()
    d_b2 = nc.dram_tensor("b2", [NE, D], F32, kind="ExternalInput").ap()
    d_wg = nc.dram_tensor("Wg", [D, NE], F32, kind="ExternalInput").ap()
    d_bg = nc.dram_tensor("bg", [NE], F32, kind="ExternalInput").ap()
    d_y = nc.dram_tensor("y", [cfg.OT * 128, D], F32, kind="ExternalOutput").ap()

    c_w1 = nc.dram_tensor("c_w1", [NE, D, FF], BF16, kind="Internal").ap()
    c_w2 = nc.dram_tensor("c_w2", [NE, FF, D], BF16, kind="Internal").ap()
    xb_dram = nc.dram_tensor("xb_dram", [NE * 640, D], BF16, kind="Internal").ap()
    y_raw = nc.dram_tensor("y_raw", [NE * 640, D], F32, kind="Internal").ap()
    c_vbo = nc.dram_tensor("c_vbo", [2, D], BF16, kind="Internal").ap()
    c_q = nc.dram_tensor("c_q", [8, 128, D], BF16, kind="Internal").ap()
    c_k = nc.dram_tensor("c_k", [8, 128, D], BF16, kind="Internal").ap()
    c_v = nc.dram_tensor("c_v", [8, 128, D], BF16, kind="Internal").ap()
    c_o = nc.dram_tensor("c_o", [8, 128, D], BF16, kind="Internal").ap()
    if debug_stage is not None:
        d_dbg = nc.dram_tensor("dbg", [SLOTS * 128, D], F32,
                               kind="ExternalOutput").ap()

    off = [17408]
    sb = {}
    sb_off = {}

    def alloc(name, shape, dtype, at=None):
        nbytes = int(np.prod(shape[1:])) * (2 if dtype == BF16 else 4)
        if at is None:
            o = off[0]
            off[0] += (nbytes + 31) & ~31
        else:
            o = at
        t = nc.alloc_sbuf_tensor_at(name, list(shape), dtype, offset=o)
        sb[name] = t
        sb_off[name] = o
        return t

    x_sb = alloc("x_sb", [128, SLOTS * 1024], F32)
    xnT = alloc("xnT", [128, DCN * XNC], BF16)
    off[0] = max(off[0], sb_off["xnT"] + 20480)
    w1A = alloc("w1A", [128, 16 * 1024], BF16)
    w2A = alloc("w2A", [128, 16 * 1024], BF16)
    WB_off = off[0]
    w1B = alloc("w1B", [128, 16 * 1024], BF16)
    w2B = alloc("w2B", [128, 16 * 1024], BF16)
    WB_end = off[0]
    hT = [alloc("hT0", [128, 256], BF16), alloc("hT1", [128, 256], BF16),
          alloc("hT2", [128, 256], BF16)]
    aT_sb = alloc("aT_sb", [128, 1024], BF16)
    stats_t = alloc("stats", [128, 32], F32)
    gtmp_t = alloc("gtmp", [128, 96], F32)
    s_gate = alloc("s_gate", [128, SLOTS * 8], F32)
    sT_bf = alloc("sT_bf", [8, 128], BF16)
    b1T = alloc("b1T", [128, FCN * 8], F32)
    qbT = alloc("qbT", [128, 8], F32)
    kbT = alloc("kbT", [128, 8], F32)
    qb8T = alloc("qb8T", [128, 8], F32)
    b2_bf = alloc("b2_bf", [8, D], BF16)
    vb_bf = alloc("vb_bf", [1, D], BF16)
    ob_bf = alloc("ob_bf", [1, D], BF16)
    bg_f = alloc("bg_f", [1, 8], F32)
    ones_f = alloc("ones_f", [1, 128], F32)
    ones_bf = alloc("ones_bf", [1, 128], BF16)
    identf = alloc("identf", [128, 128], F32)
    identb = alloc("identb", [128, 128], BF16)
    wg_sb = alloc("wg_sb", [128, DCN * 8], F32)
    vmask = alloc("vmask", [128, VT], F32)
    band = alloc("band", [128, 192], F32)
    ltriu = alloc("ltriu", [128, 128], F32)
    ones_col = alloc("ones_col", [128, 1], F32)
    rdall = alloc("rdall", [128, SLOTS * 2], I32)
    sall = alloc("sall", [128, SLOTS * 2], F32)
    toff_row = alloc("toff_row", [1, 8], F32)
    rtmp_t = alloc("rtmp", [128, 64], F32)
    ysc = [alloc("ysc0", [128, 512], F32, at=sb_off["aT_sb"]),
           alloc("ysc1", [128, 512], F32)]
    ytmp = None
    persist_end = off[0]

    tmp4k_t = [alloc("tmp4k0", [128, 1024], F32, at=WB_off),
               alloc("tmp4k1", [128, 1024], F32, at=WB_off + 4096)]
    xnT32_t = [alloc("xnT320", [128, 1024], F32, at=WB_off + 8192),
               alloc("xnT321", [128, 1024], F32, at=WB_off + 12288)]
    ao = WB_off
    kT = alloc("kT", [128, DCN * KC], BF16, at=ao); ao += DCN * KC * 2
    v_sb = alloc("v_sb", [128, VT * 1024], BF16, at=ao); ao += VT * 1024 * 2
    wattn = alloc("wattn", [128, 8 * 1024], BF16, at=ao); ao += 8 * 1024 * 2
    maskrep = alloc("maskrep", [128, KC], F32, at=ao); ao += KC * 4
    kmask_bf = alloc("kmask_bf", [1, KC + 64], BF16, at=ao); ao += (((KC + 64) * 2 + 31) & ~31)
    Sm, Ef, Eb, ETb = [], [], [], []
    for r in range(2):
        Sm.append(alloc(f"Sm{r}", [128, 192], F32, at=ao)); ao += 768
        Ef.append(alloc(f"Ef{r}", [128, 192], F32, at=ao)); ao += 768
        Eb.append(alloc(f"Eb{r}", [128, 192], BF16, at=ao)); ao += 384
        ETb.append(alloc(f"ET{r}", [128, 256], BF16, at=ao)); ao += 512
    astat = alloc("astat", [128, 8], F32, at=ao); ao += 32
    assert ao <= WB_end, f"attn overlay too big: {ao - WB_off}"
    XN_off = sb_off["xnT"]
    xbT = [alloc("xbT0", [128, 4096], BF16, at=XN_off),
           alloc("xbT1", [128, 4096], BF16, at=XN_off + 8192)]
    xbstg = [alloc("xbstg0", [128, 1024], BF16, at=XN_off + 16384),
             alloc("xbstg1", [128, 1024], BF16, at=XN_off + 18432)]
    grb = [alloc("grb0", [128, 1024], F32, at=XN_off),
           alloc("grb1", [128, 1024], F32, at=XN_off + 4096)]
    ytmp = [alloc("ytmp0", [128, 512], F32, at=sb_off["vb_bf"]),
            alloc("ytmp1", [128, 512], F32, at=sb_off["ob_bf"])]
    XN_ALL = [xnT, xbT[0], xbT[1], xbstg[0], xbstg[1], grb[0], grb[1],
              aT_sb, ysc[0], ytmp[0], ytmp[1], vb_bf, ob_bf]

    WB2 = WB_off + 16384
    b1nat = alloc("b1nat", [8, FF], F32, at=WB2)
    rowld = alloc("rowld", [8, D], F32, at=WB2 + 16384)
    qst = [alloc("qst0", [128, D], F32, at=WB2),
           alloc("qst1", [128, D], F32, at=WB2 + 4096)]
    qso = [alloc("qso0", [128, D], BF16, at=WB2 + 8192),
           alloc("qso1", [128, D], BF16, at=WB2 + 10240)]
    tstg = alloc("tstg", [128, 8 * 1024], BF16, at=WB2 + 12288)
    cva4 = [alloc(f"cv{i}", [128, 2048], F32, at=WB2 + i * 8192)
            for i in range(4)]
    cvo4 = [alloc(f"cvo{i}", [128, 2048], BF16, at=WB2 + 32768 + i * 4096)
            for i in range(4)]
    cvA, cvB, cvOA, cvOB = cva4[0], cva4[1], cvo4[0], cvo4[1]
    assert WB2 + 49152 <= WB_end

    WB_ALL = [w1B, w2B, tmp4k_t[0], tmp4k_t[1], xnT32_t[0], xnT32_t[1],
              kT, v_sb, wattn, maskrep, kmask_bf,
              Sm[0], Sm[1], Ef[0], Ef[1], Eb[0], Eb[1], ETb[0], ETb[1], astat,
              cva4[0], cva4[1], cva4[2], cva4[3],
              cvo4[0], cvo4[1], cvo4[2], cvo4[3], tstg, b1nat, rowld,
              qst[0], qst[1], qso[0], qso[1]]

    avail = nc.sbuf_top
    assert persist_end <= avail, f"SBUF over budget: {persist_end} > {avail}"

    pr = Prog(nc)
    W1s = [w1A, w1B]
    W2s = [w2A, w2B]


    pr.emit("pool", lambda g: g.memset(ones_f[:, :], 1.0), writes=[ones_f])
    pr.emit("pool", lambda g: g.memset(ones_bf[:, :], 1.0), writes=[ones_bf])
    for ident in (identf, identb):
        pr.emit("pool", lambda g, t=ident: g.memset(t[:, :], 0.0), writes=[ident])
        pr.emit("pool", lambda g, t=ident: g.affine_select(
            out=t[:, :], in_=t[:, :], compare_op=ALU.not_equal, fill=1.0,
            base=0, pattern=[[-1, 128]], channel_multiplier=1),
            reads=[ident], writes=[ident])
    pr.emit("pool", lambda g: g.memset(ltriu[:, :], 1.0), writes=[ltriu])
    pr.emit("pool", lambda g: g.affine_select(
        out=ltriu[:, :], in_=ltriu[:, :], compare_op=ALU.is_ge, fill=0.0,
        base=0, pattern=[[1, 128]], channel_multiplier=-1),
        reads=[ltriu], writes=[ltriu])
    pr.emit("pool", lambda g: g.memset(ones_col[:, :], 1.0), writes=[ones_col])
    pr.emit("pool", lambda g: g.memset(xbstg[0][:, :], 0.0), writes=[xbstg[0]])
    for i in range(NE * 640 // 128):
        pr.emit("act", lambda a, i=i: a.dma_start(
            out=xb_dram[i * 128:(i + 1) * 128, :], in_=xbstg[0][:, :]),
            reads=[xbstg[0]], writes=[xb_dram], dma="Q_act")
    pr.emit("pool", lambda g: g.memset(grb[0][:, :], 0.0), writes=[grb[0]])
    for i in range(NE * 640 // 128):
        pr.emit("act", lambda a, i=i: a.dma_start(
            out=y_raw[i * 128:(i + 1) * 128, :], in_=grb[0][:, :]),
            reads=[grb[0]], writes=[y_raw], dma="Q_act")
    pr.emit("pool", lambda g: g.memset(band[:, :], 0.0), writes=[band])
    pr.emit("pool", lambda g: g.affine_select(
        out=band[:, :], in_=band[:, :], compare_op=ALU.is_ge, fill=NEG,
        base=0, pattern=[[1, 192]], channel_multiplier=-1),
        reads=[band], writes=[band])
    pr.emit("pool", lambda g: g.affine_select(
        out=band[:, :], in_=band[:, :], compare_op=ALU.is_ge, fill=NEG,
        base=64, pattern=[[-1, 192]], channel_multiplier=1),
        reads=[band], writes=[band])

    cva = [cvA, cvB]
    cvo = [cvOA, cvOB]
    cast_engs = ["pool", "dve", "act"]

    def load_x_slots(c, slots):
        for slot in slots:
            row0 = c * CT * 128 + slot * 128
            src = d_x[row0:row0 + 128, :]
            dst = x_sb[:, slot * 1024:(slot + 1) * 1024]
            pr.emit("act", lambda a, src=src, dst=dst: a.dma_start(
                out=dst, in_=src), reads=[d_x], writes=[x_sb], dma="Q_act")

    load_x_slots(0, list(range(SLOTS)))

    def copy_fn(eng, dst_ap, src_ap):
        if eng == "act":
            return lambda E, d=dst_ap, s=src_ap: E.copy(out=d, in_=s)
        return lambda E, d=dst_ap, s=src_ap: E.tensor_copy(out=d, in_=s)

    ci = 0
    with nc.psum_tensor("bp0", [128, 8], F32) as bp0, \
         nc.psum_tensor("bp1", [128, 8], F32) as bp1:
        bp = [bp0, bp1]
        pr.emit("sp", lambda _: _.dma_start(out=b1nat[:, :], in_=d_b1),
                reads=[d_b1], writes=[b1nat], dma="Q_sp")
        for fc in range(FCN):
            t = bp[fc % 2]
            pr.emit("pe", lambda T, t=t, fc=fc: T.transpose(
                t[:, :], b1nat[0:8, fc * 128:(fc + 1) * 128], identf[0:8, 0:8]),
                reads=[b1nat, identf], writes=[t])
            pr.emit("dve", lambda v, t=t, fc=fc: v.tensor_copy(
                out=b1T[:, fc * 8:(fc + 1) * 8], in_=t[:, :]),
                reads=[t], writes=[b1T])
        for d_b, tgt in ((d_qb, qbT), (d_kb, kbT)):
            pr.emit("sp", lambda _, d_b=d_b: _.dma_start(
                out=rowld[0:1, :], in_=d_b[None, :]),
                reads=[d_b], writes=[rowld], dma="Q_sp")
            for dc in range(DCN):
                t = bp[dc % 2]
                pr.emit("pe", lambda T, t=t, dc=dc: T.transpose(
                    t[:, 0:1], rowld[0:1, dc * 128:(dc + 1) * 128],
                    identf[0:1, 0:1]), reads=[rowld, identf], writes=[t])
                pr.emit("dve", lambda v, t=t, tgt=tgt, dc=dc: v.tensor_copy(
                    out=tgt[:, dc:dc + 1], in_=t[:, 0:1]),
                    reads=[t], writes=[tgt])
        pr.emit("dve", lambda v: v.tensor_scalar(
            out=qb8T[:, :], in0=qbT[:, :], scalar1=0.125, scalar2=None,
            op0=ALU.mult), reads=[qbT], writes=[qb8T])
        pr.emit("sp", lambda _: _.dma_start(out=rowld[:, :], in_=d_b2),
                reads=[d_b2], writes=[rowld], dma="Q_sp")
        pr.emit("dve", lambda v: v.tensor_copy(out=b2_bf[:, :], in_=rowld[:, :]),
                reads=[rowld], writes=[b2_bf])
        pr.emit("sp", lambda _: _.dma_start(out=rowld[0:1, :], in_=d_vb[None, :]),
                reads=[d_vb], writes=[rowld], dma="Q_sp")
        pr.emit("dve", lambda v: v.tensor_copy(out=vb_bf[:, :], in_=rowld[0:1, :]),
                reads=[rowld], writes=[vb_bf])
        pr.emit("sp", lambda _: _.dma_start(out=rowld[0:1, :], in_=d_ob[None, :]),
                reads=[d_ob], writes=[rowld], dma="Q_sp")
        pr.emit("dve", lambda v: v.tensor_copy(out=ob_bf[:, :], in_=rowld[0:1, :]),
                reads=[rowld], writes=[ob_bf])
        pr.emit("act", lambda a: a.dma_start(out=c_vbo[0:1, :], in_=vb_bf[:, :]),
                reads=[vb_bf], writes=[c_vbo], dma="Q_act")
        pr.emit("act", lambda a: a.dma_start(out=c_vbo[1:2, :], in_=ob_bf[:, :]),
                reads=[ob_bf], writes=[c_vbo], dma="Q_act")
        pr.emit("sp", lambda _: _.dma_start(out=bg_f[:, :], in_=d_bg[None, :]),
                reads=[d_bg], writes=[bg_f], dma="Q_sp")
        for dc in range(DCN):
            src = d_wg[dc * 128:(dc + 1) * 128, :]
            pr.emit("sp", lambda _, src=src, dc=dc: _.dma_start(
                out=wg_sb[:, dc * 8:(dc + 1) * 8], in_=src),
                reads=[d_wg], writes=[wg_sb], dma="Q_sp")

    pr.claim([b1nat, rowld, qst[0], qst[1], qso[0], qso[1], tstg])
    with nc.psum_tensor("trp0", [128, 128], BF16) as trp0, \
         nc.psum_tensor("trp1", [128, 128], BF16) as trp1:
        trp = [trp0, trp1]
        for d_w, c_t, flip in ((d_qw, c_q, False), (d_kw, c_k, False),
                               (d_vw, c_v, True), (d_ow, c_o, True)):
            for ic in range(8):
                s, o = qst[ci % 2], qso[ci % 2]
                src = d_w[ic * 128:(ic + 1) * 128, :]
                pr.emit("act", lambda a, s=s, src=src: a.dma_start(
                    out=s[:, 0:D], in_=src), reads=[d_w], writes=[s], dma="Q_act")
                pr.emit("pool", lambda g, s=s, o=o: g.tensor_copy(
                    out=o[:, 0:D], in_=s[:, 0:D]), reads=[s], writes=[o])
                for dc in range(DCN):
                    t = trp[dc % 2]
                    col = (dc * 1024 + ic * 128) if flip else (ic * 1024 + dc * 128)
                    pr.emit("pe", lambda T, o=o, t=t, dc=dc: T.transpose(
                        t[:, :], o[:, dc * 128:(dc + 1) * 128], identb[:, :]),
                        reads=[o, identb], writes=[t])
                    pr.emit("dve", lambda v, t=t, col=col: v.tensor_copy(
                        out=tstg[:, col:col + 128], in_=t[:, :]),
                        reads=[t], writes=[tstg])
                ci += 1
            for blk in range(8):
                dst = c_t[blk]
                pr.emit("act", lambda a, dst=dst, blk=blk: a.dma_start(
                    out=dst, in_=tstg[:, blk * 1024:(blk + 1) * 1024]),
                    reads=[tstg], writes=[c_t], dma="Q_act")

    pr.claim([qst[0], qst[1], qso[0], qso[1], tstg,
              cva4[0], cva4[1], cva4[2], cva4[3],
              cvo4[0], cvo4[1], cvo4[2], cvo4[3]])
    ci = 0
    for e in range(NE):
        for dc in range(DCN):
            for hh in range(2):
                s, o = cva4[ci % 4], cvo4[ci % 4]
                src = d_w1[e, dc * 128:(dc + 1) * 128,
                           hh * 2048:(hh + 1) * 2048]
                dst = c_w1[e, dc * 128:(dc + 1) * 128,
                           hh * 2048:(hh + 1) * 2048]
                pr.emit("sp", lambda _, s=s, src=src: _.dma_start(
                    out=s[:, :], in_=src), reads=[d_w1], writes=[s],
                    dma="Q_sp")
                pr.emit(cast_engs[ci % 3], copy_fn(cast_engs[ci % 3],
                                                   o[:, :], s[:, :]),
                        reads=[s], writes=[o])
                pr.emit("act", lambda a, o=o, dst=dst: a.dma_start(
                    out=dst, in_=o[:, :]), reads=[o], writes=[f"cw1_{e}"],
                    dma="Q_act")
                ci += 1
        for fq in range(FF // 256):
            s, o = cva4[ci % 4], cvo4[ci % 4]
            src = d_w2[e, fq * 256:(fq + 1) * 256, :].rearrange(
                "(a p) d -> p a d", p=128)
            dst = c_w2[e, fq * 256:(fq + 1) * 256, :].rearrange(
                "(a p) d -> p a d", p=128)
            pr.emit("sp", lambda _, s=s, src=src: _.dma_start(
                out=s[:, 0:2048].rearrange("p (a d) -> p a d", a=2), in_=src),
                reads=[d_w2], writes=[s], dma="Q_sp")
            pr.emit(cast_engs[ci % 3], copy_fn(cast_engs[ci % 3],
                                               o[:, 0:2048], s[:, 0:2048]),
                    reads=[s], writes=[o])
            pr.emit("act", lambda a, o=o, dst=dst: a.dma_start(
                out=dst, in_=o[:, 0:2048].rearrange("p (a d) -> p a d", a=2)),
                reads=[o], writes=[f"cw2_{e}"], dma="Q_act")
            ci += 1


    def w_prefetch(e, half, slot):
        w1t, w2t = W1s[slot], W2s[slot]
        src1 = c_w1[e, :, half * 2048:(half + 1) * 2048].rearrange(
            "(a p) f -> p a f", p=128)
        src2 = c_w2[e, half * 2048:(half + 1) * 2048, :].rearrange(
            "(a p) d -> p a d", p=128)
        pr.emit("act", lambda a, w1t=w1t, src1=src1: a.dma_start(
            out=w1t[:, :].rearrange("p (a f) -> p a f", a=8), in_=src1),
            reads=[f"cw1_{e}"], writes=[w1t], dma="Q_act")
        pr.emit("act", lambda a, w2t=w2t, src2=src2: a.dma_start(
            out=w2t[:, :].rearrange("p (a d) -> p a d", a=16), in_=src2),
            reads=[f"cw2_{e}"], writes=[w2t], dma="Q_act")

    def ln_tile(slot, do_gate, tag, par=0):
        stats = stats_t[:, par * 16:par * 16 + 16]
        statk = f"stats{par}"
        tmp4k = tmp4k_t[par]
        xnT32 = xnT32_t[par]
        x_t = x_sb[:, slot * 1024:(slot + 1) * 1024]
        xncol = slot * 128
        pr.emit("dve", lambda v, x_t=x_t: v.reduce_sum(
            out=stats[:, 0:1], in_=x_t, axis=AX.X), reads=[x_sb], writes=[statk])
        pr.emit("act", lambda a, x_t=x_t: a.activation(
            out=tmp4k[:, :], in_=x_t, func=AF.Square), reads=[x_sb],
            writes=[tmp4k])
        pr.emit("dve", lambda v: v.reduce_sum(
            out=stats[:, 1:2], in_=tmp4k[:, :], axis=AX.X),
            reads=[tmp4k], writes=[statk])
        pr.emit("dve", lambda v: v.tensor_scalar(
            out=stats[:, 2:3], in0=stats[:, 0:1], scalar1=1.0 / D, scalar2=None,
            op0=ALU.mult), reads=[statk], writes=[statk])
        pr.emit("dve", lambda v: v.tensor_tensor(
            out=stats[:, 3:4], in0=stats[:, 2:3], in1=stats[:, 2:3],
            op=ALU.mult), reads=[statk], writes=[statk])
        pr.emit("dve", lambda v: v.tensor_scalar(
            out=stats[:, 4:5], in0=stats[:, 1:2], scalar1=1.0 / D,
            scalar2=stats[:, 3:4], op0=ALU.mult, op1=ALU.subtract),
            reads=[statk], writes=[statk])
        pr.emit("dve", lambda v: v.tensor_scalar(
            out=stats[:, 4:5], in0=stats[:, 4:5], scalar1=1e-5, scalar2=None,
            op0=ALU.add), reads=[statk], writes=[statk])
        pr.emit("act", lambda a: a.activation(
            out=stats[:, 5:6], in_=stats[:, 4:5], func=AF.Sqrt),
            reads=[statk], writes=[statk])
        pr.emit("dve", lambda v: v.reciprocal(
            out=stats[:, 6:7], in_=stats[:, 5:6]), reads=[statk], writes=[statk])
        pr.emit("dve", lambda v, x_t=x_t: v.tensor_scalar(
            out=tmp4k[:, :], in0=x_t, scalar1=stats[:, 2:3],
            scalar2=stats[:, 6:7], op0=ALU.subtract, op1=ALU.mult),
            reads=[x_sb, statk], writes=[tmp4k])
        with nc.psum_tensor("trf0" + tag, [128, 128], F32) as t0, \
             nc.psum_tensor("trf1" + tag, [128, 128], F32) as t1:
            trp = [t0, t1]
            for dc in range(DCN):
                if not do_gate:
                    t = trp[dc % 2]
                    pr.emit("pe", lambda T, t=t, dc=dc: T.transpose(
                        t[:, :], tmp4k[:, dc * 128:(dc + 1) * 128],
                        identf[:, :]), reads=[tmp4k, identf], writes=[t])
                    pr.emit("dve", lambda v, t=t, dc=dc, xncol=xncol:
                            v.tensor_copy(
                        out=xnT[:, dc * XNC + xncol:dc * XNC + xncol + 128],
                        in_=t[:, :]), reads=[t], writes=[xnT])
                else:
                    t = trp[dc % 2]
                    pr.emit("pe", lambda T, t=t, dc=dc: T.transpose(
                        t[:, :], tmp4k[:, dc * 128:(dc + 1) * 128],
                        identf[:, :]), reads=[tmp4k, identf], writes=[t])
                    pr.emit("dve", lambda v, t=t, dc=dc: v.tensor_copy(
                        out=xnT32[:, dc * 128:(dc + 1) * 128], in_=t[:, :]),
                        reads=[t], writes=[xnT32])
        if do_gate:
            pr.emit("dve", lambda v: v.tensor_copy(
                out=xbstg[par][:, :], in_=tmp4k[:, :]),
                reads=[tmp4k], writes=[xbstg[par]])
            gate_tile(slot, tag, par, stats, xnT32)
            route_tile(slot, tag, par)

    def gate_tile(slot, tag, par, stats, xnT32):
        gtmp = gtmp_t[:, par * 48:par * 48 + 48]
        statk = f"stats{par}"
        gtk = f"gtmp{par}"
        with nc.psum_tensor("gp" + tag, [128, 8], F32) as gp, \
             nc.psum_tensor("sTp" + tag, [8, 128], F32) as sTp, \
             nc.psum_tensor("b2p" + tag, [128, 1024], F32) as b2p:
            fns = []
            for dc in range(DCN):
                fns.append(lambda T, dc=dc: T.matmul(
                    gp[:, :], xnT32[:, dc * 128:(dc + 1) * 128],
                    wg_sb[:, dc * 8:(dc + 1) * 8], start=(dc == 0), stop=False))
            fns.append(lambda T: T.matmul(
                gp[:, :], ones_f[:, :], bg_f[:, :], start=False, stop=True))
            pr.emit("pe", fns, reads=[xnT32, wg_sb, ones_f, bg_f], writes=[gp])
            L = gtmp[:, 0:8]
            pr.emit("dve", lambda v: v.tensor_copy(out=L, in_=gp[:, :]),
                    reads=[gp], writes=[gtk])
            c_m1 = stats[:, 8:9]
            c_cnt = stats[:, 9:10]
            c_t2 = stats[:, 10:11]
            c_t2e = stats[:, 11:12]
            c_nm1 = stats[:, 12:13]
            c_den = stats[:, 13:14]
            c_rd2 = stats[:, 14:15]
            c_c2 = stats[:, 15:16]
            t8a = gtmp[:, 8:16]
            t8b = gtmp[:, 16:24]
            t8c = gtmp[:, 24:32]
            t8d = gtmp[:, 32:40]
            pr.emit("dve", lambda v: v.reduce_max(out=c_m1, in_=L, axis=AX.X),
                    reads=[gtk], writes=[statk])
            pr.emit("dve", lambda v: v.tensor_scalar(
                out=t8a, in0=L, scalar1=c_m1, scalar2=None, op0=ALU.is_ge),
                reads=[gtk, statk], writes=[gtk])
            pr.emit("dve", lambda v: v.reduce_sum(out=c_cnt, in_=t8a, axis=AX.X),
                    reads=[gtk], writes=[statk])
            pr.emit("dve", lambda v: v.scalar_tensor_tensor(
                out=t8b, in0=t8a, scalar=NEG, in1=L, op0=ALU.mult, op1=ALU.add),
                reads=[gtk], writes=[gtk])
            pr.emit("dve", lambda v: v.reduce_max(out=c_t2, in_=t8b, axis=AX.X),
                    reads=[gtk], writes=[statk])
            pr.emit("dve", lambda v: v.tensor_scalar(
                out=c_c2, in0=c_cnt, scalar1=2.0, scalar2=None, op0=ALU.is_ge),
                reads=[statk], writes=[statk])
            pr.emit("dve", lambda v: v.tensor_tensor(
                out=c_t2e, in0=c_m1, in1=c_t2, op=ALU.subtract),
                reads=[statk], writes=[statk])
            pr.emit("dve", lambda v: v.scalar_tensor_tensor(
                out=c_t2e, in0=c_t2e, scalar=c_c2, in1=c_t2,
                op0=ALU.mult, op1=ALU.add), reads=[statk], writes=[statk])
            pr.emit("dve", lambda v: v.tensor_scalar(
                out=t8c, in0=L, scalar1=c_t2e, scalar2=None, op0=ALU.is_ge),
                reads=[gtk, statk], writes=[gtk])
            pr.emit("dve", lambda v: v.tensor_scalar(
                out=c_nm1, in0=c_m1, scalar1=-1.0, scalar2=None, op0=ALU.mult),
                reads=[statk], writes=[statk])
            pr.emit("act", lambda a: a.activation(
                out=t8d, in_=L, func=AF.Exp, bias=c_nm1),
                reads=[gtk, statk], writes=[gtk])
            pr.emit("dve", lambda v: v.tensor_tensor(
                out=t8d, in0=t8d, in1=t8c, op=ALU.mult),
                reads=[gtk], writes=[gtk])
            pr.emit("dve", lambda v: v.reduce_sum(out=c_den, in_=t8d, axis=AX.X),
                    reads=[gtk], writes=[statk])
            pr.emit("dve", lambda v: v.reciprocal(out=c_rd2, in_=c_den),
                    reads=[statk], writes=[statk])
            pr.emit("dve", lambda v: v.tensor_scalar(
                out=c_rd2, in0=c_rd2, scalar1=0.5, scalar2=None, op0=ALU.mult),
                reads=[statk], writes=[statk])
            sg = s_gate[:, slot * 8:(slot + 1) * 8]
            pr.emit("dve", lambda v, sg=sg: v.tensor_scalar(
                out=sg, in0=t8d, scalar1=c_rd2, scalar2=None, op0=ALU.mult),
                reads=[gtk, statk], writes=[s_gate])
            pr.emit("pe", lambda T, sg=sg: T.transpose(
                sTp[:, :], sg, identf[:, :]), reads=[s_gate, identf],
                writes=[sTp])
            pr.emit("dve", lambda v: v.tensor_copy(out=sT_bf[:, :],
                                                   in_=sTp[:, :]),
                    reads=[sTp], writes=[sT_bf])
            pr.emit("pe", [
                lambda T: T.matmul(b2p[:, 0:512], sT_bf[:, :], b2_bf[:, 0:512],
                                   start=True, stop=True),
                lambda T: T.matmul(b2p[:, 512:1024], sT_bf[:, :],
                                   b2_bf[:, 512:1024], start=True, stop=True)],
                reads=[sT_bf, b2_bf], writes=[b2p])
            x_t = x_sb[:, slot * 1024:(slot + 1) * 1024]
            pr.emit("dve", lambda v, x_t=x_t: v.tensor_tensor(
                out=x_t, in0=x_t, in1=b2p[:, :], op=ALU.add),
                reads=[x_sb, b2p], writes=[x_sb])

    def init_toff():
        pr.emit("pool", lambda g: g.iota(
            rtmp_t[:, 0:8], [[640, 8]], base=int(BIG), channel_multiplier=0,
            allow_small_or_imprecise_dtypes=True), writes=["rtmp0"])
        pr.emit("dve", lambda v: v.tensor_copy(
            out=toff_row[:, :], in_=rtmp_t[0:1, 0:8]),
            reads=["rtmp0"], writes=[toff_row])

    def route_tile(slot, tag, par):
        rt = rtmp_t[:, par * 32:par * 32 + 32]
        rk = f"rtmp{par}"
        sg = s_gate[:, slot * 8:(slot + 1) * 8]
        mk = rt[:, 0:8]
        rdest = rt[:, 8:16]
        dst2 = rt[:, 16:24]
        m0 = rt[:, 28:29]
        m1 = rt[:, 29:30]
        with nc.psum_tensor("csum" + tag, [128, 8], F32) as csum, \
             nc.psum_tensor("ttot" + tag, [1, 8], F32) as ttot:
            pr.emit("dve", lambda v: v.tensor_scalar(
                out=mk, in0=sg, scalar1=0.0, scalar2=None, op0=ALU.is_gt),
                reads=[s_gate], writes=[rk])
            pr.emit("pe", [
                lambda T: T.matmul(csum[:, :], ltriu[:, :], mk,
                                   start=True, stop=False),
                lambda T: T.matmul(csum[:, :], ones_f[:, :], toff_row[:, :],
                                   start=False, stop=True)],
                reads=[ltriu, rk, ones_f, toff_row], writes=[csum])
            pr.emit("pe", lambda T: T.matmul(
                ttot[:, :], ones_col[:, :], mk, start=True, stop=True),
                reads=[ones_col, rk], writes=[ttot])
            pr.emit("dve", lambda v: v.tensor_tensor(
                out=toff_row[:, :], in0=toff_row[:, :], in1=ttot[:, :],
                op=ALU.add), reads=[toff_row, ttot], writes=[toff_row])
            pr.emit("dve", lambda v: v.scalar_tensor_tensor(
                out=rdest, in0=mk, scalar=-(BIG + 1.0), in1=csum[:, :],
                op0=ALU.mult, op1=ALU.add), reads=[rk, csum], writes=[rk])
            pr.emit("dve", lambda v: v.tensor_reduce(
                out=m0, in_=rdest, axis=AX.X, op=ALU.min),
                reads=[rk], writes=[rk])
            pr.emit("dve", lambda v: v.tensor_scalar(
                out=dst2, in0=rdest, scalar1=m0, scalar2=None, op0=ALU.is_le),
                reads=[rk], writes=[rk])
            pr.emit("dve", lambda v: v.tensor_tensor(
                out=rt[:, 24:32] if False else dst2, in0=dst2, in1=sg,
                op=ALU.mult), reads=[rk, s_gate], writes=[rk])
            pr.emit("dve", lambda v: v.reduce_sum(
                out=sall[:, slot * 2:slot * 2 + 1], in_=dst2, axis=AX.X),
                reads=[rk], writes=[sall])
            pr.emit("dve", lambda v: v.tensor_copy(
                out=rdall[:, slot * 2:slot * 2 + 1], in_=m0),
                reads=[rk], writes=[rdall])
            pr.emit("dve", lambda v: v.tensor_scalar(
                out=dst2, in0=rdest, scalar1=m0, scalar2=None, op0=ALU.is_le),
                reads=[rk], writes=[rk])
            pr.emit("dve", lambda v: v.scalar_tensor_tensor(
                out=dst2, in0=dst2, scalar=BIG2, in1=rdest,
                op0=ALU.mult, op1=ALU.add), reads=[rk], writes=[rk])
            pr.emit("dve", lambda v: v.tensor_reduce(
                out=m1, in_=dst2, axis=AX.X, op=ALU.min),
                reads=[rk], writes=[rk])
            pr.emit("dve", lambda v: v.tensor_scalar(
                out=dst2, in0=dst2, scalar1=m1, scalar2=None, op0=ALU.is_le),
                reads=[rk], writes=[rk])
            pr.emit("dve", lambda v: v.tensor_tensor(
                out=dst2, in0=dst2, in1=sg, op=ALU.mult),
                reads=[rk, s_gate], writes=[rk])
            pr.emit("dve", lambda v: v.reduce_sum(
                out=sall[:, slot * 2 + 1:slot * 2 + 2], in_=dst2, axis=AX.X),
                reads=[rk], writes=[sall])
            pr.emit("dve", lambda v: v.tensor_copy(
                out=rdall[:, slot * 2 + 1:slot * 2 + 2], in_=m1),
                reads=[rk], writes=[rdall])
        for r in range(2):
            roff = rdall[:, slot * 2 + r:slot * 2 + r + 1]
            pr.emit("pool", lambda g, roff=roff: g.indirect_dma_start(
                xb_dram[:, :], IndirectOffsetOnAxis(roff, 0),
                xbstg[par][:, :], None),
                reads=[xbstg[par], rdall], writes=[xb_dram], dma="Q_pool")

    def ffn_pass(slots, next_first, tag, slot_sched=None):
        steps = [(e, h) for e in range(NE) for h in range(2)]
        if slot_sched is None:
            slot_sched = [i % 2 for i in range(len(steps))]

        def gather_load(e, m):
            stg = xbstg[m % 2]
            srcr = xb_dram[e * 640 + m * 128:e * 640 + m * 128 + 128, :]
            pr.emit("sp", lambda _, stg=stg, srcr=srcr: _.dma_start(
                out=stg[:, :], in_=srcr),
                reads=[xb_dram], writes=[stg], dma="Q_sp")

        def gather_tr(e, m, tg):
            par = e % 2
            stg = xbstg[m % 2]
            for dc in range(DCN):
                t = tg[dc % 2]
                pr.emit("pe", lambda T, t=t, stg=stg, dc=dc: T.transpose(
                    t[:, :], stg[:, dc * 128:(dc + 1) * 128],
                    identb[:, :]), reads=[stg, identb], writes=[t])
                dst = xbT[par][:, dc * 512 + m * 128:dc * 512 + m * 128 + 128]
                pr.emit("dve", lambda v, t=t, dst=dst: v.tensor_copy(
                    out=dst, in_=t[:, :]), reads=[t], writes=[xbT[par]])

        zrot = [0]
        LAG = 2
        with nc.psum_tensor("zp0" + tag, [128, 512], F32) as z0, \
             nc.psum_tensor("zp1" + tag, [128, 512], F32) as z1, \
             nc.psum_tensor("yp0" + tag, [128, 1024], F32) as y0, \
             nc.psum_tensor("yp1" + tag, [128, 1024], F32) as y1, \
             nc.psum_tensor("tg0" + tag, [128, 128], BF16) as tg0, \
             nc.psum_tensor("tg1" + tag, [128, 128], BF16) as tg1:
            tg = [tg0, tg1]
            for m in range(4):
                gather_load(0, m)
                gather_tr(0, m, tg)
            zp = [z0, z1]
            yp = [y0, y1]
            for step, (e, half) in enumerate(steps):
                pf_late = None
                nf_late = False
                if step + 1 < len(steps):
                    e2, h2 = steps[step + 1]
                    if slot_sched[step + 1] != slot_sched[step]:
                        w_prefetch(e2, h2, slot_sched[step + 1])
                    else:
                        pf_late = (e2, h2, slot_sched[step + 1])
                elif next_first is not None:
                    if slot_sched[step] != 0:
                        next_first()
                    else:
                        nf_late = True
                hooks = {}
                if e + 1 < NE:
                    if half == 0:
                        hooks[0] = [("L", 0), ("L", 1)]
                        hooks[11] = [("T", 0), ("L", 2)]
                        hooks[22] = [("T", 1), ("L", 3)]
                    else:
                        hooks[0] = [("T", 2)]
                        hooks[11] = [("T", 3)]
                ws = slot_sched[step]
                w1t, w2t = W1s[ws], W2s[ws]
                xbe = xbT[e % 2]
                groups = [(0, 1), (2, 3)]
                seq = [(g, fc) for g in groups for fc in range(16)]

                def emit_z(i):
                    g, fc = seq[i]
                    z = zp[i % 2]
                    h = hT[i % 3]
                    fns = []
                    for dc in range(DCN):
                        fns.append(lambda T, z=z, dc=dc, fc=fc, g=g,
                                   w1t=w1t, xbe=xbe: T.matmul(
                            z[:, 0:256],
                            w1t[:, dc * 2048 + fc * 128:
                                dc * 2048 + fc * 128 + 128],
                            xbe[:, dc * 512 + g[0] * 128:
                                dc * 512 + g[0] * 128 + 256],
                            start=(dc == 0), stop=(dc == DCN - 1)))
                    pr.emit("pe", fns, reads=[w1t, xbe], writes=[z])
                    fcg = half * 16 + fc
                    bcol = b1T[:, fcg * 8 + e:fcg * 8 + e + 1]
                    pr.emit("act", lambda a, z=z, h=h, bcol=bcol:
                            a.activation(out=h[:, 0:256], in_=z[:, 0:256],
                                         func=AF.Gelu, bias=bcol),
                            reads=[z, b1T], writes=[h])

                def emit_y(i):
                    g, fc = seq[i]
                    h = hT[i % 3]
                    for ti in range(2):
                        yb = yp[ti]
                        yfns = []
                        for hf in range(2):
                            yfns.append(lambda T, yb=yb, h=h, ti=ti, fc=fc,
                                        hf=hf, w2t=w2t: T.matmul(
                                yb[:, hf * 512:(hf + 1) * 512],
                                h[:, ti * 128:(ti + 1) * 128],
                                w2t[:, fc * 1024 + hf * 512:
                                    fc * 1024 + hf * 512 + 512],
                                start=(fc == 0), stop=(fc == 15)))
                        pr.emit("pe", yfns, reads=[h, w2t], writes=[yb])
                    if fc == 15:
                        for ti in range(2):
                            yb = yp[ti]
                            m = g[ti]
                            for hf in range(2):
                                dst = y_raw[e * 640 + m * 128:
                                            e * 640 + m * 128 + 128,
                                            hf * 512:(hf + 1) * 512]
                                pr.emit("dve", lambda v, yb=yb, hf=hf:
                                        v.tensor_copy(
                                    out=ysc[hf][:, :],
                                    in_=yb[:, hf * 512:(hf + 1) * 512]),
                                    reads=[yb], writes=[ysc[hf]])
                                if half == 0:
                                    pr.emit("act", lambda a, dst=dst, hf=hf:
                                            a.dma_start(out=dst,
                                                        in_=ysc[hf][:, :]),
                                        reads=[ysc[hf]], writes=[y_raw],
                                        dma="Q_act")
                                else:
                                    pr.emit("pool", lambda g, dst=dst, hf=hf:
                                            g.dma_start(out=dst,
                                                        in_=ysc[hf][:, :],
                                                        accum_op=ALU.add),
                                        reads=[ysc[hf]], writes=[y_raw],
                                        dma="Q_pool")

                n = len(seq)
                for i in range(n + LAG):
                    for kind, m in hooks.get(i, ()):
                        if kind == "L":
                            gather_load(e + 1, m)
                        else:
                            gather_tr(e + 1, m, tg)
                    if i < n:
                        emit_z(i)
                    if i >= LAG:
                        emit_y(i - LAG)
                if pf_late is not None:
                    w_prefetch(*pf_late)
                if nf_late:
                    next_first()
        pr.claim([xbT[0], grb[0], grb[1]])
        gi = 0
        for slot in slots:
            x_t = x_sb[:, slot * 1024:(slot + 1) * 1024]
            for r in range(2):
                gb = grb[gi % 2]
                gi += 1
                roff = rdall[:, slot * 2 + r:slot * 2 + r + 1]
                pr.emit("pool", lambda g, gb=gb, roff=roff:
                        g.indirect_dma_start(
                    gb[:, :], None, y_raw[:, :],
                    IndirectOffsetOnAxis(roff, 0)),
                    reads=[y_raw, rdall], writes=[gb], dma="Q_pool")
                scol = sall[:, slot * 2 + r:slot * 2 + r + 1]
                pr.emit("dve", lambda v, gb=gb, x_t=x_t, scol=scol:
                        v.scalar_tensor_tensor(
                    out=x_t, in0=gb[:, :], scalar=scol, in1=x_t,
                    op0=ALU.mult, op1=ALU.add),
                    reads=[gb, sall, x_sb], writes=[x_sb])

    def chunk_masks(c):
        start = c * CT * 128 + 96
        src2 = d_mask[start:start + KC + 64]
        pr.emit("act", lambda a, src2=src2: a.dma_start(
            out=kmask_bf[:, :], in_=src2[None, :]),
            reads=[d_mask], writes=[kmask_bf], dma="Q_act")
        with nc.psum_tensor(f"vmp{c}", [128, 1], BF16) as vmp:
            for m in range(VT):
                pr.emit("pe", lambda T, m=m: T.transpose(
                    vmp[:, :], kmask_bf[0:1, m * 128:(m + 1) * 128],
                    identb[0:1, 0:1]), reads=[kmask_bf, identb], writes=[vmp])
                pr.emit("dve", lambda v, m=m: v.tensor_copy(
                    out=vmask[:, m:m + 1], in_=vmp[:, :]),
                    reads=[vmp], writes=[vmask])
        with nc.psum_tensor(f"mrp{c}", [128, 512], F32) as mrp:
            nsl = (KC + 511) // 512
            for s in range(nsl):
                wdt = min(512, KC - s * 512)
                pr.emit("pe", lambda T, s=s, wdt=wdt: T.matmul(
                    mrp[:, 0:wdt], ones_bf[:, :],
                    kmask_bf[:, s * 512:s * 512 + wdt], start=True, stop=True),
                    reads=[ones_bf, kmask_bf], writes=[mrp])
                pr.emit("dve", lambda v, s=s, wdt=wdt: v.tensor_copy(
                    out=maskrep[:, s * 512:s * 512 + wdt], in_=mrp[:, 0:wdt]),
                    reads=[mrp], writes=[maskrep])

    def load_wattn(c_t):
        src = c_t[:, :, :].rearrange("a p d -> p a d")
        pr.emit("sp", lambda _, src=src: _.dma_start(
            out=wattn[:, :].rearrange("p (a d) -> p a d", a=8), in_=src),
            reads=[c_t], writes=[wattn], dma="Q_sp")

    def qkv_phase(c):
        for slot in range(SLOTS):
            ln_tile(slot, do_gate=False, tag=f"n2_{c}_{slot}", par=slot % 2)
        pr.claim([tmp4k_t[0], tmp4k_t[1], xnT32_t[0], xnT32_t[1], kT])
        load_wattn(c_k)
        slices = []
        pos = 0
        while pos < KC:
            wdt = min(512, KC - pos)
            slices.append((pos, wdt))
            pos += wdt
        with nc.psum_tensor(f"kqp0_{c}", [128, 512], F32) as kp0, \
             nc.psum_tensor(f"kqp1_{c}", [128, 512], F32) as kp1:
            kp = [kp0, kp1]
            r = 0
            for ic in range(8):
                for (pos, wdt) in slices:
                    p = kp[r % 2]
                    r += 1
                    xc = 96 + pos
                    fns = []
                    for dc in range(DCN):
                        fns.append(lambda T, p=p, dc=dc, ic=ic, xc=xc, wdt=wdt:
                                   T.matmul(
                            p[:, 0:wdt],
                            wattn[:, ic * 1024 + dc * 128:
                                  ic * 1024 + dc * 128 + 128],
                            xnT[:, dc * XNC + xc:dc * XNC + xc + wdt],
                            start=(dc == 0), stop=(dc == DCN - 1)))
                    pr.emit("pe", fns, reads=[wattn, xnT], writes=[p])
                    kb = kbT[:, ic:ic + 1]
                    dst = kT[:, ic * KC + pos:ic * KC + pos + wdt]
                    mr = maskrep[:, pos:pos + wdt]
                    pr.emit("dve", lambda v, p=p, dst=dst, kb=kb, mr=mr,
                            wdt=wdt: v.scalar_tensor_tensor(
                        out=dst, in0=p[:, 0:wdt], scalar=kb, in1=mr,
                        op0=ALU.add, op1=ALU.mult),
                        reads=[p, kbT, maskrep], writes=[kT])
        load_wattn(c_v)
        with nc.psum_tensor(f"vp0_{c}", [128, 1024], F32) as vp0, \
             nc.psum_tensor(f"vp1_{c}", [128, 1024], F32) as vp1:
            vp = [vp0, vp1]
            for m in range(VT):
                p = vp[m % 2]
                xc = 96 + m * 128
                fns = []
                for dc in range(DCN):
                    for hf in range(2):
                        fns.append(lambda T, p=p, dc=dc, hf=hf, xc=xc: T.matmul(
                            p[:, hf * 512:(hf + 1) * 512],
                            xnT[:, dc * XNC + xc:dc * XNC + xc + 128],
                            wattn[:, dc * 1024 + hf * 512:
                                  dc * 1024 + hf * 512 + 512],
                            start=(dc == 0), stop=False))
                for hf in range(2):
                    fns.append(lambda T, p=p, hf=hf: T.matmul(
                        p[:, hf * 512:(hf + 1) * 512], ones_bf[:, :],
                        vb_bf[:, hf * 512:(hf + 1) * 512],
                        start=False, stop=True))
                pr.emit("pe", fns, reads=[wattn, xnT, ones_bf, vb_bf],
                        writes=[p])
                dst = v_sb[:, m * 1024:(m + 1) * 1024]
                vm = vmask[:, m:m + 1]
                pr.emit("dve", lambda v, p=p, dst=dst, vm=vm: v.tensor_scalar(
                    out=dst, in0=p[:, :], scalar1=vm, scalar2=None,
                    op0=ALU.mult), reads=[p, vmask], writes=[v_sb])
        load_wattn(c_q)
        nsl = max(CT * 128 // 512, 1)
        with nc.psum_tensor(f"qp_{c}", [128, 8 * 512], F32) as qp:
            for s in range(nsl):
                wdt = min(512, CT * 128 - s * 512)
                xc = 128 + s * 512
                for ic in range(8):
                    fns = []
                    for dc in range(DCN):
                        fns.append(lambda T, ic=ic, dc=dc, xc=xc, wdt=wdt:
                                   T.matmul(
                            qp[:, ic * 512:ic * 512 + wdt],
                            wattn[:, ic * 1024 + dc * 128:
                                  ic * 1024 + dc * 128 + 128],
                            xnT[:, dc * XNC + xc:dc * XNC + xc + wdt],
                            start=(dc == 0), stop=(dc == DCN - 1)))
                    pr.emit("pe", fns, reads=[wattn, xnT], writes=[qp])
                for ic in range(8):
                    qb = qb8T[:, ic:ic + 1]
                    dst = xnT[:, ic * XNC + xc:ic * XNC + xc + wdt]
                    pr.emit("dve", lambda v, ic=ic, dst=dst, qb=qb, wdt=wdt:
                            v.tensor_scalar(
                        out=dst, in0=qp[:, ic * 512:ic * 512 + wdt],
                        scalar1=0.125, scalar2=qb, op0=ALU.mult, op1=ALU.add),
                        reads=[qp, qb8T], writes=[xnT])

    def attn_phase(c):
        load_wattn(c_o)
        with nc.psum_tensor(f"Sp0_{c}", [128, 192], F32) as S0, \
             nc.psum_tensor(f"Sp1_{c}", [128, 192], F32) as S1, \
             nc.psum_tensor(f"ETp0_{c}", [128, 256], BF16) as E0, \
             nc.psum_tensor(f"ETp1_{c}", [128, 256], BF16) as E1, \
             nc.psum_tensor(f"ATp0_{c}", [128, 128], F32) as A0, \
             nc.psum_tensor(f"ATp1_{c}", [128, 128], F32) as A1, \
             nc.psum_tensor(f"Op_{c}", [128, 1024], F32) as Op:
            Sp = [S0, S1]
            Ep = [E0, E1]
            Ap = [A0, A1]
            for t in range(CT):
                qcol = 128 + t * 128
                for h in range(NH):
                    r = (t * NH + h) % 2
                    ic, sub = h // 2, (h % 2) * 64
                    Spp, Epp = Sp[r], Ep[r]
                    App = Ap[((t * NH + h) // 2) % 2]
                    kpos = t * 128
                    pr.emit("pe", lambda T, Spp=Spp, ic=ic, sub=sub, qcol=qcol,
                            kpos=kpos: T.matmul(
                        Spp[:, :],
                        xnT[sub:sub + 64,
                            ic * XNC + qcol:ic * XNC + qcol + 128],
                        kT[sub:sub + 64, ic * KC + kpos:ic * KC + kpos + 192],
                        start=True, stop=True),
                        reads=[xnT, kT], writes=[Spp])
                    pr.emit("dve", lambda v, Spp=Spp, r=r: v.tensor_tensor(
                        out=Sm[r][:, :], in0=Spp[:, :], in1=band[:, :],
                        op=ALU.add), reads=[Spp, band], writes=[Sm[r]])
                    pr.emit("dve", lambda v, r=r: v.reduce_max(
                        out=astat[:, r:r + 1], in_=Sm[r][:, :], axis=AX.X,
                        negate=True), reads=[Sm[r]], writes=[astat])
                    pr.emit("act", lambda a, r=r: a.activation(
                        out=Ef[r][:, :], in_=Sm[r][:, :], func=AF.Exp,
                        bias=astat[:, r:r + 1]), reads=[Sm[r], astat],
                        writes=[Ef[r]])
                    pr.emit("dve", lambda v, r=r: v.reduce_sum(
                        out=astat[:, 2 + r:3 + r], in_=Ef[r][:, :], axis=AX.X),
                        reads=[Ef[r]], writes=[astat])
                    pr.emit("dve", lambda v, r=r: v.reciprocal(
                        out=astat[:, 4 + r:5 + r], in_=astat[:, 2 + r:3 + r]),
                        reads=[astat], writes=[astat])
                    pr.emit("dve", lambda v, r=r: v.tensor_scalar(
                        out=Eb[r][:, :], in0=Ef[r][:, :],
                        scalar1=astat[:, 4 + r:5 + r], scalar2=None,
                        op0=ALU.mult), reads=[Ef[r], astat], writes=[Eb[r]])
                    pr.emit("pe", [
                        lambda T, Epp=Epp, r=r: T.transpose(
                            Epp[:, 0:128], Eb[r][:, 0:128], identb[:, :]),
                        lambda T, Epp=Epp, r=r: T.transpose(
                            Epp[0:64, 128:256], Eb[r][:, 128:192],
                            identb[:, :])],
                        reads=[Eb[r], identb], writes=[Epp])
                    pr.emit("dve", [
                        lambda v, Epp=Epp, r=r: v.tensor_copy(
                            out=ETb[r][:, 0:128], in_=Epp[:, 0:128]),
                        lambda v, Epp=Epp, r=r: v.tensor_copy(
                            out=ETb[r][0:64, 128:256], in_=Epp[0:64, 128:256])],
                        reads=[Epp], writes=[ETb[r]])
                    pr.emit("pe", [
                        lambda T, App=App, r=r, t=t, h=h, sub=sub: T.matmul(
                            App[sub:sub + 64, :],
                            v_sb[:, t * 1024 + h * 64:t * 1024 + h * 64 + 64],
                            ETb[r][:, 0:128], start=True, stop=False),
                        lambda T, App=App, r=r, t=t, h=h, sub=sub: T.matmul(
                            App[sub:sub + 64, :],
                            v_sb[0:64, (t + 1) * 1024 + h * 64:
                                 (t + 1) * 1024 + h * 64 + 64],
                            ETb[r][0:64, 128:256], start=False, stop=True)],
                        reads=[v_sb, ETb[r]], writes=[App])
                    if h % 2 == 1:
                        ic2 = h // 2
                        pr.emit("dve", lambda v, App=App, ic2=ic2: v.tensor_copy(
                            out=aT_sb[:, ic2 * 128:ic2 * 128 + 128],
                            in_=App[:, :]), reads=[App], writes=[aT_sb])
                fns = []
                for fc in range(8):
                    for hf in range(2):
                        fns.append(lambda T, fc=fc, hf=hf: T.matmul(
                            Op[:, hf * 512:(hf + 1) * 512],
                            aT_sb[:, fc * 128:fc * 128 + 128],
                            wattn[:, fc * 1024 + hf * 512:
                                  fc * 1024 + hf * 512 + 512],
                            start=(fc == 0), stop=False))
                for hf in range(2):
                    fns.append(lambda T, hf=hf: T.matmul(
                        Op[:, hf * 512:(hf + 1) * 512], ones_bf[:, :],
                        ob_bf[:, hf * 512:(hf + 1) * 512],
                        start=False, stop=True))
                pr.emit("pe", fns, reads=[aT_sb, wattn, ones_bf, ob_bf],
                        writes=[Op])
                slot = t + 1
                x_t = x_sb[:, slot * 1024:(slot + 1) * 1024]
                pr.emit("dve", lambda v, x_t=x_t: v.tensor_tensor(
                    out=x_t, in0=x_t, in1=Op[:, :], op=ALU.add),
                    reads=[x_sb, Op], writes=[x_sb])

    def store_and_carry(c):
        for slot in range(1, SLOTS - 1):
            row0 = (c * CT + slot - 1) * 128
            dst = d_y[row0:row0 + 128, :]
            src = x_sb[:, slot * 1024:(slot + 1) * 1024]
            pr.emit("act", lambda a, dst=dst, src=src: a.dma_start(
                out=dst, in_=src), reads=[x_sb], writes=[d_y], dma="Q_act")
        if c < cfg.NCHUNK - 1:
            pr.emit("dve", lambda v: v.tensor_copy(
                out=x_sb[:, 0:1024],
                in_=x_sb[:, (SLOTS - 2) * 1024:(SLOTS - 1) * 1024]),
                reads=[x_sb], writes=[x_sb])
            pr.emit("dve", lambda v: v.tensor_copy(
                out=x_sb[:, 1024:2048],
                in_=x_sb[:, (SLOTS - 1) * 1024:SLOTS * 1024]),
                reads=[x_sb], writes=[x_sb])

    def stop(name):
        return stop_after == name

    for c in range(cfg.NCHUNK):
        ffn1_slots = list(range(SLOTS)) if c == 0 else list(range(2, SLOTS))
        ffn2_slots = list(range(1, SLOTS - 1))

        if stop("conv"):
            break
        if c > 0:
            pr.phase(f"chunk{c}")
            load_x_slots(c, ffn1_slots)
        pr.claim(WB_ALL)
        if c == 0:
            w_prefetch(0, 0, 0)
        if stop("ln1load"):
            break
        pr.claim(XN_ALL)
        init_toff()
        for i, slot in enumerate(ffn1_slots):
            ln_tile(slot, do_gate=True, tag=f"a_{c}_{slot}", par=i % 2)
        if stop("ln1"):
            break

        pr.claim(WB_ALL)
        ffn_pass(ffn1_slots, lambda: w_prefetch(0, 0, 0), tag=f"f1_{c}")
        if stop("ffn1"):
            break

        pr.claim(WB_ALL)
        pr.claim(XN_ALL)
        pr.emit("sp", lambda _: _.dma_start(out=vb_bf[:, :], in_=c_vbo[0:1, :]),
                reads=[c_vbo], writes=[vb_bf], dma="Q_sp")
        pr.emit("sp", lambda _: _.dma_start(out=ob_bf[:, :], in_=c_vbo[1:2, :]),
                reads=[c_vbo], writes=[ob_bf], dma="Q_sp")
        chunk_masks(c)
        qkv_phase(c)
        if stop("qkv"):
            break

        attn_phase(c)
        if stop("attn"):
            break

        pr.claim(WB_ALL)
        pr.claim(XN_ALL)
        init_toff()
        for i, slot in enumerate(ffn2_slots):
            ln_tile(slot, do_gate=True, tag=f"b_{c}_{slot}", par=i % 2)
        if stop("ln3"):
            break

        pr.claim(WB_ALL)
        nxt = (lambda: w_prefetch(0, 0, 0)) if c < cfg.NCHUNK - 1 else None
        ffn_pass(ffn2_slots, nxt, tag=f"f2_{c}")
        if stop("ffn2"):
            break

        store_and_carry(c)
        if debug_stage == c:
            for slot in range(SLOTS):
                dst = d_dbg[slot * 128:(slot + 1) * 128, :]
                src = x_sb[:, slot * 1024:(slot + 1) * 1024]
                pr.emit("act", lambda a, dst=dst, src=src: a.dma_start(
                    out=dst, in_=src), reads=[x_sb], writes=["dbg"],
                    dma="Q_act")
            break

    pr.phase("fin")
    pr.finish()
    return nc



def _shard(x, cfg, ncores=8):
    B, S, _ = x.shape
    own = cfg.OT * 128
    nq = S // own
    halo = 128
    slabs = []
    for core in range(ncores):
        b, q = divmod(core, nq)
        lo = q * own - halo
        hi = q * own + own + halo
        slab = np.zeros((own + 2 * halo, D), np.float32)
        m = np.zeros((own + 2 * halo,), ml_dtypes.bfloat16)
        s0, s1 = max(lo, 0), min(hi, S)
        slab[s0 - lo:s1 - lo] = x[b, s0:s1]
        m[s0 - lo:s1 - lo] = 1.0
        slabs.append((np.ascontiguousarray(slab), m))
    return slabs


def _weight_map(L0_attn_q_w, L0_attn_q_b, L0_attn_k_w, L0_attn_k_b,
                L0_attn_v_w, L0_attn_v_b, L0_attn_o_w, L0_attn_o_b,
                L0_ffn_W1, L0_ffn_b1, L0_ffn_W2, L0_ffn_b2, L0_ffn_Wg,
                L0_ffn_bg):
    f = lambda a: np.ascontiguousarray(np.asarray(a, np.float32))
    return {
        "q_w": f(L0_attn_q_w), "q_b": f(L0_attn_q_b),
        "k_w": f(L0_attn_k_w), "k_b": f(L0_attn_k_b),
        "v_w": f(L0_attn_v_w), "v_b": f(L0_attn_v_b),
        "o_w": f(L0_attn_o_w), "o_b": f(L0_attn_o_b),
        "W1": f(L0_ffn_W1), "b1": f(L0_ffn_b1),
        "W2": f(L0_ffn_W2), "b2": f(L0_ffn_b2),
        "Wg": f(L0_ffn_Wg), "bg": f(L0_ffn_bg),
    }


def kernel(L0_attn_hidden_states, **w):
    cfg = Cfg(own_tiles=32, chunk_tiles=8)
    nc = build_program(cfg)
    x = np.asarray(L0_attn_hidden_states, np.float32)
    slabs = _shard(x, cfg)
    weights = _weight_map(**w)
    in_maps = []
    for core in range(8):
        slab, m = slabs[core]
        d = dict(weights)
        d["x_slab"] = slab
        d["kvmask"] = m
        in_maps.append(d)
    res = run_bass_kernel_spmd(nc, in_maps, core_ids=list(range(8)))
    out = np.empty((2, 16384, 1024), np.float32)
    own = cfg.OT * 128
    for core in range(8):
        b, q = divmod(core, 4)
        out[b, q * own:(q + 1) * own] = res.results[core]["y"]
    return out


# version 18
# speedup vs baseline: 1.1429x; 1.1429x over previous
import numpy as np
import ml_dtypes

import concourse.bass as bass
import concourse.mybir as mybir
from concourse.bass_utils import run_bass_kernel_spmd
from concourse.bass_core_rust import IndirectOffsetOnAxis

F32 = mybir.dt.float32
I32 = mybir.dt.int32
BF16 = mybir.dt.bfloat16
AF = mybir.ActivationFunctionType
ALU = mybir.AluOpType
AX = mybir.AxisListType

D = 1024
DCN = 8
FF = 4096
FCN = 32
NE = 8
NH = 16
HD = 64
NEG = -1.0e30
BIG = float(1 << 20)
BIG2 = float(1 << 22)


class Cfg:
    def __init__(self, own_tiles=32, chunk_tiles=8):
        self.OT = own_tiles
        self.CT = chunk_tiles
        assert own_tiles % chunk_tiles == 0
        self.NCHUNK = own_tiles // chunk_tiles
        self.SLOTS = chunk_tiles + 2
        self.SLAB = own_tiles + 2
        self.KC = chunk_tiles * 128 + 64
        self.VT = chunk_tiles + 1



BLOCKNAME = {"pe": "tensor", "dve": "vector", "act": "scalar",
             "pool": "gpsimd", "sp": "sync"}
DMA_ENG = {"Q_sp": "sp", "Q_act": "act", "Q_pool": "pool"}


def _handle(x):
    return x.tensor if isinstance(x, bass.AP) else x


class Prog:
    def __init__(self, nc):
        self.nc = nc
        self.plans = {k: [] for k in BLOCKNAME}
        self.sems = {}
        self.cnt = {}
        self.clock = {e: {} for e in BLOCKNAME}
        self.lastw = {}
        self.readers = {}
        self.qlane = {}
        self.cur_phase = "init"
        self.n_ops = 0
        self._keycache = {}

    def keys(self, x):
        if isinstance(x, str):
            return [x]
        h = _handle(x)
        name = h.name
        got = self._keycache.get(name)
        if got is None:
            if getattr(h, "space", None) is not None and "PSUM" in str(h.space):
                ml = self.nc.lookup_mls(h).memorylocations[0]
                nbytes = int(ml.dims[1])
                b0 = int(ml.bank)
                nb = (nbytes + 2047) // 2048
                got = [f"PSB{b}" for b in range(b0, b0 + nb)]
            else:
                got = [name]
            self._keycache[name] = got
        return got

    def sem(self, name):
        if name not in self.sems:
            self.sems[name] = self.nc.alloc_semaphore(name)
            self.cnt[name] = 0
        return self.sems[name]

    def _wait(self, eng, need):
        plan = self.plans[eng]
        clk = self.clock[eng]
        for s, v in sorted(need.items()):
            if clk.get(s, 0) >= v:
                continue
            plan.append(("w", s, v))
            clk[s] = v

    def emit(self, eng, fns, reads=(), writes=(), dma=None):
        if not isinstance(fns, (list, tuple)):
            fns = [fns]
        rk = [k for r in reads for k in self.keys(r)]
        wk = [k for w in writes for k in self.keys(w)]
        need = {}
        for k in rk:
            lw = self.lastw.get(k)
            if lw:
                need[lw[0]] = max(need.get(lw[0], 0), lw[1])
        for k in wk:
            lw = self.lastw.get(k)
            if lw:
                need[lw[0]] = max(need.get(lw[0], 0), lw[1])
            for s, v in self.readers.get(k, {}).items():
                need[s] = max(need.get(s, 0), v)
        if dma is not None:
            assert len(fns) == 1
            lane = self.qlane[dma] = (self.qlane.get(dma, -1) + 1) % 8
            sname = f"{dma}_{lane}"
            issuer = DMA_ENG[dma]
            self.sem(sname)
            if self.cnt[sname] > 0:
                need[sname] = max(need.get(sname, 0), self.cnt[sname])
        else:
            sname = "P_" + eng
            issuer = eng
            self.sem(sname)
        self._wait(issuer, need)
        plan = self.plans[issuer]
        n = len(fns)
        for i, fn in enumerate(fns):
            if dma is not None:
                self.cnt[sname] += 16
                plan.append(("op", fn, sname, 16))
            else:
                inc = 1 if i == n - 1 else 0
                if inc:
                    self.cnt[sname] += 1
                plan.append(("op", fn, sname, inc))
            self.n_ops += 1
        final = (sname, self.cnt[sname])
        for k in wk:
            self.lastw[k] = final
            self.readers[k] = {}
        for k in rk:
            self.readers.setdefault(k, {})[final[0]] = final[1]

    def claim(self, writes):
        self.emit("pool", lambda g: g.nop(), reads=(), writes=writes)

    def raw_wait(self, eng, sname, val):
        self._wait(eng, {sname: val})

    def phase(self, name):
        self._flush()
        self.cur_phase = name

    def finish(self):
        for q in ("Q_sp", "Q_act", "Q_pool"):
            for lane in range(8):
                s = f"{q}_{lane}"
                if s in self.sems and self.cnt[s] > 0:
                    self.raw_wait("pool", s, self.cnt[s])
        self._flush()

    def _flush(self):
        plans = self.plans
        if not any(plans.values()):
            return
        self.plans = {k: [] for k in BLOCKNAME}
        sems = self.sems

        def mk_replay(plan):
            def replay(eng):
                for item in plan:
                    if item[0] == "w":
                        eng.wait_ge(sems[item[1]], item[2])
                    else:
                        ins = item[1](eng)
                        if item[3]:
                            ins.then_inc(sems[item[2]], item[3])
            return replay

        with self.nc.Block() as block:
            for ename, plan in plans.items():
                if plan:
                    getattr(block, BLOCKNAME[ename])(mk_replay(plan))



def build_program(cfg, debug_stage=None, stop_after=None):
    nc = bass.Bass("TRN2", target_bir_lowering=False, num_swdge_queues=4)
    SLOTS, KC, VT, CT = cfg.SLOTS, cfg.KC, cfg.VT, cfg.CT
    XNC = SLOTS * 128

    slab_rows = cfg.SLAB * 128
    d_x = nc.dram_tensor("x_slab", [slab_rows, D], F32, kind="ExternalInput").ap()
    d_mask = nc.dram_tensor("kvmask", [slab_rows], BF16, kind="ExternalInput").ap()
    d_qw = nc.dram_tensor("q_w", [D, D], F32, kind="ExternalInput").ap()
    d_qb = nc.dram_tensor("q_b", [D], F32, kind="ExternalInput").ap()
    d_kw = nc.dram_tensor("k_w", [D, D], F32, kind="ExternalInput").ap()
    d_kb = nc.dram_tensor("k_b", [D], F32, kind="ExternalInput").ap()
    d_vw = nc.dram_tensor("v_w", [D, D], F32, kind="ExternalInput").ap()
    d_vb = nc.dram_tensor("v_b", [D], F32, kind="ExternalInput").ap()
    d_ow = nc.dram_tensor("o_w", [D, D], F32, kind="ExternalInput").ap()
    d_ob = nc.dram_tensor("o_b", [D], F32, kind="ExternalInput").ap()
    d_w1 = nc.dram_tensor("W1", [NE, D, FF], F32, kind="ExternalInput").ap()
    d_b1 = nc.dram_tensor("b1", [NE, FF], F32, kind="ExternalInput").ap()
    d_w2 = nc.dram_tensor("W2", [NE, FF, D], F32, kind="ExternalInput").ap()
    d_b2 = nc.dram_tensor("b2", [NE, D], F32, kind="ExternalInput").ap()
    d_wg = nc.dram_tensor("Wg", [D, NE], F32, kind="ExternalInput").ap()
    d_bg = nc.dram_tensor("bg", [NE], F32, kind="ExternalInput").ap()
    d_y = nc.dram_tensor("y", [cfg.OT * 128, D], F32, kind="ExternalOutput").ap()

    c_w1 = nc.dram_tensor("c_w1", [NE, D, FF], BF16, kind="Internal").ap()
    c_w2 = nc.dram_tensor("c_w2", [NE, FF, D], BF16, kind="Internal").ap()
    xb_dram = nc.dram_tensor("xb_dram", [NE * 640, D], BF16, kind="Internal").ap()
    y_raw = nc.dram_tensor("y_raw", [NE * 640, D], F32, kind="Internal").ap()
    c_vbo = nc.dram_tensor("c_vbo", [2, D], BF16, kind="Internal").ap()
    c_q = nc.dram_tensor("c_q", [8, 128, D], BF16, kind="Internal").ap()
    c_k = nc.dram_tensor("c_k", [8, 128, D], BF16, kind="Internal").ap()
    c_v = nc.dram_tensor("c_v", [8, 128, D], BF16, kind="Internal").ap()
    c_o = nc.dram_tensor("c_o", [8, 128, D], BF16, kind="Internal").ap()
    if debug_stage is not None:
        d_dbg = nc.dram_tensor("dbg", [SLOTS * 128, D], F32,
                               kind="ExternalOutput").ap()

    off = [17408]
    sb = {}
    sb_off = {}

    def alloc(name, shape, dtype, at=None):
        nbytes = int(np.prod(shape[1:])) * (2 if dtype == BF16 else 4)
        if at is None:
            o = off[0]
            off[0] += (nbytes + 31) & ~31
        else:
            o = at
        t = nc.alloc_sbuf_tensor_at(name, list(shape), dtype, offset=o)
        sb[name] = t
        sb_off[name] = o
        return t

    x_sb = alloc("x_sb", [128, SLOTS * 1024], F32)
    xnT = alloc("xnT", [128, DCN * XNC], BF16)
    off[0] = max(off[0], sb_off["xnT"] + 20480)
    w1A = alloc("w1A", [128, 16 * 1024], BF16)
    w2A = alloc("w2A", [128, 16 * 1024], BF16)
    WB_off = off[0]
    w1B = alloc("w1B", [128, 16 * 1024], BF16)
    w2B = alloc("w2B", [128, 16 * 1024], BF16)
    WB_end = off[0]
    hT = [alloc("hT0", [128, 256], BF16), alloc("hT1", [128, 256], BF16),
          alloc("hT2", [128, 256], BF16)]
    aT_sb = alloc("aT_sb", [128, 1024], BF16)
    stats_t = alloc("stats", [128, 32], F32)
    gtmp_t = alloc("gtmp", [128, 96], F32)
    s_gate = alloc("s_gate", [128, SLOTS * 8], F32)
    sT_bf = alloc("sT_bf", [8, 128], BF16)
    b1T = alloc("b1T", [128, FCN * 8], F32)
    qbT = alloc("qbT", [128, 8], F32)
    kbT = alloc("kbT", [128, 8], F32)
    qb8T = alloc("qb8T", [128, 8], F32)
    b2_bf = alloc("b2_bf", [8, D], BF16)
    vb_bf = alloc("vb_bf", [1, D], BF16)
    ob_bf = alloc("ob_bf", [1, D], BF16)
    bg_f = alloc("bg_f", [1, 8], F32)
    ones_f = alloc("ones_f", [1, 128], F32)
    ones_bf = alloc("ones_bf", [1, 128], BF16)
    identf = alloc("identf", [128, 128], F32)
    identb = alloc("identb", [128, 128], BF16)
    wg_sb = alloc("wg_sb", [128, DCN * 8], F32)
    vmask = alloc("vmask", [128, VT], F32)
    band = alloc("band", [128, 192], F32)
    ltriu = alloc("ltriu", [128, 128], F32)
    ones_col = alloc("ones_col", [128, 1], F32)
    rdall = alloc("rdall", [128, SLOTS * 2], I32)
    sall = alloc("sall", [128, SLOTS * 2], F32)
    toff_row = alloc("toff_row", [1, 8], F32)
    rtmp_t = alloc("rtmp", [128, 64], F32)
    ysc = [alloc("ysc0", [128, 512], F32, at=sb_off["aT_sb"]),
           alloc("ysc1", [128, 512], F32)]
    ytmp = None
    persist_end = off[0]

    tmp4k_t = [alloc("tmp4k0", [128, 1024], F32, at=WB_off),
               alloc("tmp4k1", [128, 1024], F32, at=WB_off + 4096)]
    xnT32_t = [alloc("xnT320", [128, 1024], F32, at=WB_off + 8192),
               alloc("xnT321", [128, 1024], F32, at=WB_off + 12288)]
    ao = WB_off
    kT = alloc("kT", [128, DCN * KC], BF16, at=ao); ao += DCN * KC * 2
    v_sb = alloc("v_sb", [128, VT * 1024], BF16, at=ao); ao += VT * 1024 * 2
    wattn = alloc("wattn", [128, 8 * 1024], BF16, at=ao); ao += 8 * 1024 * 2
    maskrep = alloc("maskrep", [128, KC], F32, at=ao); ao += KC * 4
    kmask_bf = alloc("kmask_bf", [1, KC + 64], BF16, at=ao); ao += (((KC + 64) * 2 + 31) & ~31)
    Sm, Ef, Eb, ETb = [], [], [], []
    for r in range(2):
        Sm.append(alloc(f"Sm{r}", [128, 192], F32, at=ao)); ao += 768
        Ef.append(alloc(f"Ef{r}", [128, 192], F32, at=ao)); ao += 768
        Eb.append(alloc(f"Eb{r}", [128, 192], BF16, at=ao)); ao += 384
        ETb.append(alloc(f"ET{r}", [128, 256], BF16, at=ao)); ao += 512
    astat = alloc("astat", [128, 8], F32, at=ao); ao += 32
    assert ao <= WB_end, f"attn overlay too big: {ao - WB_off}"
    XN_off = sb_off["xnT"]
    xbT = [alloc("xbT0", [128, 4096], BF16, at=XN_off),
           alloc("xbT1", [128, 4096], BF16, at=XN_off + 8192)]
    xbstg = [alloc("xbstg0", [128, 1024], BF16, at=XN_off + 16384),
             alloc("xbstg1", [128, 1024], BF16, at=XN_off + 18432)]
    grb = [alloc("grb0", [128, 1024], F32, at=XN_off),
           alloc("grb1", [128, 1024], F32, at=XN_off + 4096)]
    ytmp = [alloc("ytmp0", [128, 512], F32, at=sb_off["vb_bf"]),
            alloc("ytmp1", [128, 512], F32, at=sb_off["ob_bf"])]
    XN_ALL = [xnT, xbT[0], xbT[1], xbstg[0], xbstg[1], grb[0], grb[1],
              aT_sb, ysc[0], ytmp[0], ytmp[1], vb_bf, ob_bf]

    WB2 = WB_off + 16384
    b1nat = alloc("b1nat", [8, FF], F32, at=WB2)
    rowld = alloc("rowld", [8, D], F32, at=WB2 + 16384)
    qst = [alloc("qst0", [128, D], F32, at=WB2),
           alloc("qst1", [128, D], F32, at=WB2 + 4096)]
    qso = [alloc("qso0", [128, D], BF16, at=WB2 + 8192),
           alloc("qso1", [128, D], BF16, at=WB2 + 10240)]
    tstg = alloc("tstg", [128, 8 * 1024], BF16, at=WB2 + 12288)
    cva4 = [alloc(f"cv{i}", [128, 2048], F32, at=WB2 + i * 8192)
            for i in range(4)]
    cvo4 = [alloc(f"cvo{i}", [128, 2048], BF16, at=WB2 + 32768 + i * 4096)
            for i in range(4)]
    cvA, cvB, cvOA, cvOB = cva4[0], cva4[1], cvo4[0], cvo4[1]
    assert WB2 + 49152 <= WB_end

    WB_ALL = [w1B, w2B, tmp4k_t[0], tmp4k_t[1], xnT32_t[0], xnT32_t[1],
              kT, v_sb, wattn, maskrep, kmask_bf,
              Sm[0], Sm[1], Ef[0], Ef[1], Eb[0], Eb[1], ETb[0], ETb[1], astat,
              cva4[0], cva4[1], cva4[2], cva4[3],
              cvo4[0], cvo4[1], cvo4[2], cvo4[3], tstg, b1nat, rowld,
              qst[0], qst[1], qso[0], qso[1]]

    avail = nc.sbuf_top
    assert persist_end <= avail, f"SBUF over budget: {persist_end} > {avail}"

    pr = Prog(nc)
    W1s = [w1A, w1B]
    W2s = [w2A, w2B]


    pr.emit("pool", lambda g: g.memset(ones_f[:, :], 1.0), writes=[ones_f])
    pr.emit("pool", lambda g: g.memset(ones_bf[:, :], 1.0), writes=[ones_bf])
    for ident in (identf, identb):
        pr.emit("pool", lambda g, t=ident: g.memset(t[:, :], 0.0), writes=[ident])
        pr.emit("pool", lambda g, t=ident: g.affine_select(
            out=t[:, :], in_=t[:, :], compare_op=ALU.not_equal, fill=1.0,
            base=0, pattern=[[-1, 128]], channel_multiplier=1),
            reads=[ident], writes=[ident])
    pr.emit("pool", lambda g: g.memset(ltriu[:, :], 1.0), writes=[ltriu])
    pr.emit("pool", lambda g: g.affine_select(
        out=ltriu[:, :], in_=ltriu[:, :], compare_op=ALU.is_ge, fill=0.0,
        base=0, pattern=[[1, 128]], channel_multiplier=-1),
        reads=[ltriu], writes=[ltriu])
    pr.emit("pool", lambda g: g.memset(ones_col[:, :], 1.0), writes=[ones_col])
    pr.emit("pool", lambda g: g.memset(xbstg[0][:, :], 0.0), writes=[xbstg[0]])
    for i in range(NE * 640 // 128):
        pr.emit("act", lambda a, i=i: a.dma_start(
            out=xb_dram[i * 128:(i + 1) * 128, :], in_=xbstg[0][:, :]),
            reads=[xbstg[0]], writes=[xb_dram], dma="Q_act")
    pr.emit("pool", lambda g: g.memset(grb[0][:, :], 0.0), writes=[grb[0]])
    for i in range(NE * 640 // 128):
        pr.emit("act", lambda a, i=i: a.dma_start(
            out=y_raw[i * 128:(i + 1) * 128, :], in_=grb[0][:, :]),
            reads=[grb[0]], writes=[y_raw], dma="Q_act")
    pr.emit("pool", lambda g: g.memset(band[:, :], 0.0), writes=[band])
    pr.emit("pool", lambda g: g.affine_select(
        out=band[:, :], in_=band[:, :], compare_op=ALU.is_ge, fill=NEG,
        base=0, pattern=[[1, 192]], channel_multiplier=-1),
        reads=[band], writes=[band])
    pr.emit("pool", lambda g: g.affine_select(
        out=band[:, :], in_=band[:, :], compare_op=ALU.is_ge, fill=NEG,
        base=64, pattern=[[-1, 192]], channel_multiplier=1),
        reads=[band], writes=[band])

    cva = [cvA, cvB]
    cvo = [cvOA, cvOB]
    cast_engs = ["pool", "dve", "act"]

    def load_x_slots(c, slots):
        for slot in slots:
            row0 = c * CT * 128 + slot * 128
            src = d_x[row0:row0 + 128, :]
            dst = x_sb[:, slot * 1024:(slot + 1) * 1024]
            pr.emit("act", lambda a, src=src, dst=dst: a.dma_start(
                out=dst, in_=src), reads=[d_x], writes=[x_sb], dma="Q_act")

    load_x_slots(0, list(range(SLOTS)))

    def copy_fn(eng, dst_ap, src_ap):
        if eng == "act":
            return lambda E, d=dst_ap, s=src_ap: E.copy(out=d, in_=s)
        return lambda E, d=dst_ap, s=src_ap: E.tensor_copy(out=d, in_=s)

    ci = 0
    with nc.psum_tensor("bp0", [128, 8], F32) as bp0, \
         nc.psum_tensor("bp1", [128, 8], F32) as bp1:
        bp = [bp0, bp1]
        pr.emit("sp", lambda _: _.dma_start(out=b1nat[:, :], in_=d_b1),
                reads=[d_b1], writes=[b1nat], dma="Q_sp")
        for fc in range(FCN):
            t = bp[fc % 2]
            pr.emit("pe", lambda T, t=t, fc=fc: T.transpose(
                t[:, :], b1nat[0:8, fc * 128:(fc + 1) * 128], identf[0:8, 0:8]),
                reads=[b1nat, identf], writes=[t])
            pr.emit("dve", lambda v, t=t, fc=fc: v.tensor_copy(
                out=b1T[:, fc * 8:(fc + 1) * 8], in_=t[:, :]),
                reads=[t], writes=[b1T])
        for d_b, tgt in ((d_qb, qbT), (d_kb, kbT)):
            pr.emit("sp", lambda _, d_b=d_b: _.dma_start(
                out=rowld[0:1, :], in_=d_b[None, :]),
                reads=[d_b], writes=[rowld], dma="Q_sp")
            for dc in range(DCN):
                t = bp[dc % 2]
                pr.emit("pe", lambda T, t=t, dc=dc: T.transpose(
                    t[:, 0:1], rowld[0:1, dc * 128:(dc + 1) * 128],
                    identf[0:1, 0:1]), reads=[rowld, identf], writes=[t])
                pr.emit("dve", lambda v, t=t, tgt=tgt, dc=dc: v.tensor_copy(
                    out=tgt[:, dc:dc + 1], in_=t[:, 0:1]),
                    reads=[t], writes=[tgt])
        pr.emit("dve", lambda v: v.tensor_scalar(
            out=qb8T[:, :], in0=qbT[:, :], scalar1=0.125, scalar2=None,
            op0=ALU.mult), reads=[qbT], writes=[qb8T])
        pr.emit("sp", lambda _: _.dma_start(out=rowld[:, :], in_=d_b2),
                reads=[d_b2], writes=[rowld], dma="Q_sp")
        pr.emit("dve", lambda v: v.tensor_copy(out=b2_bf[:, :], in_=rowld[:, :]),
                reads=[rowld], writes=[b2_bf])
        pr.emit("sp", lambda _: _.dma_start(out=rowld[0:1, :], in_=d_vb[None, :]),
                reads=[d_vb], writes=[rowld], dma="Q_sp")
        pr.emit("dve", lambda v: v.tensor_copy(out=vb_bf[:, :], in_=rowld[0:1, :]),
                reads=[rowld], writes=[vb_bf])
        pr.emit("sp", lambda _: _.dma_start(out=rowld[0:1, :], in_=d_ob[None, :]),
                reads=[d_ob], writes=[rowld], dma="Q_sp")
        pr.emit("dve", lambda v: v.tensor_copy(out=ob_bf[:, :], in_=rowld[0:1, :]),
                reads=[rowld], writes=[ob_bf])
        pr.emit("act", lambda a: a.dma_start(out=c_vbo[0:1, :], in_=vb_bf[:, :]),
                reads=[vb_bf], writes=[c_vbo], dma="Q_act")
        pr.emit("act", lambda a: a.dma_start(out=c_vbo[1:2, :], in_=ob_bf[:, :]),
                reads=[ob_bf], writes=[c_vbo], dma="Q_act")
        pr.emit("sp", lambda _: _.dma_start(out=bg_f[:, :], in_=d_bg[None, :]),
                reads=[d_bg], writes=[bg_f], dma="Q_sp")
        for dc in range(DCN):
            src = d_wg[dc * 128:(dc + 1) * 128, :]
            pr.emit("sp", lambda _, src=src, dc=dc: _.dma_start(
                out=wg_sb[:, dc * 8:(dc + 1) * 8], in_=src),
                reads=[d_wg], writes=[wg_sb], dma="Q_sp")

    pr.claim([b1nat, rowld, qst[0], qst[1], qso[0], qso[1], tstg])
    with nc.psum_tensor("trp0", [128, 128], BF16) as trp0, \
         nc.psum_tensor("trp1", [128, 128], BF16) as trp1:
        trp = [trp0, trp1]
        for d_w, c_t, flip in ((d_qw, c_q, False), (d_kw, c_k, False),
                               (d_vw, c_v, True), (d_ow, c_o, True)):
            for ic in range(8):
                s, o = qst[ci % 2], qso[ci % 2]
                src = d_w[ic * 128:(ic + 1) * 128, :]
                pr.emit("act", lambda a, s=s, src=src: a.dma_start(
                    out=s[:, 0:D], in_=src), reads=[d_w], writes=[s], dma="Q_act")
                pr.emit("pool", lambda g, s=s, o=o: g.tensor_copy(
                    out=o[:, 0:D], in_=s[:, 0:D]), reads=[s], writes=[o])
                for dc in range(DCN):
                    t = trp[dc % 2]
                    col = (dc * 1024 + ic * 128) if flip else (ic * 1024 + dc * 128)
                    pr.emit("pe", lambda T, o=o, t=t, dc=dc: T.transpose(
                        t[:, :], o[:, dc * 128:(dc + 1) * 128], identb[:, :]),
                        reads=[o, identb], writes=[t])
                    pr.emit("dve", lambda v, t=t, col=col: v.tensor_copy(
                        out=tstg[:, col:col + 128], in_=t[:, :]),
                        reads=[t], writes=[tstg])
                ci += 1
            for blk in range(8):
                dst = c_t[blk]
                pr.emit("act", lambda a, dst=dst, blk=blk: a.dma_start(
                    out=dst, in_=tstg[:, blk * 1024:(blk + 1) * 1024]),
                    reads=[tstg], writes=[c_t], dma="Q_act")

    pr.claim([qst[0], qst[1], qso[0], qso[1], tstg,
              cva4[0], cva4[1], cva4[2], cva4[3],
              cvo4[0], cvo4[1], cvo4[2], cvo4[3]])
    ci = 0
    for e in range(NE):
        for dc in range(DCN):
            for hh in range(2):
                s, o = cva4[ci % 4], cvo4[ci % 4]
                src = d_w1[e, dc * 128:(dc + 1) * 128,
                           hh * 2048:(hh + 1) * 2048]
                dst = c_w1[e, dc * 128:(dc + 1) * 128,
                           hh * 2048:(hh + 1) * 2048]
                pr.emit("sp", lambda _, s=s, src=src: _.dma_start(
                    out=s[:, :], in_=src), reads=[d_w1], writes=[s],
                    dma="Q_sp")
                pr.emit(cast_engs[ci % 3], copy_fn(cast_engs[ci % 3],
                                                   o[:, :], s[:, :]),
                        reads=[s], writes=[o])
                pr.emit("act", lambda a, o=o, dst=dst: a.dma_start(
                    out=dst, in_=o[:, :]), reads=[o], writes=[f"cw1_{e}"],
                    dma="Q_act")
                ci += 1
        for fq in range(FF // 256):
            s, o = cva4[ci % 4], cvo4[ci % 4]
            src = d_w2[e, fq * 256:(fq + 1) * 256, :].rearrange(
                "(a p) d -> p a d", p=128)
            dst = c_w2[e, fq * 256:(fq + 1) * 256, :].rearrange(
                "(a p) d -> p a d", p=128)
            pr.emit("sp", lambda _, s=s, src=src: _.dma_start(
                out=s[:, 0:2048].rearrange("p (a d) -> p a d", a=2), in_=src),
                reads=[d_w2], writes=[s], dma="Q_sp")
            pr.emit(cast_engs[ci % 3], copy_fn(cast_engs[ci % 3],
                                               o[:, 0:2048], s[:, 0:2048]),
                    reads=[s], writes=[o])
            pr.emit("act", lambda a, o=o, dst=dst: a.dma_start(
                out=dst, in_=o[:, 0:2048].rearrange("p (a d) -> p a d", a=2)),
                reads=[o], writes=[f"cw2_{e}"], dma="Q_act")
            ci += 1


    def w_prefetch(e, half, slot):
        w1t, w2t = W1s[slot], W2s[slot]
        src1 = c_w1[e, :, half * 2048:(half + 1) * 2048].rearrange(
            "(a p) f -> p a f", p=128)
        src2 = c_w2[e, half * 2048:(half + 1) * 2048, :].rearrange(
            "(a p) d -> p a d", p=128)
        pr.emit("act", lambda a, w1t=w1t, src1=src1: a.dma_start(
            out=w1t[:, :].rearrange("p (a f) -> p a f", a=8), in_=src1),
            reads=[f"cw1_{e}"], writes=[w1t], dma="Q_act")
        pr.emit("act", lambda a, w2t=w2t, src2=src2: a.dma_start(
            out=w2t[:, :].rearrange("p (a d) -> p a d", a=16), in_=src2),
            reads=[f"cw2_{e}"], writes=[w2t], dma="Q_act")

    def ln_tile(slot, do_gate, tag, par=0):
        veng = "dve" if par % 2 == 0 else "pool"
        stats = stats_t[:, par * 16:par * 16 + 16]
        statk = f"stats{par}"
        tmp4k = tmp4k_t[par]
        xnT32 = xnT32_t[par]
        x_t = x_sb[:, slot * 1024:(slot + 1) * 1024]
        xncol = slot * 128
        pr.emit("dve", lambda v, x_t=x_t: v.reduce_sum(
            out=stats[:, 0:1], in_=x_t, axis=AX.X), reads=[x_sb], writes=[statk])
        pr.emit("act", lambda a, x_t=x_t: a.activation(
            out=tmp4k[:, :], in_=x_t, func=AF.Square), reads=[x_sb],
            writes=[tmp4k])
        pr.emit("dve", lambda v: v.reduce_sum(
            out=stats[:, 1:2], in_=tmp4k[:, :], axis=AX.X),
            reads=[tmp4k], writes=[statk])
        pr.emit(veng, lambda v: v.tensor_scalar(
            out=stats[:, 2:3], in0=stats[:, 0:1], scalar1=1.0 / D, scalar2=None,
            op0=ALU.mult), reads=[statk], writes=[statk])
        pr.emit(veng, lambda v: v.tensor_tensor(
            out=stats[:, 3:4], in0=stats[:, 2:3], in1=stats[:, 2:3],
            op=ALU.mult), reads=[statk], writes=[statk])
        pr.emit("dve", lambda v: v.tensor_scalar(
            out=stats[:, 4:5], in0=stats[:, 1:2], scalar1=1.0 / D,
            scalar2=stats[:, 3:4], op0=ALU.mult, op1=ALU.subtract),
            reads=[statk], writes=[statk])
        pr.emit(veng, lambda v: v.tensor_scalar(
            out=stats[:, 4:5], in0=stats[:, 4:5], scalar1=1e-5, scalar2=None,
            op0=ALU.add), reads=[statk], writes=[statk])
        pr.emit("act", lambda a: a.activation(
            out=stats[:, 5:6], in_=stats[:, 4:5], func=AF.Sqrt),
            reads=[statk], writes=[statk])
        pr.emit("dve", lambda v: v.reciprocal(
            out=stats[:, 6:7], in_=stats[:, 5:6]), reads=[statk], writes=[statk])
        pr.emit("dve", lambda v, x_t=x_t: v.tensor_scalar(
            out=tmp4k[:, :], in0=x_t, scalar1=stats[:, 2:3],
            scalar2=stats[:, 6:7], op0=ALU.subtract, op1=ALU.mult),
            reads=[x_sb, statk], writes=[tmp4k])
        with nc.psum_tensor("trf0" + tag, [128, 128], F32) as t0, \
             nc.psum_tensor("trf1" + tag, [128, 128], F32) as t1:
            trp = [t0, t1]
            for dc in range(DCN):
                if not do_gate:
                    t = trp[dc % 2]
                    pr.emit("pe", lambda T, t=t, dc=dc: T.transpose(
                        t[:, :], tmp4k[:, dc * 128:(dc + 1) * 128],
                        identf[:, :]), reads=[tmp4k, identf], writes=[t])
                    pr.emit("dve", lambda v, t=t, dc=dc, xncol=xncol:
                            v.tensor_copy(
                        out=xnT[:, dc * XNC + xncol:dc * XNC + xncol + 128],
                        in_=t[:, :]), reads=[t], writes=[xnT])
                else:
                    t = trp[dc % 2]
                    pr.emit("pe", lambda T, t=t, dc=dc: T.transpose(
                        t[:, :], tmp4k[:, dc * 128:(dc + 1) * 128],
                        identf[:, :]), reads=[tmp4k, identf], writes=[t])
                    pr.emit("dve", lambda v, t=t, dc=dc: v.tensor_copy(
                        out=xnT32[:, dc * 128:(dc + 1) * 128], in_=t[:, :]),
                        reads=[t], writes=[xnT32])
        if do_gate:
            pr.emit(veng, lambda v: v.tensor_copy(
                out=xbstg[par][:, :], in_=tmp4k[:, :]),
                reads=[tmp4k], writes=[xbstg[par]])
            gate_tile(slot, tag, par, stats, xnT32)
            route_tile(slot, tag, par)

    def gate_tile(slot, tag, par, stats, xnT32):
        veng = "dve" if par % 2 == 0 else "pool"
        gtmp = gtmp_t[:, par * 48:par * 48 + 48]
        statk = f"stats{par}"
        gtk = f"gtmp{par}"
        with nc.psum_tensor("gp" + tag, [128, 8], F32) as gp, \
             nc.psum_tensor("sTp" + tag, [8, 128], F32) as sTp, \
             nc.psum_tensor("b2p" + tag, [128, 1024], F32) as b2p:
            fns = []
            for dc in range(DCN):
                fns.append(lambda T, dc=dc: T.matmul(
                    gp[:, :], xnT32[:, dc * 128:(dc + 1) * 128],
                    wg_sb[:, dc * 8:(dc + 1) * 8], start=(dc == 0), stop=False))
            fns.append(lambda T: T.matmul(
                gp[:, :], ones_f[:, :], bg_f[:, :], start=False, stop=True))
            pr.emit("pe", fns, reads=[xnT32, wg_sb, ones_f, bg_f], writes=[gp])
            L = gtmp[:, 0:8]
            pr.emit("dve", lambda v: v.tensor_copy(out=L, in_=gp[:, :]),
                    reads=[gp], writes=[gtk])
            c_m1 = stats[:, 8:9]
            c_cnt = stats[:, 9:10]
            c_t2 = stats[:, 10:11]
            c_t2e = stats[:, 11:12]
            c_nm1 = stats[:, 12:13]
            c_den = stats[:, 13:14]
            c_rd2 = stats[:, 14:15]
            c_c2 = stats[:, 15:16]
            t8a = gtmp[:, 8:16]
            t8b = gtmp[:, 16:24]
            t8c = gtmp[:, 24:32]
            t8d = gtmp[:, 32:40]
            pr.emit("dve", lambda v: v.reduce_max(out=c_m1, in_=L, axis=AX.X),
                    reads=[gtk], writes=[statk])
            pr.emit("dve", lambda v: v.tensor_scalar(
                out=t8a, in0=L, scalar1=c_m1, scalar2=None, op0=ALU.is_ge),
                reads=[gtk, statk], writes=[gtk])
            pr.emit("dve", lambda v: v.reduce_sum(out=c_cnt, in_=t8a, axis=AX.X),
                    reads=[gtk], writes=[statk])
            pr.emit("dve", lambda v: v.scalar_tensor_tensor(
                out=t8b, in0=t8a, scalar=NEG, in1=L, op0=ALU.mult, op1=ALU.add),
                reads=[gtk], writes=[gtk])
            pr.emit("dve", lambda v: v.reduce_max(out=c_t2, in_=t8b, axis=AX.X),
                    reads=[gtk], writes=[statk])
            pr.emit(veng, lambda v: v.tensor_scalar(
                out=c_c2, in0=c_cnt, scalar1=2.0, scalar2=None, op0=ALU.is_ge),
                reads=[statk], writes=[statk])
            pr.emit(veng, lambda v: v.tensor_tensor(
                out=c_t2e, in0=c_m1, in1=c_t2, op=ALU.subtract),
                reads=[statk], writes=[statk])
            pr.emit("dve", lambda v: v.scalar_tensor_tensor(
                out=c_t2e, in0=c_t2e, scalar=c_c2, in1=c_t2,
                op0=ALU.mult, op1=ALU.add), reads=[statk], writes=[statk])
            pr.emit("dve", lambda v: v.tensor_scalar(
                out=t8c, in0=L, scalar1=c_t2e, scalar2=None, op0=ALU.is_ge),
                reads=[gtk, statk], writes=[gtk])
            pr.emit(veng, lambda v: v.tensor_scalar(
                out=c_nm1, in0=c_m1, scalar1=-1.0, scalar2=None, op0=ALU.mult),
                reads=[statk], writes=[statk])
            pr.emit("act", lambda a: a.activation(
                out=t8d, in_=L, func=AF.Exp, bias=c_nm1),
                reads=[gtk, statk], writes=[gtk])
            pr.emit(veng, lambda v: v.tensor_tensor(
                out=t8d, in0=t8d, in1=t8c, op=ALU.mult),
                reads=[gtk], writes=[gtk])
            pr.emit("dve", lambda v: v.reduce_sum(out=c_den, in_=t8d, axis=AX.X),
                    reads=[gtk], writes=[statk])
            pr.emit("dve", lambda v: v.reciprocal(out=c_rd2, in_=c_den),
                    reads=[statk], writes=[statk])
            pr.emit(veng, lambda v: v.tensor_scalar(
                out=c_rd2, in0=c_rd2, scalar1=0.5, scalar2=None, op0=ALU.mult),
                reads=[statk], writes=[statk])
            sg = s_gate[:, slot * 8:(slot + 1) * 8]
            pr.emit("dve", lambda v, sg=sg: v.tensor_scalar(
                out=sg, in0=t8d, scalar1=c_rd2, scalar2=None, op0=ALU.mult),
                reads=[gtk, statk], writes=[s_gate])
            pr.emit("pe", lambda T, sg=sg: T.transpose(
                sTp[:, :], sg, identf[:, :]), reads=[s_gate, identf],
                writes=[sTp])
            pr.emit("dve", lambda v: v.tensor_copy(out=sT_bf[:, :],
                                                   in_=sTp[:, :]),
                    reads=[sTp], writes=[sT_bf])
            pr.emit("pe", [
                lambda T: T.matmul(b2p[:, 0:512], sT_bf[:, :], b2_bf[:, 0:512],
                                   start=True, stop=True),
                lambda T: T.matmul(b2p[:, 512:1024], sT_bf[:, :],
                                   b2_bf[:, 512:1024], start=True, stop=True)],
                reads=[sT_bf, b2_bf], writes=[b2p])
            x_t = x_sb[:, slot * 1024:(slot + 1) * 1024]
            pr.emit("dve", lambda v, x_t=x_t: v.tensor_tensor(
                out=x_t, in0=x_t, in1=b2p[:, :], op=ALU.add),
                reads=[x_sb, b2p], writes=[x_sb])

    def init_toff():
        pr.emit("pool", lambda g: g.iota(
            rtmp_t[:, 0:8], [[640, 8]], base=int(BIG), channel_multiplier=0,
            allow_small_or_imprecise_dtypes=True), writes=["rtmp0"])
        pr.emit("dve", lambda v: v.tensor_copy(
            out=toff_row[:, :], in_=rtmp_t[0:1, 0:8]),
            reads=["rtmp0"], writes=[toff_row])

    def route_tile(slot, tag, par):
        veng = "dve" if par % 2 == 0 else "pool"
        rt = rtmp_t[:, par * 32:par * 32 + 32]
        rk = f"rtmp{par}"
        sg = s_gate[:, slot * 8:(slot + 1) * 8]
        mk = rt[:, 0:8]
        rdest = rt[:, 8:16]
        dst2 = rt[:, 16:24]
        m0 = rt[:, 28:29]
        m1 = rt[:, 29:30]
        with nc.psum_tensor("csum" + tag, [128, 8], F32) as csum, \
             nc.psum_tensor("ttot" + tag, [1, 8], F32) as ttot:
            pr.emit(veng, lambda v: v.tensor_scalar(
                out=mk, in0=sg, scalar1=0.0, scalar2=None, op0=ALU.is_gt),
                reads=[s_gate], writes=[rk])
            pr.emit("pe", [
                lambda T: T.matmul(csum[:, :], ltriu[:, :], mk,
                                   start=True, stop=False),
                lambda T: T.matmul(csum[:, :], ones_f[:, :], toff_row[:, :],
                                   start=False, stop=True)],
                reads=[ltriu, rk, ones_f, toff_row], writes=[csum])
            pr.emit("pe", lambda T: T.matmul(
                ttot[:, :], ones_col[:, :], mk, start=True, stop=True),
                reads=[ones_col, rk], writes=[ttot])
            pr.emit("dve", lambda v: v.tensor_tensor(
                out=toff_row[:, :], in0=toff_row[:, :], in1=ttot[:, :],
                op=ALU.add), reads=[toff_row, ttot], writes=[toff_row])
            pr.emit("dve", lambda v: v.scalar_tensor_tensor(
                out=rdest, in0=mk, scalar=-(BIG + 1.0), in1=csum[:, :],
                op0=ALU.mult, op1=ALU.add), reads=[rk, csum], writes=[rk])
            pr.emit("dve", lambda v: v.tensor_reduce(
                out=m0, in_=rdest, axis=AX.X, op=ALU.min),
                reads=[rk], writes=[rk])
            pr.emit("dve", lambda v: v.tensor_scalar(
                out=dst2, in0=rdest, scalar1=m0, scalar2=None, op0=ALU.is_le),
                reads=[rk], writes=[rk])
            pr.emit(veng, lambda v: v.tensor_tensor(
                out=rt[:, 24:32] if False else dst2, in0=dst2, in1=sg,
                op=ALU.mult), reads=[rk, s_gate], writes=[rk])
            pr.emit("dve", lambda v: v.reduce_sum(
                out=sall[:, slot * 2:slot * 2 + 1], in_=dst2, axis=AX.X),
                reads=[rk], writes=[sall])
            pr.emit(veng, lambda v: v.tensor_copy(
                out=rdall[:, slot * 2:slot * 2 + 1], in_=m0),
                reads=[rk], writes=[rdall])
            pr.emit("dve", lambda v: v.tensor_scalar(
                out=dst2, in0=rdest, scalar1=m0, scalar2=None, op0=ALU.is_le),
                reads=[rk], writes=[rk])
            pr.emit("dve", lambda v: v.scalar_tensor_tensor(
                out=dst2, in0=dst2, scalar=BIG2, in1=rdest,
                op0=ALU.mult, op1=ALU.add), reads=[rk], writes=[rk])
            pr.emit("dve", lambda v: v.tensor_reduce(
                out=m1, in_=dst2, axis=AX.X, op=ALU.min),
                reads=[rk], writes=[rk])
            pr.emit("dve", lambda v: v.tensor_scalar(
                out=dst2, in0=dst2, scalar1=m1, scalar2=None, op0=ALU.is_le),
                reads=[rk], writes=[rk])
            pr.emit(veng, lambda v: v.tensor_tensor(
                out=dst2, in0=dst2, in1=sg, op=ALU.mult),
                reads=[rk, s_gate], writes=[rk])
            pr.emit("dve", lambda v: v.reduce_sum(
                out=sall[:, slot * 2 + 1:slot * 2 + 2], in_=dst2, axis=AX.X),
                reads=[rk], writes=[sall])
            pr.emit(veng, lambda v: v.tensor_copy(
                out=rdall[:, slot * 2 + 1:slot * 2 + 2], in_=m1),
                reads=[rk], writes=[rdall])
        for r in range(2):
            roff = rdall[:, slot * 2 + r:slot * 2 + r + 1]
            pr.emit("pool", lambda g, roff=roff: g.indirect_dma_start(
                xb_dram[:, :], IndirectOffsetOnAxis(roff, 0),
                xbstg[par][:, :], None),
                reads=[xbstg[par], rdall], writes=[xb_dram], dma="Q_pool")

    def ffn_pass(slots, next_first, tag, slot_sched=None):
        steps = [(e, h) for e in range(NE) for h in range(2)]
        if slot_sched is None:
            slot_sched = [i % 2 for i in range(len(steps))]

        def gather_load(e, m):
            stg = xbstg[m % 2]
            srcr = xb_dram[e * 640 + m * 128:e * 640 + m * 128 + 128, :]
            pr.emit("sp", lambda _, stg=stg, srcr=srcr: _.dma_start(
                out=stg[:, :], in_=srcr),
                reads=[xb_dram], writes=[stg], dma="Q_sp")

        def gather_tr(e, m, tg):
            par = e % 2
            stg = xbstg[m % 2]
            for dc in range(DCN):
                t = tg[dc % 2]
                pr.emit("pe", lambda T, t=t, stg=stg, dc=dc: T.transpose(
                    t[:, :], stg[:, dc * 128:(dc + 1) * 128],
                    identb[:, :]), reads=[stg, identb], writes=[t])
                dst = xbT[par][:, dc * 512 + m * 128:dc * 512 + m * 128 + 128]
                pr.emit("dve", lambda v, t=t, dst=dst: v.tensor_copy(
                    out=dst, in_=t[:, :]), reads=[t], writes=[xbT[par]])

        zrot = [0]
        LAG = 2
        with nc.psum_tensor("zp0" + tag, [128, 512], F32) as z0, \
             nc.psum_tensor("zp1" + tag, [128, 512], F32) as z1, \
             nc.psum_tensor("yp0" + tag, [128, 1024], F32) as y0, \
             nc.psum_tensor("yp1" + tag, [128, 1024], F32) as y1, \
             nc.psum_tensor("tg0" + tag, [128, 128], BF16) as tg0, \
             nc.psum_tensor("tg1" + tag, [128, 128], BF16) as tg1:
            tg = [tg0, tg1]
            for m in range(4):
                gather_load(0, m)
                gather_tr(0, m, tg)
            zp = [z0, z1]
            yp = [y0, y1]
            for step, (e, half) in enumerate(steps):
                pf_late = None
                nf_late = False
                if step + 1 < len(steps):
                    e2, h2 = steps[step + 1]
                    if slot_sched[step + 1] != slot_sched[step]:
                        w_prefetch(e2, h2, slot_sched[step + 1])
                    else:
                        pf_late = (e2, h2, slot_sched[step + 1])
                elif next_first is not None:
                    if slot_sched[step] != 0:
                        next_first()
                    else:
                        nf_late = True
                hooks = {}
                if e + 1 < NE:
                    if half == 0:
                        hooks[0] = [("L", 0), ("L", 1)]
                        hooks[11] = [("T", 0), ("L", 2)]
                        hooks[22] = [("T", 1), ("L", 3)]
                    else:
                        hooks[0] = [("T", 2)]
                        hooks[11] = [("T", 3)]
                ws = slot_sched[step]
                w1t, w2t = W1s[ws], W2s[ws]
                xbe = xbT[e % 2]
                groups = [(0, 1), (2, 3)]
                seq = [(g, fc) for g in groups for fc in range(16)]

                def emit_z(i):
                    g, fc = seq[i]
                    z = zp[i % 2]
                    h = hT[i % 3]
                    fns = []
                    for dc in range(DCN):
                        fns.append(lambda T, z=z, dc=dc, fc=fc, g=g,
                                   w1t=w1t, xbe=xbe: T.matmul(
                            z[:, 0:256],
                            w1t[:, dc * 2048 + fc * 128:
                                dc * 2048 + fc * 128 + 128],
                            xbe[:, dc * 512 + g[0] * 128:
                                dc * 512 + g[0] * 128 + 256],
                            start=(dc == 0), stop=(dc == DCN - 1)))
                    pr.emit("pe", fns, reads=[w1t, xbe], writes=[z])
                    fcg = half * 16 + fc
                    bcol = b1T[:, fcg * 8 + e:fcg * 8 + e + 1]
                    pr.emit("act", lambda a, z=z, h=h, bcol=bcol:
                            a.activation(out=h[:, 0:256], in_=z[:, 0:256],
                                         func=AF.Gelu, bias=bcol),
                            reads=[z, b1T], writes=[h])

                def emit_y(i):
                    g, fc = seq[i]
                    h = hT[i % 3]
                    for ti in range(2):
                        yb = yp[ti]
                        yfns = []
                        for hf in range(2):
                            yfns.append(lambda T, yb=yb, h=h, ti=ti, fc=fc,
                                        hf=hf, w2t=w2t: T.matmul(
                                yb[:, hf * 512:(hf + 1) * 512],
                                h[:, ti * 128:(ti + 1) * 128],
                                w2t[:, fc * 1024 + hf * 512:
                                    fc * 1024 + hf * 512 + 512],
                                start=(fc == 0), stop=(fc == 15)))
                        pr.emit("pe", yfns, reads=[h, w2t], writes=[yb])
                    if fc == 15:
                        for ti in range(2):
                            yb = yp[ti]
                            m = g[ti]
                            for hf in range(2):
                                dst = y_raw[e * 640 + m * 128:
                                            e * 640 + m * 128 + 128,
                                            hf * 512:(hf + 1) * 512]
                                pr.emit("dve", lambda v, yb=yb, hf=hf:
                                        v.tensor_copy(
                                    out=ysc[hf][:, :],
                                    in_=yb[:, hf * 512:(hf + 1) * 512]),
                                    reads=[yb], writes=[ysc[hf]])
                                if half == 0:
                                    pr.emit("act", lambda a, dst=dst, hf=hf:
                                            a.dma_start(out=dst,
                                                        in_=ysc[hf][:, :]),
                                        reads=[ysc[hf]], writes=[y_raw],
                                        dma="Q_act")
                                else:
                                    pr.emit("pool", lambda g, dst=dst, hf=hf:
                                            g.dma_start(out=dst,
                                                        in_=ysc[hf][:, :],
                                                        accum_op=ALU.add),
                                        reads=[ysc[hf]], writes=[y_raw],
                                        dma="Q_pool")

                n = len(seq)
                for i in range(n + LAG):
                    for kind, m in hooks.get(i, ()):
                        if kind == "L":
                            gather_load(e + 1, m)
                        else:
                            gather_tr(e + 1, m, tg)
                    if i < n:
                        emit_z(i)
                    if i >= LAG:
                        emit_y(i - LAG)
                if pf_late is not None:
                    w_prefetch(*pf_late)
                if nf_late:
                    next_first()
        pr.claim([xbT[0], grb[0], grb[1]])
        gi = 0
        for slot in slots:
            x_t = x_sb[:, slot * 1024:(slot + 1) * 1024]
            for r in range(2):
                gb = grb[gi % 2]
                gi += 1
                roff = rdall[:, slot * 2 + r:slot * 2 + r + 1]
                pr.emit("pool", lambda g, gb=gb, roff=roff:
                        g.indirect_dma_start(
                    gb[:, :], None, y_raw[:, :],
                    IndirectOffsetOnAxis(roff, 0)),
                    reads=[y_raw, rdall], writes=[gb], dma="Q_pool")
                scol = sall[:, slot * 2 + r:slot * 2 + r + 1]
                pr.emit("dve", lambda v, gb=gb, x_t=x_t, scol=scol:
                        v.scalar_tensor_tensor(
                    out=x_t, in0=gb[:, :], scalar=scol, in1=x_t,
                    op0=ALU.mult, op1=ALU.add),
                    reads=[gb, sall, x_sb], writes=[x_sb])

    def chunk_masks(c):
        start = c * CT * 128 + 96
        src2 = d_mask[start:start + KC + 64]
        pr.emit("act", lambda a, src2=src2: a.dma_start(
            out=kmask_bf[:, :], in_=src2[None, :]),
            reads=[d_mask], writes=[kmask_bf], dma="Q_act")
        with nc.psum_tensor(f"vmp{c}", [128, 1], BF16) as vmp:
            for m in range(VT):
                pr.emit("pe", lambda T, m=m: T.transpose(
                    vmp[:, :], kmask_bf[0:1, m * 128:(m + 1) * 128],
                    identb[0:1, 0:1]), reads=[kmask_bf, identb], writes=[vmp])
                pr.emit("dve", lambda v, m=m: v.tensor_copy(
                    out=vmask[:, m:m + 1], in_=vmp[:, :]),
                    reads=[vmp], writes=[vmask])
        with nc.psum_tensor(f"mrp{c}", [128, 512], F32) as mrp:
            nsl = (KC + 511) // 512
            for s in range(nsl):
                wdt = min(512, KC - s * 512)
                pr.emit("pe", lambda T, s=s, wdt=wdt: T.matmul(
                    mrp[:, 0:wdt], ones_bf[:, :],
                    kmask_bf[:, s * 512:s * 512 + wdt], start=True, stop=True),
                    reads=[ones_bf, kmask_bf], writes=[mrp])
                pr.emit("dve", lambda v, s=s, wdt=wdt: v.tensor_copy(
                    out=maskrep[:, s * 512:s * 512 + wdt], in_=mrp[:, 0:wdt]),
                    reads=[mrp], writes=[maskrep])

    def load_wattn(c_t):
        src = c_t[:, :, :].rearrange("a p d -> p a d")
        pr.emit("sp", lambda _, src=src: _.dma_start(
            out=wattn[:, :].rearrange("p (a d) -> p a d", a=8), in_=src),
            reads=[c_t], writes=[wattn], dma="Q_sp")

    def qkv_phase(c):
        for slot in range(SLOTS):
            ln_tile(slot, do_gate=False, tag=f"n2_{c}_{slot}", par=slot % 2)
        pr.claim([tmp4k_t[0], tmp4k_t[1], xnT32_t[0], xnT32_t[1], kT])
        load_wattn(c_k)
        slices = []
        pos = 0
        while pos < KC:
            wdt = min(512, KC - pos)
            slices.append((pos, wdt))
            pos += wdt
        with nc.psum_tensor(f"kqp0_{c}", [128, 512], F32) as kp0, \
             nc.psum_tensor(f"kqp1_{c}", [128, 512], F32) as kp1:
            kp = [kp0, kp1]
            r = 0
            for ic in range(8):
                for (pos, wdt) in slices:
                    p = kp[r % 2]
                    r += 1
                    xc = 96 + pos
                    fns = []
                    for dc in range(DCN):
                        fns.append(lambda T, p=p, dc=dc, ic=ic, xc=xc, wdt=wdt:
                                   T.matmul(
                            p[:, 0:wdt],
                            wattn[:, ic * 1024 + dc * 128:
                                  ic * 1024 + dc * 128 + 128],
                            xnT[:, dc * XNC + xc:dc * XNC + xc + wdt],
                            start=(dc == 0), stop=(dc == DCN - 1)))
                    pr.emit("pe", fns, reads=[wattn, xnT], writes=[p])
                    kb = kbT[:, ic:ic + 1]
                    dst = kT[:, ic * KC + pos:ic * KC + pos + wdt]
                    mr = maskrep[:, pos:pos + wdt]
                    pr.emit("dve", lambda v, p=p, dst=dst, kb=kb, mr=mr,
                            wdt=wdt: v.scalar_tensor_tensor(
                        out=dst, in0=p[:, 0:wdt], scalar=kb, in1=mr,
                        op0=ALU.add, op1=ALU.mult),
                        reads=[p, kbT, maskrep], writes=[kT])
        load_wattn(c_v)
        with nc.psum_tensor(f"vp0_{c}", [128, 1024], F32) as vp0, \
             nc.psum_tensor(f"vp1_{c}", [128, 1024], F32) as vp1:
            vp = [vp0, vp1]
            for m in range(VT):
                p = vp[m % 2]
                xc = 96 + m * 128
                fns = []
                for dc in range(DCN):
                    for hf in range(2):
                        fns.append(lambda T, p=p, dc=dc, hf=hf, xc=xc: T.matmul(
                            p[:, hf * 512:(hf + 1) * 512],
                            xnT[:, dc * XNC + xc:dc * XNC + xc + 128],
                            wattn[:, dc * 1024 + hf * 512:
                                  dc * 1024 + hf * 512 + 512],
                            start=(dc == 0), stop=False))
                for hf in range(2):
                    fns.append(lambda T, p=p, hf=hf: T.matmul(
                        p[:, hf * 512:(hf + 1) * 512], ones_bf[:, :],
                        vb_bf[:, hf * 512:(hf + 1) * 512],
                        start=False, stop=True))
                pr.emit("pe", fns, reads=[wattn, xnT, ones_bf, vb_bf],
                        writes=[p])
                dst = v_sb[:, m * 1024:(m + 1) * 1024]
                vm = vmask[:, m:m + 1]
                pr.emit("dve", lambda v, p=p, dst=dst, vm=vm: v.tensor_scalar(
                    out=dst, in0=p[:, :], scalar1=vm, scalar2=None,
                    op0=ALU.mult), reads=[p, vmask], writes=[v_sb])
        load_wattn(c_q)
        nsl = max(CT * 128 // 512, 1)
        with nc.psum_tensor(f"qp_{c}", [128, 8 * 512], F32) as qp:
            for s in range(nsl):
                wdt = min(512, CT * 128 - s * 512)
                xc = 128 + s * 512
                for ic in range(8):
                    fns = []
                    for dc in range(DCN):
                        fns.append(lambda T, ic=ic, dc=dc, xc=xc, wdt=wdt:
                                   T.matmul(
                            qp[:, ic * 512:ic * 512 + wdt],
                            wattn[:, ic * 1024 + dc * 128:
                                  ic * 1024 + dc * 128 + 128],
                            xnT[:, dc * XNC + xc:dc * XNC + xc + wdt],
                            start=(dc == 0), stop=(dc == DCN - 1)))
                    pr.emit("pe", fns, reads=[wattn, xnT], writes=[qp])
                for ic in range(8):
                    qb = qb8T[:, ic:ic + 1]
                    dst = xnT[:, ic * XNC + xc:ic * XNC + xc + wdt]
                    pr.emit("dve", lambda v, ic=ic, dst=dst, qb=qb, wdt=wdt:
                            v.tensor_scalar(
                        out=dst, in0=qp[:, ic * 512:ic * 512 + wdt],
                        scalar1=0.125, scalar2=qb, op0=ALU.mult, op1=ALU.add),
                        reads=[qp, qb8T], writes=[xnT])

    def attn_phase(c):
        load_wattn(c_o)
        with nc.psum_tensor(f"Sp0_{c}", [128, 192], F32) as S0, \
             nc.psum_tensor(f"Sp1_{c}", [128, 192], F32) as S1, \
             nc.psum_tensor(f"ETp0_{c}", [128, 256], BF16) as E0, \
             nc.psum_tensor(f"ETp1_{c}", [128, 256], BF16) as E1, \
             nc.psum_tensor(f"ATp0_{c}", [128, 128], F32) as A0, \
             nc.psum_tensor(f"ATp1_{c}", [128, 128], F32) as A1, \
             nc.psum_tensor(f"Op_{c}", [128, 1024], F32) as Op:
            Sp = [S0, S1]
            Ep = [E0, E1]
            Ap = [A0, A1]
            for t in range(CT):
                qcol = 128 + t * 128
                for h in range(NH):
                    r = (t * NH + h) % 2
                    ic, sub = h // 2, (h % 2) * 64
                    Spp, Epp = Sp[r], Ep[r]
                    App = Ap[((t * NH + h) // 2) % 2]
                    kpos = t * 128
                    pr.emit("pe", lambda T, Spp=Spp, ic=ic, sub=sub, qcol=qcol,
                            kpos=kpos: T.matmul(
                        Spp[:, :],
                        xnT[sub:sub + 64,
                            ic * XNC + qcol:ic * XNC + qcol + 128],
                        kT[sub:sub + 64, ic * KC + kpos:ic * KC + kpos + 192],
                        start=True, stop=True),
                        reads=[xnT, kT], writes=[Spp])
                    pr.emit("dve", lambda v, Spp=Spp, r=r: v.tensor_tensor(
                        out=Sm[r][:, :], in0=Spp[:, :], in1=band[:, :],
                        op=ALU.add), reads=[Spp, band], writes=[Sm[r]])
                    pr.emit("dve", lambda v, r=r: v.reduce_max(
                        out=astat[:, r:r + 1], in_=Sm[r][:, :], axis=AX.X,
                        negate=True), reads=[Sm[r]], writes=[astat])
                    pr.emit("act", lambda a, r=r: a.activation(
                        out=Ef[r][:, :], in_=Sm[r][:, :], func=AF.Exp,
                        bias=astat[:, r:r + 1]), reads=[Sm[r], astat],
                        writes=[Ef[r]])
                    pr.emit("dve", lambda v, r=r: v.reduce_sum(
                        out=astat[:, 2 + r:3 + r], in_=Ef[r][:, :], axis=AX.X),
                        reads=[Ef[r]], writes=[astat])
                    pr.emit("dve", lambda v, r=r: v.reciprocal(
                        out=astat[:, 4 + r:5 + r], in_=astat[:, 2 + r:3 + r]),
                        reads=[astat], writes=[astat])
                    pr.emit("dve", lambda v, r=r: v.tensor_scalar(
                        out=Eb[r][:, :], in0=Ef[r][:, :],
                        scalar1=astat[:, 4 + r:5 + r], scalar2=None,
                        op0=ALU.mult), reads=[Ef[r], astat], writes=[Eb[r]])
                    pr.emit("pe", [
                        lambda T, Epp=Epp, r=r: T.transpose(
                            Epp[:, 0:128], Eb[r][:, 0:128], identb[:, :]),
                        lambda T, Epp=Epp, r=r: T.transpose(
                            Epp[0:64, 128:256], Eb[r][:, 128:192],
                            identb[:, :])],
                        reads=[Eb[r], identb], writes=[Epp])
                    pr.emit("dve", [
                        lambda v, Epp=Epp, r=r: v.tensor_copy(
                            out=ETb[r][:, 0:128], in_=Epp[:, 0:128]),
                        lambda v, Epp=Epp, r=r: v.tensor_copy(
                            out=ETb[r][0:64, 128:256], in_=Epp[0:64, 128:256])],
                        reads=[Epp], writes=[ETb[r]])
                    pr.emit("pe", [
                        lambda T, App=App, r=r, t=t, h=h, sub=sub: T.matmul(
                            App[sub:sub + 64, :],
                            v_sb[:, t * 1024 + h * 64:t * 1024 + h * 64 + 64],
                            ETb[r][:, 0:128], start=True, stop=False),
                        lambda T, App=App, r=r, t=t, h=h, sub=sub: T.matmul(
                            App[sub:sub + 64, :],
                            v_sb[0:64, (t + 1) * 1024 + h * 64:
                                 (t + 1) * 1024 + h * 64 + 64],
                            ETb[r][0:64, 128:256], start=False, stop=True)],
                        reads=[v_sb, ETb[r]], writes=[App])
                    if h % 2 == 1:
                        ic2 = h // 2
                        pr.emit("dve", lambda v, App=App, ic2=ic2: v.tensor_copy(
                            out=aT_sb[:, ic2 * 128:ic2 * 128 + 128],
                            in_=App[:, :]), reads=[App], writes=[aT_sb])
                fns = []
                for fc in range(8):
                    for hf in range(2):
                        fns.append(lambda T, fc=fc, hf=hf: T.matmul(
                            Op[:, hf * 512:(hf + 1) * 512],
                            aT_sb[:, fc * 128:fc * 128 + 128],
                            wattn[:, fc * 1024 + hf * 512:
                                  fc * 1024 + hf * 512 + 512],
                            start=(fc == 0), stop=False))
                for hf in range(2):
                    fns.append(lambda T, hf=hf: T.matmul(
                        Op[:, hf * 512:(hf + 1) * 512], ones_bf[:, :],
                        ob_bf[:, hf * 512:(hf + 1) * 512],
                        start=False, stop=True))
                pr.emit("pe", fns, reads=[aT_sb, wattn, ones_bf, ob_bf],
                        writes=[Op])
                slot = t + 1
                x_t = x_sb[:, slot * 1024:(slot + 1) * 1024]
                pr.emit("dve", lambda v, x_t=x_t: v.tensor_tensor(
                    out=x_t, in0=x_t, in1=Op[:, :], op=ALU.add),
                    reads=[x_sb, Op], writes=[x_sb])

    def store_and_carry(c):
        for slot in range(1, SLOTS - 1):
            row0 = (c * CT + slot - 1) * 128
            dst = d_y[row0:row0 + 128, :]
            src = x_sb[:, slot * 1024:(slot + 1) * 1024]
            pr.emit("act", lambda a, dst=dst, src=src: a.dma_start(
                out=dst, in_=src), reads=[x_sb], writes=[d_y], dma="Q_act")
        if c < cfg.NCHUNK - 1:
            pr.emit("dve", lambda v: v.tensor_copy(
                out=x_sb[:, 0:1024],
                in_=x_sb[:, (SLOTS - 2) * 1024:(SLOTS - 1) * 1024]),
                reads=[x_sb], writes=[x_sb])
            pr.emit("dve", lambda v: v.tensor_copy(
                out=x_sb[:, 1024:2048],
                in_=x_sb[:, (SLOTS - 1) * 1024:SLOTS * 1024]),
                reads=[x_sb], writes=[x_sb])

    def stop(name):
        return stop_after == name

    for c in range(cfg.NCHUNK):
        ffn1_slots = list(range(SLOTS)) if c == 0 else list(range(2, SLOTS))
        ffn2_slots = list(range(1, SLOTS - 1))

        if stop("conv"):
            break
        if c > 0:
            pr.phase(f"chunk{c}")
            load_x_slots(c, ffn1_slots)
        pr.claim(WB_ALL)
        if c == 0:
            w_prefetch(0, 0, 0)
        if stop("ln1load"):
            break
        pr.claim(XN_ALL)
        init_toff()
        for i, slot in enumerate(ffn1_slots):
            ln_tile(slot, do_gate=True, tag=f"a_{c}_{slot}", par=i % 2)
        if stop("ln1"):
            break

        pr.claim(WB_ALL)
        ffn_pass(ffn1_slots, lambda: w_prefetch(0, 0, 0), tag=f"f1_{c}")
        if stop("ffn1"):
            break

        pr.claim(WB_ALL)
        pr.claim(XN_ALL)
        pr.emit("sp", lambda _: _.dma_start(out=vb_bf[:, :], in_=c_vbo[0:1, :]),
                reads=[c_vbo], writes=[vb_bf], dma="Q_sp")
        pr.emit("sp", lambda _: _.dma_start(out=ob_bf[:, :], in_=c_vbo[1:2, :]),
                reads=[c_vbo], writes=[ob_bf], dma="Q_sp")
        chunk_masks(c)
        qkv_phase(c)
        if stop("qkv"):
            break

        attn_phase(c)
        if stop("attn"):
            break

        pr.claim(WB_ALL)
        pr.claim(XN_ALL)
        init_toff()
        for i, slot in enumerate(ffn2_slots):
            ln_tile(slot, do_gate=True, tag=f"b_{c}_{slot}", par=i % 2)
        if stop("ln3"):
            break

        pr.claim(WB_ALL)
        nxt = (lambda: w_prefetch(0, 0, 0)) if c < cfg.NCHUNK - 1 else None
        ffn_pass(ffn2_slots, nxt, tag=f"f2_{c}")
        if stop("ffn2"):
            break

        store_and_carry(c)
        if debug_stage == c:
            for slot in range(SLOTS):
                dst = d_dbg[slot * 128:(slot + 1) * 128, :]
                src = x_sb[:, slot * 1024:(slot + 1) * 1024]
                pr.emit("act", lambda a, dst=dst, src=src: a.dma_start(
                    out=dst, in_=src), reads=[x_sb], writes=["dbg"],
                    dma="Q_act")
            break

    pr.phase("fin")
    pr.finish()
    return nc



def _shard(x, cfg, ncores=8):
    B, S, _ = x.shape
    own = cfg.OT * 128
    nq = S // own
    halo = 128
    slabs = []
    for core in range(ncores):
        b, q = divmod(core, nq)
        lo = q * own - halo
        hi = q * own + own + halo
        slab = np.zeros((own + 2 * halo, D), np.float32)
        m = np.zeros((own + 2 * halo,), ml_dtypes.bfloat16)
        s0, s1 = max(lo, 0), min(hi, S)
        slab[s0 - lo:s1 - lo] = x[b, s0:s1]
        m[s0 - lo:s1 - lo] = 1.0
        slabs.append((np.ascontiguousarray(slab), m))
    return slabs


def _weight_map(L0_attn_q_w, L0_attn_q_b, L0_attn_k_w, L0_attn_k_b,
                L0_attn_v_w, L0_attn_v_b, L0_attn_o_w, L0_attn_o_b,
                L0_ffn_W1, L0_ffn_b1, L0_ffn_W2, L0_ffn_b2, L0_ffn_Wg,
                L0_ffn_bg):
    f = lambda a: np.ascontiguousarray(np.asarray(a, np.float32))
    return {
        "q_w": f(L0_attn_q_w), "q_b": f(L0_attn_q_b),
        "k_w": f(L0_attn_k_w), "k_b": f(L0_attn_k_b),
        "v_w": f(L0_attn_v_w), "v_b": f(L0_attn_v_b),
        "o_w": f(L0_attn_o_w), "o_b": f(L0_attn_o_b),
        "W1": f(L0_ffn_W1), "b1": f(L0_ffn_b1),
        "W2": f(L0_ffn_W2), "b2": f(L0_ffn_b2),
        "Wg": f(L0_ffn_Wg), "bg": f(L0_ffn_bg),
    }


def kernel(L0_attn_hidden_states, **w):
    cfg = Cfg(own_tiles=32, chunk_tiles=8)
    nc = build_program(cfg)
    x = np.asarray(L0_attn_hidden_states, np.float32)
    slabs = _shard(x, cfg)
    weights = _weight_map(**w)
    in_maps = []
    for core in range(8):
        slab, m = slabs[core]
        d = dict(weights)
        d["x_slab"] = slab
        d["kvmask"] = m
        in_maps.append(d)
    res = run_bass_kernel_spmd(nc, in_maps, core_ids=list(range(8)))
    out = np.empty((2, 16384, 1024), np.float32)
    own = cfg.OT * 128
    for core in range(8):
        b, q = divmod(core, 4)
        out[b, q * own:(q + 1) * own] = res.results[core]["y"]
    return out


# version 23
# speedup vs baseline: 1.1765x; 1.0294x over previous
import numpy as np
import ml_dtypes

import concourse.bass as bass
import concourse.mybir as mybir
from concourse.bass_utils import run_bass_kernel_spmd
from concourse.bass_core_rust import IndirectOffsetOnAxis

F32 = mybir.dt.float32
I32 = mybir.dt.int32
BF16 = mybir.dt.bfloat16
AF = mybir.ActivationFunctionType
ALU = mybir.AluOpType
AX = mybir.AxisListType

D = 1024
DCN = 8
FF = 4096
FCN = 32
NE = 8
NH = 16
HD = 64
NEG = -1.0e30
BIG = float(1 << 20)
BIG2 = float(1 << 22)


class Cfg:
    def __init__(self, own_tiles=32, chunk_tiles=8):
        self.OT = own_tiles
        self.CT = chunk_tiles
        assert own_tiles % chunk_tiles == 0
        self.NCHUNK = own_tiles // chunk_tiles
        self.SLOTS = chunk_tiles + 2
        self.SLAB = own_tiles + 2
        self.KC = chunk_tiles * 128 + 64
        self.VT = chunk_tiles + 1



BLOCKNAME = {"pe": "tensor", "dve": "vector", "act": "scalar",
             "pool": "gpsimd", "sp": "sync"}
DMA_ENG = {"Q_sp": "sp", "Q_act": "act", "Q_pool": "pool"}


def _handle(x):
    return x.tensor if isinstance(x, bass.AP) else x


class Prog:
    def __init__(self, nc):
        self.nc = nc
        self.plans = {k: [] for k in BLOCKNAME}
        self.sems = {}
        self.cnt = {}
        self.clock = {e: {} for e in BLOCKNAME}
        self.lastw = {}
        self.readers = {}
        self.qlane = {}
        self.cur_phase = "init"
        self.n_ops = 0
        self._keycache = {}

    def keys(self, x):
        if isinstance(x, str):
            return [x]
        h = _handle(x)
        name = h.name
        got = self._keycache.get(name)
        if got is None:
            if getattr(h, "space", None) is not None and "PSUM" in str(h.space):
                ml = self.nc.lookup_mls(h).memorylocations[0]
                nbytes = int(ml.dims[1])
                b0 = int(ml.bank)
                nb = (nbytes + 2047) // 2048
                got = [f"PSB{b}" for b in range(b0, b0 + nb)]
            else:
                got = [name]
            self._keycache[name] = got
        return got

    def sem(self, name):
        if name not in self.sems:
            self.sems[name] = self.nc.alloc_semaphore(name)
            self.cnt[name] = 0
        return self.sems[name]

    def _wait(self, eng, need):
        plan = self.plans[eng]
        clk = self.clock[eng]
        for s, v in sorted(need.items()):
            if clk.get(s, 0) >= v:
                continue
            plan.append(("w", s, v))
            clk[s] = v

    def emit(self, eng, fns, reads=(), writes=(), dma=None):
        if not isinstance(fns, (list, tuple)):
            fns = [fns]
        rk = [k for r in reads for k in self.keys(r)]
        wk = [k for w in writes for k in self.keys(w)]
        need = {}
        for k in rk:
            lw = self.lastw.get(k)
            if lw:
                need[lw[0]] = max(need.get(lw[0], 0), lw[1])
        for k in wk:
            lw = self.lastw.get(k)
            if lw:
                need[lw[0]] = max(need.get(lw[0], 0), lw[1])
            for s, v in self.readers.get(k, {}).items():
                need[s] = max(need.get(s, 0), v)
        if dma is not None:
            assert len(fns) == 1
            lane = self.qlane[dma] = (self.qlane.get(dma, -1) + 1) % 8
            sname = f"{dma}_{lane}"
            issuer = DMA_ENG[dma]
            self.sem(sname)
            if self.cnt[sname] > 0:
                need[sname] = max(need.get(sname, 0), self.cnt[sname])
        else:
            sname = "P_" + eng
            issuer = eng
            self.sem(sname)
        self._wait(issuer, need)
        plan = self.plans[issuer]
        n = len(fns)
        for i, fn in enumerate(fns):
            if dma is not None:
                self.cnt[sname] += 16
                plan.append(("op", fn, sname, 16))
            else:
                inc = 1 if i == n - 1 else 0
                if inc:
                    self.cnt[sname] += 1
                plan.append(("op", fn, sname, inc))
            self.n_ops += 1
        final = (sname, self.cnt[sname])
        for k in wk:
            self.lastw[k] = final
            self.readers[k] = {}
        for k in rk:
            self.readers.setdefault(k, {})[final[0]] = final[1]

    def claim(self, writes):
        self.emit("pool", lambda g: g.nop(), reads=(), writes=writes)

    def raw_wait(self, eng, sname, val):
        self._wait(eng, {sname: val})

    def phase(self, name):
        self._flush()
        self.cur_phase = name

    def finish(self):
        for q in ("Q_sp", "Q_act", "Q_pool"):
            for lane in range(8):
                s = f"{q}_{lane}"
                if s in self.sems and self.cnt[s] > 0:
                    self.raw_wait("pool", s, self.cnt[s])
        self._flush()

    def _flush(self):
        plans = self.plans
        if not any(plans.values()):
            return
        self.plans = {k: [] for k in BLOCKNAME}
        sems = self.sems

        def mk_replay(plan):
            def replay(eng):
                for item in plan:
                    if item[0] == "w":
                        eng.wait_ge(sems[item[1]], item[2])
                    else:
                        ins = item[1](eng)
                        if item[3]:
                            ins.then_inc(sems[item[2]], item[3])
            return replay

        with self.nc.Block() as block:
            for ename, plan in plans.items():
                if plan:
                    getattr(block, BLOCKNAME[ename])(mk_replay(plan))



def build_program(cfg, debug_stage=None, stop_after=None):
    nc = bass.Bass("TRN2", target_bir_lowering=False, num_swdge_queues=4)
    SLOTS, KC, VT, CT = cfg.SLOTS, cfg.KC, cfg.VT, cfg.CT
    XNC = SLOTS * 128

    slab_rows = cfg.SLAB * 128
    d_x = nc.dram_tensor("x_slab", [slab_rows, D], F32, kind="ExternalInput").ap()
    d_mask = nc.dram_tensor("kvmask", [slab_rows], BF16, kind="ExternalInput").ap()
    d_qw = nc.dram_tensor("q_w", [D, D], F32, kind="ExternalInput").ap()
    d_qb = nc.dram_tensor("q_b", [D], F32, kind="ExternalInput").ap()
    d_kw = nc.dram_tensor("k_w", [D, D], F32, kind="ExternalInput").ap()
    d_kb = nc.dram_tensor("k_b", [D], F32, kind="ExternalInput").ap()
    d_vw = nc.dram_tensor("v_w", [D, D], F32, kind="ExternalInput").ap()
    d_vb = nc.dram_tensor("v_b", [D], F32, kind="ExternalInput").ap()
    d_ow = nc.dram_tensor("o_w", [D, D], F32, kind="ExternalInput").ap()
    d_ob = nc.dram_tensor("o_b", [D], F32, kind="ExternalInput").ap()
    d_w1 = nc.dram_tensor("W1", [NE, D, FF], F32, kind="ExternalInput").ap()
    d_b1 = nc.dram_tensor("b1", [NE, FF], F32, kind="ExternalInput").ap()
    d_w2 = nc.dram_tensor("W2", [NE, FF, D], F32, kind="ExternalInput").ap()
    d_b2 = nc.dram_tensor("b2", [NE, D], F32, kind="ExternalInput").ap()
    d_wg = nc.dram_tensor("Wg", [D, NE], F32, kind="ExternalInput").ap()
    d_bg = nc.dram_tensor("bg", [NE], F32, kind="ExternalInput").ap()
    d_y = nc.dram_tensor("y", [cfg.OT * 128, D], F32, kind="ExternalOutput").ap()

    c_w1 = nc.dram_tensor("c_w1", [NE, D, FF], BF16, kind="Internal").ap()
    c_w2 = nc.dram_tensor("c_w2", [NE, FF, D], BF16, kind="Internal").ap()
    xb_dram = nc.dram_tensor("xb_dram", [NE * 640, D], BF16, kind="Internal").ap()
    y_raw = nc.dram_tensor("y_raw", [NE * 640, D], F32, kind="Internal").ap()
    c_vbo = nc.dram_tensor("c_vbo", [2, D], BF16, kind="Internal").ap()
    c_q = nc.dram_tensor("c_q", [8, 128, D], BF16, kind="Internal").ap()
    c_k = nc.dram_tensor("c_k", [8, 128, D], BF16, kind="Internal").ap()
    c_v = nc.dram_tensor("c_v", [8, 128, D], BF16, kind="Internal").ap()
    c_o = nc.dram_tensor("c_o", [8, 128, D], BF16, kind="Internal").ap()
    if debug_stage is not None:
        d_dbg = nc.dram_tensor("dbg", [SLOTS * 128, D], F32,
                               kind="ExternalOutput").ap()

    off = [17408]
    sb = {}
    sb_off = {}

    def alloc(name, shape, dtype, at=None):
        nbytes = int(np.prod(shape[1:])) * (2 if dtype == BF16 else 4)
        if at is None:
            o = off[0]
            off[0] += (nbytes + 31) & ~31
        else:
            o = at
        t = nc.alloc_sbuf_tensor_at(name, list(shape), dtype, offset=o)
        sb[name] = t
        sb_off[name] = o
        return t

    x_sb = alloc("x_sb", [128, SLOTS * 1024], F32)
    xnT = alloc("xnT", [128, DCN * XNC], BF16)
    off[0] = max(off[0], sb_off["xnT"] + 20480)
    w1A = alloc("w1A", [128, 16 * 1024], BF16)
    w2A = alloc("w2A", [128, 16 * 1024], BF16)
    WB_off = off[0]
    w1B = alloc("w1B", [128, 16 * 1024], BF16)
    w2B = alloc("w2B", [128, 16 * 1024], BF16)
    WB_end = off[0]
    hT = [alloc("hT0", [128, 256], BF16), alloc("hT1", [128, 256], BF16),
          alloc("hT2", [128, 256], BF16)]
    aT_sb = alloc("aT_sb", [128, 1024], BF16)
    stats_t = alloc("stats", [128, 32], F32)
    gtmp_t = alloc("gtmp", [128, 96], F32)
    s_gate = alloc("s_gate", [128, SLOTS * 8], F32)
    sT_bf = alloc("sT_bf", [8, 128], BF16)
    b1T = alloc("b1T", [128, FCN * 8], F32)
    qbT = alloc("qbT", [128, 8], F32)
    kbT = alloc("kbT", [128, 8], F32)
    qb8T = alloc("qb8T", [128, 8], F32)
    b2_bf = alloc("b2_bf", [8, D], BF16)
    vb_bf = alloc("vb_bf", [1, D], BF16)
    ob_bf = alloc("ob_bf", [1, D], BF16)
    bg_f = alloc("bg_f", [1, 8], F32)
    ones_f = alloc("ones_f", [1, 128], F32)
    ones_bf = alloc("ones_bf", [1, 128], BF16)
    identf = alloc("identf", [128, 128], F32)
    identb = alloc("identb", [128, 128], BF16)
    wg_sb = alloc("wg_sb", [128, DCN * 8], F32)
    vmask = alloc("vmask", [128, VT], F32)
    band = alloc("band", [128, 192], F32)
    band2 = alloc("band2", [128, 384], F32)
    astat = alloc("astat", [128, 8], F32)
    ltriu = alloc("ltriu", [128, 128], F32)
    ones_col = alloc("ones_col", [128, 1], F32)
    rdall = alloc("rdall", [128, SLOTS * 2], I32)
    sall = alloc("sall", [128, SLOTS * 2], F32)
    toff_row = alloc("toff_row", [1, 8], F32)
    rtmp_t = alloc("rtmp", [128, 64], F32)
    ysc = [alloc("ysc0", [128, 512], F32, at=sb_off["aT_sb"]),
           alloc("ysc1", [128, 512], F32)]
    ytmp = None
    persist_end = off[0]

    tmp4k_t = [alloc("tmp4k0", [128, 1024], F32, at=WB_off),
               alloc("tmp4k1", [128, 1024], F32, at=WB_off + 4096)]
    xnT32_t = [alloc("xnT320", [128, 1024], F32, at=WB_off + 8192),
               alloc("xnT321", [128, 1024], F32, at=WB_off + 12288)]
    ao = WB_off
    kT = alloc("kT", [128, DCN * KC], BF16, at=ao); ao += DCN * KC * 2
    v_sb = alloc("v_sb", [128, VT * 1024], BF16, at=ao); ao += VT * 1024 * 2
    wattn = alloc("wattn", [128, 8 * 1024], BF16, at=ao); ao += 8 * 1024 * 2
    maskrep = alloc("maskrep", [128, KC], F32, at=ao); ao += KC * 4
    kmask_bf = alloc("kmask_bf", [1, KC + 64], BF16, at=ao); ao += (((KC + 64) * 2 + 31) & ~31)
    Sm, Eb, ETb = [], [], []
    for r in range(2):
        Sm.append(alloc(f"Sm{r}", [128, 384], F32, at=ao)); ao += 1536
        Eb.append(alloc(f"Eb{r}", [128, 384], BF16, at=ao)); ao += 768
        ETb.append(alloc(f"ET{r}", [128, 512], BF16, at=ao)); ao += 1024

    assert ao <= WB_end, f"attn overlay too big: {ao - WB_off}"
    XN_off = sb_off["xnT"]
    xbT = [alloc("xbT0", [128, 4096], BF16, at=XN_off),
           alloc("xbT1", [128, 4096], BF16, at=XN_off + 8192)]
    xbstg = [alloc("xbstg0", [128, 1024], BF16, at=XN_off + 16384),
             alloc("xbstg1", [128, 1024], BF16, at=XN_off + 18432)]
    grb = [alloc("grb0", [128, 1024], F32, at=XN_off),
           alloc("grb1", [128, 1024], F32, at=XN_off + 4096)]
    ytmp = [alloc("ytmp0", [128, 512], F32, at=sb_off["vb_bf"]),
            alloc("ytmp1", [128, 512], F32, at=sb_off["ob_bf"])]
    XN_ALL = [xnT, xbT[0], xbT[1], xbstg[0], xbstg[1], grb[0], grb[1],
              aT_sb, ysc[0], ytmp[0], ytmp[1], vb_bf, ob_bf]

    WB2 = WB_off + 16384
    b1nat = alloc("b1nat", [8, FF], F32, at=WB2)
    rowld = alloc("rowld", [8, D], F32, at=WB2 + 16384)
    qst = [alloc("qst0", [128, D], F32, at=WB2),
           alloc("qst1", [128, D], F32, at=WB2 + 4096)]
    qso = [alloc("qso0", [128, D], BF16, at=WB2 + 8192),
           alloc("qso1", [128, D], BF16, at=WB2 + 10240)]
    tstg = alloc("tstg", [128, 8 * 1024], BF16, at=WB2 + 12288)
    cva4 = [alloc(f"cv{i}", [128, 2048], F32, at=WB2 + i * 8192)
            for i in range(4)]
    cvo4 = [alloc(f"cvo{i}", [128, 2048], BF16, at=WB2 + 32768 + i * 4096)
            for i in range(4)]
    cvA, cvB, cvOA, cvOB = cva4[0], cva4[1], cvo4[0], cvo4[1]
    assert WB2 + 49152 <= WB_end

    WB_ALL = [w1B, w2B, tmp4k_t[0], tmp4k_t[1], xnT32_t[0], xnT32_t[1],
              kT, v_sb, wattn, maskrep, kmask_bf,
              Sm[0], Sm[1], Eb[0], Eb[1], ETb[0], ETb[1],
              cva4[0], cva4[1], cva4[2], cva4[3],
              cvo4[0], cvo4[1], cvo4[2], cvo4[3], tstg, b1nat, rowld,
              qst[0], qst[1], qso[0], qso[1]]

    avail = nc.sbuf_top
    assert persist_end <= avail, f"SBUF over budget: {persist_end} > {avail}"

    pr = Prog(nc)
    W1s = [w1A, w1B]
    W2s = [w2A, w2B]


    pr.emit("pool", lambda g: g.memset(ones_f[:, :], 1.0), writes=[ones_f])
    pr.emit("pool", lambda g: g.memset(ones_bf[:, :], 1.0), writes=[ones_bf])
    for ident in (identf, identb):
        pr.emit("pool", lambda g, t=ident: g.memset(t[:, :], 0.0), writes=[ident])
        pr.emit("pool", lambda g, t=ident: g.affine_select(
            out=t[:, :], in_=t[:, :], compare_op=ALU.not_equal, fill=1.0,
            base=0, pattern=[[-1, 128]], channel_multiplier=1),
            reads=[ident], writes=[ident])
    pr.emit("pool", lambda g: g.memset(ltriu[:, :], 1.0), writes=[ltriu])
    pr.emit("pool", lambda g: g.affine_select(
        out=ltriu[:, :], in_=ltriu[:, :], compare_op=ALU.is_ge, fill=0.0,
        base=0, pattern=[[1, 128]], channel_multiplier=-1),
        reads=[ltriu], writes=[ltriu])
    pr.emit("pool", lambda g: g.memset(ones_col[:, :], 1.0), writes=[ones_col])
    pr.emit("pool", lambda g: g.memset(xbstg[0][:, :], 0.0), writes=[xbstg[0]])
    for i in range(NE * 640 // 128):
        pr.emit("act", lambda a, i=i: a.dma_start(
            out=xb_dram[i * 128:(i + 1) * 128, :], in_=xbstg[0][:, :]),
            reads=[xbstg[0]], writes=[xb_dram], dma="Q_act")
    pr.emit("pool", lambda g: g.memset(grb[0][:, :], 0.0), writes=[grb[0]])
    for i in range(NE * 640 // 128):
        pr.emit("act", lambda a, i=i: a.dma_start(
            out=y_raw[i * 128:(i + 1) * 128, :], in_=grb[0][:, :]),
            reads=[grb[0]], writes=[y_raw], dma="Q_act")
    pr.emit("pool", lambda g: g.memset(band[:, :], 0.0), writes=[band])
    pr.emit("pool", lambda g: g.affine_select(
        out=band[:, :], in_=band[:, :], compare_op=ALU.is_ge, fill=NEG,
        base=0, pattern=[[1, 192]], channel_multiplier=-1),
        reads=[band], writes=[band])
    pr.emit("pool", lambda g: g.affine_select(
        out=band[:, :], in_=band[:, :], compare_op=ALU.is_ge, fill=NEG,
        base=64, pattern=[[-1, 192]], channel_multiplier=1),
        reads=[band], writes=[band])
    pr.emit("pool", lambda g: g.tensor_copy(out=band2[:, 0:192], in_=band[:, :]),
            reads=[band], writes=[band2])
    pr.emit("pool", lambda g: g.tensor_copy(out=band2[:, 192:384], in_=band[:, :]),
            reads=[band], writes=[band2])

    cva = [cvA, cvB]
    cvo = [cvOA, cvOB]
    cast_engs = ["pool", "dve", "act"]

    def load_x_slots(c, slots):
        for slot in slots:
            row0 = c * CT * 128 + slot * 128
            src = d_x[row0:row0 + 128, :]
            dst = x_sb[:, slot * 1024:(slot + 1) * 1024]
            pr.emit("act", lambda a, src=src, dst=dst: a.dma_start(
                out=dst, in_=src), reads=[d_x], writes=[x_sb], dma="Q_act")

    load_x_slots(0, list(range(SLOTS)))

    def copy_fn(eng, dst_ap, src_ap):
        if eng == "act":
            return lambda E, d=dst_ap, s=src_ap: E.copy(out=d, in_=s)
        return lambda E, d=dst_ap, s=src_ap: E.tensor_copy(out=d, in_=s)

    ci = 0
    with nc.psum_tensor("bp0", [128, 8], F32) as bp0, \
         nc.psum_tensor("bp1", [128, 8], F32) as bp1:
        bp = [bp0, bp1]
        pr.emit("sp", lambda _: _.dma_start(out=b1nat[:, :], in_=d_b1),
                reads=[d_b1], writes=[b1nat], dma="Q_sp")
        for fc in range(FCN):
            t = bp[fc % 2]
            pr.emit("pe", lambda T, t=t, fc=fc: T.transpose(
                t[:, :], b1nat[0:8, fc * 128:(fc + 1) * 128], identf[0:8, 0:8]),
                reads=[b1nat, identf], writes=[t])
            pr.emit("dve", lambda v, t=t, fc=fc: v.tensor_copy(
                out=b1T[:, fc * 8:(fc + 1) * 8], in_=t[:, :]),
                reads=[t], writes=[b1T])
        for d_b, tgt in ((d_qb, qbT), (d_kb, kbT)):
            pr.emit("sp", lambda _, d_b=d_b: _.dma_start(
                out=rowld[0:1, :], in_=d_b[None, :]),
                reads=[d_b], writes=[rowld], dma="Q_sp")
            for dc in range(DCN):
                t = bp[dc % 2]
                pr.emit("pe", lambda T, t=t, dc=dc: T.transpose(
                    t[:, 0:1], rowld[0:1, dc * 128:(dc + 1) * 128],
                    identf[0:1, 0:1]), reads=[rowld, identf], writes=[t])
                pr.emit("dve", lambda v, t=t, tgt=tgt, dc=dc: v.tensor_copy(
                    out=tgt[:, dc:dc + 1], in_=t[:, 0:1]),
                    reads=[t], writes=[tgt])
        pr.emit("dve", lambda v: v.tensor_scalar(
            out=qb8T[:, :], in0=qbT[:, :], scalar1=0.125, scalar2=None,
            op0=ALU.mult), reads=[qbT], writes=[qb8T])
        pr.emit("sp", lambda _: _.dma_start(out=rowld[:, :], in_=d_b2),
                reads=[d_b2], writes=[rowld], dma="Q_sp")
        pr.emit("dve", lambda v: v.tensor_copy(out=b2_bf[:, :], in_=rowld[:, :]),
                reads=[rowld], writes=[b2_bf])
        pr.emit("sp", lambda _: _.dma_start(out=rowld[0:1, :], in_=d_vb[None, :]),
                reads=[d_vb], writes=[rowld], dma="Q_sp")
        pr.emit("dve", lambda v: v.tensor_copy(out=vb_bf[:, :], in_=rowld[0:1, :]),
                reads=[rowld], writes=[vb_bf])
        pr.emit("sp", lambda _: _.dma_start(out=rowld[0:1, :], in_=d_ob[None, :]),
                reads=[d_ob], writes=[rowld], dma="Q_sp")
        pr.emit("dve", lambda v: v.tensor_copy(out=ob_bf[:, :], in_=rowld[0:1, :]),
                reads=[rowld], writes=[ob_bf])
        pr.emit("act", lambda a: a.dma_start(out=c_vbo[0:1, :], in_=vb_bf[:, :]),
                reads=[vb_bf], writes=[c_vbo], dma="Q_act")
        pr.emit("act", lambda a: a.dma_start(out=c_vbo[1:2, :], in_=ob_bf[:, :]),
                reads=[ob_bf], writes=[c_vbo], dma="Q_act")
        pr.emit("sp", lambda _: _.dma_start(out=bg_f[:, :], in_=d_bg[None, :]),
                reads=[d_bg], writes=[bg_f], dma="Q_sp")
        for dc in range(DCN):
            src = d_wg[dc * 128:(dc + 1) * 128, :]
            pr.emit("sp", lambda _, src=src, dc=dc: _.dma_start(
                out=wg_sb[:, dc * 8:(dc + 1) * 8], in_=src),
                reads=[d_wg], writes=[wg_sb], dma="Q_sp")

    pr.claim([b1nat, rowld, qst[0], qst[1], qso[0], qso[1], tstg])
    with nc.psum_tensor("trp0", [128, 128], BF16) as trp0, \
         nc.psum_tensor("trp1", [128, 128], BF16) as trp1:
        trp = [trp0, trp1]
        for d_w, c_t, flip in ((d_qw, c_q, False), (d_kw, c_k, False),
                               (d_vw, c_v, True), (d_ow, c_o, True)):
            for ic in range(8):
                s, o = qst[ci % 2], qso[ci % 2]
                src = d_w[ic * 128:(ic + 1) * 128, :]
                pr.emit("act", lambda a, s=s, src=src: a.dma_start(
                    out=s[:, 0:D], in_=src), reads=[d_w], writes=[s], dma="Q_act")
                pr.emit("pool", lambda g, s=s, o=o: g.tensor_copy(
                    out=o[:, 0:D], in_=s[:, 0:D]), reads=[s], writes=[o])
                for dc in range(DCN):
                    t = trp[dc % 2]
                    col = (dc * 1024 + ic * 128) if flip else (ic * 1024 + dc * 128)
                    pr.emit("pe", lambda T, o=o, t=t, dc=dc: T.transpose(
                        t[:, :], o[:, dc * 128:(dc + 1) * 128], identb[:, :]),
                        reads=[o, identb], writes=[t])
                    pr.emit("dve", lambda v, t=t, col=col: v.tensor_copy(
                        out=tstg[:, col:col + 128], in_=t[:, :]),
                        reads=[t], writes=[tstg])
                ci += 1
            for blk in range(8):
                dst = c_t[blk]
                pr.emit("act", lambda a, dst=dst, blk=blk: a.dma_start(
                    out=dst, in_=tstg[:, blk * 1024:(blk + 1) * 1024]),
                    reads=[tstg], writes=[c_t], dma="Q_act")

    pr.claim([qst[0], qst[1], qso[0], qso[1], tstg,
              cva4[0], cva4[1], cva4[2], cva4[3],
              cvo4[0], cvo4[1], cvo4[2], cvo4[3]])
    ci = 0
    for e in range(NE):
        for dc in range(DCN):
            for hh in range(2):
                s, o = cva4[ci % 4], cvo4[ci % 4]
                src = d_w1[e, dc * 128:(dc + 1) * 128,
                           hh * 2048:(hh + 1) * 2048]
                dst = c_w1[e, dc * 128:(dc + 1) * 128,
                           hh * 2048:(hh + 1) * 2048]
                pr.emit("sp", lambda _, s=s, src=src: _.dma_start(
                    out=s[:, :], in_=src), reads=[d_w1], writes=[s],
                    dma="Q_sp")
                pr.emit(cast_engs[ci % 3], copy_fn(cast_engs[ci % 3],
                                                   o[:, :], s[:, :]),
                        reads=[s], writes=[o])
                pr.emit("act", lambda a, o=o, dst=dst: a.dma_start(
                    out=dst, in_=o[:, :]), reads=[o], writes=[f"cw1_{e}"],
                    dma="Q_act")
                ci += 1
        for fq in range(FF // 256):
            s, o = cva4[ci % 4], cvo4[ci % 4]
            src = d_w2[e, fq * 256:(fq + 1) * 256, :].rearrange(
                "(a p) d -> p a d", p=128)
            dst = c_w2[e, fq * 256:(fq + 1) * 256, :].rearrange(
                "(a p) d -> p a d", p=128)
            pr.emit("sp", lambda _, s=s, src=src: _.dma_start(
                out=s[:, 0:2048].rearrange("p (a d) -> p a d", a=2), in_=src),
                reads=[d_w2], writes=[s], dma="Q_sp")
            pr.emit(cast_engs[ci % 3], copy_fn(cast_engs[ci % 3],
                                               o[:, 0:2048], s[:, 0:2048]),
                    reads=[s], writes=[o])
            pr.emit("act", lambda a, o=o, dst=dst: a.dma_start(
                out=dst, in_=o[:, 0:2048].rearrange("p (a d) -> p a d", a=2)),
                reads=[o], writes=[f"cw2_{e}"], dma="Q_act")
            ci += 1


    def w_prefetch(e, half, slot):
        w1t, w2t = W1s[slot], W2s[slot]
        src1 = c_w1[e, :, half * 2048:(half + 1) * 2048].rearrange(
            "(a p) f -> p a f", p=128)
        src2 = c_w2[e, half * 2048:(half + 1) * 2048, :].rearrange(
            "(a p) d -> p a d", p=128)
        pr.emit("act", lambda a, w1t=w1t, src1=src1: a.dma_start(
            out=w1t[:, :].rearrange("p (a f) -> p a f", a=8), in_=src1),
            reads=[f"cw1_{e}"], writes=[w1t], dma="Q_act")
        pr.emit("act", lambda a, w2t=w2t, src2=src2: a.dma_start(
            out=w2t[:, :].rearrange("p (a d) -> p a d", a=16), in_=src2),
            reads=[f"cw2_{e}"], writes=[w2t], dma="Q_act")

    def ln_tile(slot, do_gate, tag, par=0):
        veng = "dve" if par % 2 == 0 else "pool"
        stats = stats_t[:, par * 16:par * 16 + 16]
        statk = f"stats{par}"
        tmp4k = tmp4k_t[par]
        xnT32 = xnT32_t[par]
        x_t = x_sb[:, slot * 1024:(slot + 1) * 1024]
        xncol = slot * 128
        pr.emit("dve", lambda v, x_t=x_t: v.reduce_sum(
            out=stats[:, 0:1], in_=x_t, axis=AX.X), reads=[x_sb], writes=[statk])
        pr.emit("act", lambda a, x_t=x_t: a.activation(
            out=tmp4k[:, :], in_=x_t, func=AF.Square), reads=[x_sb],
            writes=[tmp4k])
        pr.emit("dve", lambda v: v.reduce_sum(
            out=stats[:, 1:2], in_=tmp4k[:, :], axis=AX.X),
            reads=[tmp4k], writes=[statk])
        pr.emit(veng, lambda v: v.tensor_scalar(
            out=stats[:, 2:3], in0=stats[:, 0:1], scalar1=1.0 / D, scalar2=None,
            op0=ALU.mult), reads=[statk], writes=[statk])
        pr.emit(veng, lambda v: v.tensor_tensor(
            out=stats[:, 3:4], in0=stats[:, 2:3], in1=stats[:, 2:3],
            op=ALU.mult), reads=[statk], writes=[statk])
        pr.emit("dve", lambda v: v.tensor_scalar(
            out=stats[:, 4:5], in0=stats[:, 1:2], scalar1=1.0 / D,
            scalar2=stats[:, 3:4], op0=ALU.mult, op1=ALU.subtract),
            reads=[statk], writes=[statk])
        pr.emit(veng, lambda v: v.tensor_scalar(
            out=stats[:, 4:5], in0=stats[:, 4:5], scalar1=1e-5, scalar2=None,
            op0=ALU.add), reads=[statk], writes=[statk])
        pr.emit("act", lambda a: a.activation(
            out=stats[:, 5:6], in_=stats[:, 4:5], func=AF.Sqrt),
            reads=[statk], writes=[statk])
        pr.emit("dve", lambda v: v.reciprocal(
            out=stats[:, 6:7], in_=stats[:, 5:6]), reads=[statk], writes=[statk])
        pr.emit("dve", lambda v, x_t=x_t: v.tensor_scalar(
            out=tmp4k[:, :], in0=x_t, scalar1=stats[:, 2:3],
            scalar2=stats[:, 6:7], op0=ALU.subtract, op1=ALU.mult),
            reads=[x_sb, statk], writes=[tmp4k])
        with nc.psum_tensor("trf0" + tag, [128, 128], F32) as t0, \
             nc.psum_tensor("trf1" + tag, [128, 128], F32) as t1:
            trp = [t0, t1]
            for dc in range(DCN):
                if not do_gate:
                    t = trp[dc % 2]
                    pr.emit("pe", lambda T, t=t, dc=dc: T.transpose(
                        t[:, :], tmp4k[:, dc * 128:(dc + 1) * 128],
                        identf[:, :]), reads=[tmp4k, identf], writes=[t])
                    pr.emit("dve", lambda v, t=t, dc=dc, xncol=xncol:
                            v.tensor_copy(
                        out=xnT[:, dc * XNC + xncol:dc * XNC + xncol + 128],
                        in_=t[:, :]), reads=[t], writes=[xnT])
                else:
                    t = trp[dc % 2]
                    pr.emit("pe", lambda T, t=t, dc=dc: T.transpose(
                        t[:, :], tmp4k[:, dc * 128:(dc + 1) * 128],
                        identf[:, :]), reads=[tmp4k, identf], writes=[t])
                    pr.emit("dve", lambda v, t=t, dc=dc: v.tensor_copy(
                        out=xnT32[:, dc * 128:(dc + 1) * 128], in_=t[:, :]),
                        reads=[t], writes=[xnT32])
        if do_gate:
            pr.emit(veng, lambda v: v.tensor_copy(
                out=xbstg[par][:, :], in_=tmp4k[:, :]),
                reads=[tmp4k], writes=[xbstg[par]])
            gate_tile(slot, tag, par, stats, xnT32)
            route_tile(slot, tag, par)

    def gate_tile(slot, tag, par, stats, xnT32):
        veng = "dve" if par % 2 == 0 else "pool"
        gtmp = gtmp_t[:, par * 48:par * 48 + 48]
        statk = f"stats{par}"
        gtk = f"gtmp{par}"
        with nc.psum_tensor("gp" + tag, [128, 8], F32) as gp, \
             nc.psum_tensor("sTp" + tag, [8, 128], F32) as sTp, \
             nc.psum_tensor("b2p" + tag, [128, 1024], F32) as b2p:
            fns = []
            for dc in range(DCN):
                fns.append(lambda T, dc=dc: T.matmul(
                    gp[:, :], xnT32[:, dc * 128:(dc + 1) * 128],
                    wg_sb[:, dc * 8:(dc + 1) * 8], start=(dc == 0), stop=False))
            fns.append(lambda T: T.matmul(
                gp[:, :], ones_f[:, :], bg_f[:, :], start=False, stop=True))
            pr.emit("pe", fns, reads=[xnT32, wg_sb, ones_f, bg_f], writes=[gp])
            L = gtmp[:, 0:8]
            pr.emit("dve", lambda v: v.tensor_copy(out=L, in_=gp[:, :]),
                    reads=[gp], writes=[gtk])
            c_m1 = stats[:, 8:9]
            c_cnt = stats[:, 9:10]
            c_t2 = stats[:, 10:11]
            c_t2e = stats[:, 11:12]
            c_nm1 = stats[:, 12:13]
            c_den = stats[:, 13:14]
            c_rd2 = stats[:, 14:15]
            c_c2 = stats[:, 15:16]
            t8a = gtmp[:, 8:16]
            t8b = gtmp[:, 16:24]
            t8c = gtmp[:, 24:32]
            t8d = gtmp[:, 32:40]
            pr.emit("dve", lambda v: v.reduce_max(out=c_m1, in_=L, axis=AX.X),
                    reads=[gtk], writes=[statk])
            pr.emit("dve", lambda v: v.tensor_scalar(
                out=t8a, in0=L, scalar1=c_m1, scalar2=None, op0=ALU.is_ge),
                reads=[gtk, statk], writes=[gtk])
            pr.emit("dve", lambda v: v.reduce_sum(out=c_cnt, in_=t8a, axis=AX.X),
                    reads=[gtk], writes=[statk])
            pr.emit("dve", lambda v: v.scalar_tensor_tensor(
                out=t8b, in0=t8a, scalar=NEG, in1=L, op0=ALU.mult, op1=ALU.add),
                reads=[gtk], writes=[gtk])
            pr.emit("dve", lambda v: v.reduce_max(out=c_t2, in_=t8b, axis=AX.X),
                    reads=[gtk], writes=[statk])
            pr.emit(veng, lambda v: v.tensor_scalar(
                out=c_c2, in0=c_cnt, scalar1=2.0, scalar2=None, op0=ALU.is_ge),
                reads=[statk], writes=[statk])
            pr.emit(veng, lambda v: v.tensor_tensor(
                out=c_t2e, in0=c_m1, in1=c_t2, op=ALU.subtract),
                reads=[statk], writes=[statk])
            pr.emit("dve", lambda v: v.scalar_tensor_tensor(
                out=c_t2e, in0=c_t2e, scalar=c_c2, in1=c_t2,
                op0=ALU.mult, op1=ALU.add), reads=[statk], writes=[statk])
            pr.emit("dve", lambda v: v.tensor_scalar(
                out=t8c, in0=L, scalar1=c_t2e, scalar2=None, op0=ALU.is_ge),
                reads=[gtk, statk], writes=[gtk])
            pr.emit(veng, lambda v: v.tensor_scalar(
                out=c_nm1, in0=c_m1, scalar1=-1.0, scalar2=None, op0=ALU.mult),
                reads=[statk], writes=[statk])
            pr.emit("act", lambda a: a.activation(
                out=t8d, in_=L, func=AF.Exp, bias=c_nm1),
                reads=[gtk, statk], writes=[gtk])
            pr.emit(veng, lambda v: v.tensor_tensor(
                out=t8d, in0=t8d, in1=t8c, op=ALU.mult),
                reads=[gtk], writes=[gtk])
            pr.emit("dve", lambda v: v.reduce_sum(out=c_den, in_=t8d, axis=AX.X),
                    reads=[gtk], writes=[statk])
            pr.emit("dve", lambda v: v.reciprocal(out=c_rd2, in_=c_den),
                    reads=[statk], writes=[statk])
            pr.emit(veng, lambda v: v.tensor_scalar(
                out=c_rd2, in0=c_rd2, scalar1=0.5, scalar2=None, op0=ALU.mult),
                reads=[statk], writes=[statk])
            sg = s_gate[:, slot * 8:(slot + 1) * 8]
            pr.emit("dve", lambda v, sg=sg: v.tensor_scalar(
                out=sg, in0=t8d, scalar1=c_rd2, scalar2=None, op0=ALU.mult),
                reads=[gtk, statk], writes=[s_gate])
            pr.emit("pe", lambda T, sg=sg: T.transpose(
                sTp[:, :], sg, identf[:, :]), reads=[s_gate, identf],
                writes=[sTp])
            pr.emit("dve", lambda v: v.tensor_copy(out=sT_bf[:, :],
                                                   in_=sTp[:, :]),
                    reads=[sTp], writes=[sT_bf])
            pr.emit("pe", [
                lambda T: T.matmul(b2p[:, 0:512], sT_bf[:, :], b2_bf[:, 0:512],
                                   start=True, stop=True),
                lambda T: T.matmul(b2p[:, 512:1024], sT_bf[:, :],
                                   b2_bf[:, 512:1024], start=True, stop=True)],
                reads=[sT_bf, b2_bf], writes=[b2p])
            x_t = x_sb[:, slot * 1024:(slot + 1) * 1024]
            pr.emit("dve", lambda v, x_t=x_t: v.tensor_tensor(
                out=x_t, in0=x_t, in1=b2p[:, :], op=ALU.add),
                reads=[x_sb, b2p], writes=[x_sb])

    def init_toff():
        pr.emit("pool", lambda g: g.iota(
            rtmp_t[:, 0:8], [[640, 8]], base=int(BIG), channel_multiplier=0,
            allow_small_or_imprecise_dtypes=True), writes=["rtmp0"])
        pr.emit("dve", lambda v: v.tensor_copy(
            out=toff_row[:, :], in_=rtmp_t[0:1, 0:8]),
            reads=["rtmp0"], writes=[toff_row])

    def route_tile(slot, tag, par):
        veng = "dve" if par % 2 == 0 else "pool"
        rt = rtmp_t[:, par * 32:par * 32 + 32]
        rk = f"rtmp{par}"
        sg = s_gate[:, slot * 8:(slot + 1) * 8]
        mk = rt[:, 0:8]
        rdest = rt[:, 8:16]
        dst2 = rt[:, 16:24]
        m0 = rt[:, 28:29]
        m1 = rt[:, 29:30]
        with nc.psum_tensor("csum" + tag, [128, 8], F32) as csum, \
             nc.psum_tensor("ttot" + tag, [1, 8], F32) as ttot:
            pr.emit(veng, lambda v: v.tensor_scalar(
                out=mk, in0=sg, scalar1=0.0, scalar2=None, op0=ALU.is_gt),
                reads=[s_gate], writes=[rk])
            pr.emit("pe", [
                lambda T: T.matmul(csum[:, :], ltriu[:, :], mk,
                                   start=True, stop=False),
                lambda T: T.matmul(csum[:, :], ones_f[:, :], toff_row[:, :],
                                   start=False, stop=True)],
                reads=[ltriu, rk, ones_f, toff_row], writes=[csum])
            pr.emit("pe", lambda T: T.matmul(
                ttot[:, :], ones_col[:, :], mk, start=True, stop=True),
                reads=[ones_col, rk], writes=[ttot])
            pr.emit("dve", lambda v: v.tensor_tensor(
                out=toff_row[:, :], in0=toff_row[:, :], in1=ttot[:, :],
                op=ALU.add), reads=[toff_row, ttot], writes=[toff_row])
            pr.emit("dve", lambda v: v.scalar_tensor_tensor(
                out=rdest, in0=mk, scalar=-(BIG + 1.0), in1=csum[:, :],
                op0=ALU.mult, op1=ALU.add), reads=[rk, csum], writes=[rk])
            pr.emit("dve", lambda v: v.tensor_reduce(
                out=m0, in_=rdest, axis=AX.X, op=ALU.min),
                reads=[rk], writes=[rk])
            pr.emit("dve", lambda v: v.tensor_scalar(
                out=dst2, in0=rdest, scalar1=m0, scalar2=None, op0=ALU.is_le),
                reads=[rk], writes=[rk])
            pr.emit(veng, lambda v: v.tensor_tensor(
                out=rt[:, 24:32] if False else dst2, in0=dst2, in1=sg,
                op=ALU.mult), reads=[rk, s_gate], writes=[rk])
            pr.emit("dve", lambda v: v.reduce_sum(
                out=sall[:, slot * 2:slot * 2 + 1], in_=dst2, axis=AX.X),
                reads=[rk], writes=[sall])
            pr.emit(veng, lambda v: v.tensor_copy(
                out=rdall[:, slot * 2:slot * 2 + 1], in_=m0),
                reads=[rk], writes=[rdall])
            pr.emit("dve", lambda v: v.tensor_scalar(
                out=dst2, in0=rdest, scalar1=m0, scalar2=None, op0=ALU.is_le),
                reads=[rk], writes=[rk])
            pr.emit("dve", lambda v: v.scalar_tensor_tensor(
                out=dst2, in0=dst2, scalar=BIG2, in1=rdest,
                op0=ALU.mult, op1=ALU.add), reads=[rk], writes=[rk])
            pr.emit("dve", lambda v: v.tensor_reduce(
                out=m1, in_=dst2, axis=AX.X, op=ALU.min),
                reads=[rk], writes=[rk])
            pr.emit("dve", lambda v: v.tensor_scalar(
                out=dst2, in0=dst2, scalar1=m1, scalar2=None, op0=ALU.is_le),
                reads=[rk], writes=[rk])
            pr.emit(veng, lambda v: v.tensor_tensor(
                out=dst2, in0=dst2, in1=sg, op=ALU.mult),
                reads=[rk, s_gate], writes=[rk])
            pr.emit("dve", lambda v: v.reduce_sum(
                out=sall[:, slot * 2 + 1:slot * 2 + 2], in_=dst2, axis=AX.X),
                reads=[rk], writes=[sall])
            pr.emit(veng, lambda v: v.tensor_copy(
                out=rdall[:, slot * 2 + 1:slot * 2 + 2], in_=m1),
                reads=[rk], writes=[rdall])
        for r in range(2):
            roff = rdall[:, slot * 2 + r:slot * 2 + r + 1]
            pr.emit("pool", lambda g, roff=roff: g.indirect_dma_start(
                xb_dram[:, :], IndirectOffsetOnAxis(roff, 0),
                xbstg[par][:, :], None),
                reads=[xbstg[par], rdall], writes=[xb_dram], dma="Q_pool")

    def ffn_pass(slots, next_first, tag, slot_sched=None):
        steps = [(e, h) for e in range(NE) for h in range(2)]
        if slot_sched is None:
            slot_sched = [i % 2 for i in range(len(steps))]

        def gather_load(e, m):
            stg = xbstg[m % 2]
            srcr = xb_dram[e * 640 + m * 128:e * 640 + m * 128 + 128, :]
            pr.emit("sp", lambda _, stg=stg, srcr=srcr: _.dma_start(
                out=stg[:, :], in_=srcr),
                reads=[xb_dram], writes=[stg], dma="Q_sp")

        def gather_tr(e, m, tg):
            par = e % 2
            stg = xbstg[m % 2]
            for dc in range(DCN):
                t = tg[dc % 2]
                pr.emit("pe", lambda T, t=t, stg=stg, dc=dc: T.transpose(
                    t[:, :], stg[:, dc * 128:(dc + 1) * 128],
                    identb[:, :]), reads=[stg, identb], writes=[t])
                dst = xbT[par][:, dc * 512 + m * 128:dc * 512 + m * 128 + 128]
                pr.emit("dve", lambda v, t=t, dst=dst: v.tensor_copy(
                    out=dst, in_=t[:, :]), reads=[t], writes=[xbT[par]])

        zrot = [0]
        LAG = 2
        with nc.psum_tensor("zp0" + tag, [128, 512], F32) as z0, \
             nc.psum_tensor("zp1" + tag, [128, 512], F32) as z1, \
             nc.psum_tensor("yp0" + tag, [128, 1024], F32) as y0, \
             nc.psum_tensor("yp1" + tag, [128, 1024], F32) as y1, \
             nc.psum_tensor("tg0" + tag, [128, 128], BF16) as tg0, \
             nc.psum_tensor("tg1" + tag, [128, 128], BF16) as tg1:
            tg = [tg0, tg1]
            for m in range(4):
                gather_load(0, m)
                gather_tr(0, m, tg)
            zp = [z0, z1]
            yp = [y0, y1]
            for step, (e, half) in enumerate(steps):
                pf_late = None
                nf_late = False
                if step + 1 < len(steps):
                    e2, h2 = steps[step + 1]
                    if slot_sched[step + 1] != slot_sched[step]:
                        w_prefetch(e2, h2, slot_sched[step + 1])
                    else:
                        pf_late = (e2, h2, slot_sched[step + 1])
                elif next_first is not None:
                    if slot_sched[step] != 0:
                        next_first()
                    else:
                        nf_late = True
                hooks = {}
                if e + 1 < NE:
                    if half == 0:
                        hooks[0] = [("L", 0), ("L", 1)]
                        hooks[11] = [("T", 0), ("L", 2)]
                        hooks[22] = [("T", 1), ("L", 3)]
                    else:
                        hooks[0] = [("T", 2)]
                        hooks[11] = [("T", 3)]
                ws = slot_sched[step]
                w1t, w2t = W1s[ws], W2s[ws]
                xbe = xbT[e % 2]
                groups = [(0, 1), (2, 3)]
                seq = [(g, fc) for g in groups for fc in range(16)]

                def emit_z(i):
                    g, fc = seq[i]
                    z = zp[i % 2]
                    h = hT[i % 3]
                    fns = []
                    for dc in range(DCN):
                        fns.append(lambda T, z=z, dc=dc, fc=fc, g=g,
                                   w1t=w1t, xbe=xbe: T.matmul(
                            z[:, 0:256],
                            w1t[:, dc * 2048 + fc * 128:
                                dc * 2048 + fc * 128 + 128],
                            xbe[:, dc * 512 + g[0] * 128:
                                dc * 512 + g[0] * 128 + 256],
                            start=(dc == 0), stop=(dc == DCN - 1)))
                    pr.emit("pe", fns, reads=[w1t, xbe], writes=[z])
                    fcg = half * 16 + fc
                    bcol = b1T[:, fcg * 8 + e:fcg * 8 + e + 1]
                    pr.emit("act", lambda a, z=z, h=h, bcol=bcol:
                            a.activation(out=h[:, 0:256], in_=z[:, 0:256],
                                         func=AF.Gelu, bias=bcol),
                            reads=[z, b1T], writes=[h])

                def emit_y(i):
                    g, fc = seq[i]
                    h = hT[i % 3]
                    for ti in range(2):
                        yb = yp[ti]
                        yfns = []
                        for hf in range(2):
                            yfns.append(lambda T, yb=yb, h=h, ti=ti, fc=fc,
                                        hf=hf, w2t=w2t: T.matmul(
                                yb[:, hf * 512:(hf + 1) * 512],
                                h[:, ti * 128:(ti + 1) * 128],
                                w2t[:, fc * 1024 + hf * 512:
                                    fc * 1024 + hf * 512 + 512],
                                start=(fc == 0), stop=(fc == 15)))
                        pr.emit("pe", yfns, reads=[h, w2t], writes=[yb])
                    if fc == 15:
                        for ti in range(2):
                            yb = yp[ti]
                            m = g[ti]
                            for hf in range(2):
                                dst = y_raw[e * 640 + m * 128:
                                            e * 640 + m * 128 + 128,
                                            hf * 512:(hf + 1) * 512]
                                pr.emit("dve", lambda v, yb=yb, hf=hf:
                                        v.tensor_copy(
                                    out=ysc[hf][:, :],
                                    in_=yb[:, hf * 512:(hf + 1) * 512]),
                                    reads=[yb], writes=[ysc[hf]])
                                if half == 0:
                                    pr.emit("act", lambda a, dst=dst, hf=hf:
                                            a.dma_start(out=dst,
                                                        in_=ysc[hf][:, :]),
                                        reads=[ysc[hf]], writes=[y_raw],
                                        dma="Q_act")
                                else:
                                    pr.emit("pool", lambda g, dst=dst, hf=hf:
                                            g.dma_start(out=dst,
                                                        in_=ysc[hf][:, :],
                                                        accum_op=ALU.add),
                                        reads=[ysc[hf]], writes=[y_raw],
                                        dma="Q_pool")

                n = len(seq)
                for i in range(n + LAG):
                    for kind, m in hooks.get(i, ()):
                        if kind == "L":
                            gather_load(e + 1, m)
                        else:
                            gather_tr(e + 1, m, tg)
                    if i < n:
                        emit_z(i)
                    if i >= LAG:
                        emit_y(i - LAG)
                if pf_late is not None:
                    w_prefetch(*pf_late)
                if nf_late:
                    next_first()
        pr.claim([xbT[0], grb[0], grb[1]])
        gi = 0
        for slot in slots:
            x_t = x_sb[:, slot * 1024:(slot + 1) * 1024]
            for r in range(2):
                gb = grb[gi % 2]
                gi += 1
                roff = rdall[:, slot * 2 + r:slot * 2 + r + 1]
                pr.emit("pool", lambda g, gb=gb, roff=roff:
                        g.indirect_dma_start(
                    gb[:, :], None, y_raw[:, :],
                    IndirectOffsetOnAxis(roff, 0)),
                    reads=[y_raw, rdall], writes=[gb], dma="Q_pool")
                scol = sall[:, slot * 2 + r:slot * 2 + r + 1]
                pr.emit("dve", lambda v, gb=gb, x_t=x_t, scol=scol:
                        v.scalar_tensor_tensor(
                    out=x_t, in0=gb[:, :], scalar=scol, in1=x_t,
                    op0=ALU.mult, op1=ALU.add),
                    reads=[gb, sall, x_sb], writes=[x_sb])

    def chunk_masks(c):
        start = c * CT * 128 + 96
        src2 = d_mask[start:start + KC + 64]
        pr.emit("act", lambda a, src2=src2: a.dma_start(
            out=kmask_bf[:, :], in_=src2[None, :]),
            reads=[d_mask], writes=[kmask_bf], dma="Q_act")
        with nc.psum_tensor(f"vmp{c}", [128, 1], BF16) as vmp:
            for m in range(VT):
                pr.emit("pe", lambda T, m=m: T.transpose(
                    vmp[:, :], kmask_bf[0:1, m * 128:(m + 1) * 128],
                    identb[0:1, 0:1]), reads=[kmask_bf, identb], writes=[vmp])
                pr.emit("dve", lambda v, m=m: v.tensor_copy(
                    out=vmask[:, m:m + 1], in_=vmp[:, :]),
                    reads=[vmp], writes=[vmask])
        with nc.psum_tensor(f"mrp{c}", [128, 512], F32) as mrp:
            nsl = (KC + 511) // 512
            for s in range(nsl):
                wdt = min(512, KC - s * 512)
                pr.emit("pe", lambda T, s=s, wdt=wdt: T.matmul(
                    mrp[:, 0:wdt], ones_bf[:, :],
                    kmask_bf[:, s * 512:s * 512 + wdt], start=True, stop=True),
                    reads=[ones_bf, kmask_bf], writes=[mrp])
                pr.emit("dve", lambda v, s=s, wdt=wdt: v.tensor_copy(
                    out=maskrep[:, s * 512:s * 512 + wdt], in_=mrp[:, 0:wdt]),
                    reads=[mrp], writes=[maskrep])

    def load_wattn(c_t):
        src = c_t[:, :, :].rearrange("a p d -> p a d")
        pr.emit("sp", lambda _, src=src: _.dma_start(
            out=wattn[:, :].rearrange("p (a d) -> p a d", a=8), in_=src),
            reads=[c_t], writes=[wattn], dma="Q_sp")

    def qkv_phase(c):
        for slot in range(SLOTS):
            ln_tile(slot, do_gate=False, tag=f"n2_{c}_{slot}", par=slot % 2)
        pr.claim([tmp4k_t[0], tmp4k_t[1], xnT32_t[0], xnT32_t[1], kT])
        load_wattn(c_k)
        slices = []
        pos = 0
        while pos < KC:
            wdt = min(512, KC - pos)
            slices.append((pos, wdt))
            pos += wdt
        with nc.psum_tensor(f"kqp0_{c}", [128, 512], F32) as kp0, \
             nc.psum_tensor(f"kqp1_{c}", [128, 512], F32) as kp1:
            kp = [kp0, kp1]
            r = 0
            for ic in range(8):
                for (pos, wdt) in slices:
                    p = kp[r % 2]
                    r += 1
                    xc = 96 + pos
                    fns = []
                    for dc in range(DCN):
                        fns.append(lambda T, p=p, dc=dc, ic=ic, xc=xc, wdt=wdt:
                                   T.matmul(
                            p[:, 0:wdt],
                            wattn[:, ic * 1024 + dc * 128:
                                  ic * 1024 + dc * 128 + 128],
                            xnT[:, dc * XNC + xc:dc * XNC + xc + wdt],
                            start=(dc == 0), stop=(dc == DCN - 1)))
                    pr.emit("pe", fns, reads=[wattn, xnT], writes=[p])
                    kb = kbT[:, ic:ic + 1]
                    dst = kT[:, ic * KC + pos:ic * KC + pos + wdt]
                    mr = maskrep[:, pos:pos + wdt]
                    pr.emit("dve", lambda v, p=p, dst=dst, kb=kb, mr=mr,
                            wdt=wdt: v.scalar_tensor_tensor(
                        out=dst, in0=p[:, 0:wdt], scalar=kb, in1=mr,
                        op0=ALU.add, op1=ALU.mult),
                        reads=[p, kbT, maskrep], writes=[kT])
        load_wattn(c_v)
        with nc.psum_tensor(f"vp0_{c}", [128, 1024], F32) as vp0, \
             nc.psum_tensor(f"vp1_{c}", [128, 1024], F32) as vp1:
            vp = [vp0, vp1]
            for m in range(VT):
                p = vp[m % 2]
                xc = 96 + m * 128
                fns = []
                for dc in range(DCN):
                    for hf in range(2):
                        fns.append(lambda T, p=p, dc=dc, hf=hf, xc=xc: T.matmul(
                            p[:, hf * 512:(hf + 1) * 512],
                            xnT[:, dc * XNC + xc:dc * XNC + xc + 128],
                            wattn[:, dc * 1024 + hf * 512:
                                  dc * 1024 + hf * 512 + 512],
                            start=(dc == 0), stop=False))
                for hf in range(2):
                    fns.append(lambda T, p=p, hf=hf: T.matmul(
                        p[:, hf * 512:(hf + 1) * 512], ones_bf[:, :],
                        vb_bf[:, hf * 512:(hf + 1) * 512],
                        start=False, stop=True))
                pr.emit("pe", fns, reads=[wattn, xnT, ones_bf, vb_bf],
                        writes=[p])
                dst = v_sb[:, m * 1024:(m + 1) * 1024]
                vm = vmask[:, m:m + 1]
                pr.emit("dve", lambda v, p=p, dst=dst, vm=vm: v.tensor_scalar(
                    out=dst, in0=p[:, :], scalar1=vm, scalar2=None,
                    op0=ALU.mult), reads=[p, vmask], writes=[v_sb])
        load_wattn(c_q)
        nsl = max(CT * 128 // 512, 1)
        with nc.psum_tensor(f"qp_{c}", [128, 8 * 512], F32) as qp:
            for s in range(nsl):
                wdt = min(512, CT * 128 - s * 512)
                xc = 128 + s * 512
                for ic in range(8):
                    fns = []
                    for dc in range(DCN):
                        fns.append(lambda T, ic=ic, dc=dc, xc=xc, wdt=wdt:
                                   T.matmul(
                            qp[:, ic * 512:ic * 512 + wdt],
                            wattn[:, ic * 1024 + dc * 128:
                                  ic * 1024 + dc * 128 + 128],
                            xnT[:, dc * XNC + xc:dc * XNC + xc + wdt],
                            start=(dc == 0), stop=(dc == DCN - 1)))
                    pr.emit("pe", fns, reads=[wattn, xnT], writes=[qp])
                for ic in range(8):
                    qb = qb8T[:, ic:ic + 1]
                    dst = xnT[:, ic * XNC + xc:ic * XNC + xc + wdt]
                    pr.emit("dve", lambda v, ic=ic, dst=dst, qb=qb, wdt=wdt:
                            v.tensor_scalar(
                        out=dst, in0=qp[:, ic * 512:ic * 512 + wdt],
                        scalar1=0.125, scalar2=qb, op0=ALU.mult, op1=ALU.add),
                        reads=[qp, qb8T], writes=[xnT])

    def attn_phase(c):
        load_wattn(c_o)
        with nc.psum_tensor(f"Sp0_{c}", [128, 192], F32) as S0, \
             nc.psum_tensor(f"Sp1_{c}", [128, 192], F32) as S1, \
             nc.psum_tensor(f"ETp0_{c}", [128, 512], BF16) as E0, \
             nc.psum_tensor(f"ETp1_{c}", [128, 512], BF16) as E1, \
             nc.psum_tensor(f"ATp0_{c}", [128, 128], F32) as A0, \
             nc.psum_tensor(f"ATp1_{c}", [128, 128], F32) as A1, \
             nc.psum_tensor(f"Op_{c}", [128, 1024], F32) as Op:
            Sp = [S0, S1]
            Ep = [E0, E1]
            Ap = [A0, A1]
            import os
            ASTOP = os.environ.get("ASTOP", "")
            for t in range(CT):
                qcol = 128 + t * 128
                for hp in range(8):
                    r = (t * 8 + hp) % 2
                    Epp = Ep[r]
                    App = Ap[r]
                    kpos = t * 128
                    for j in (0, 1):
                        sub = j * 64
                        pr.emit("pe", lambda T, j=j, hp=hp, qcol=qcol,
                                kpos=kpos, sub=sub: T.matmul(
                            Sp[j][:, :],
                            xnT[sub:sub + 64,
                                hp * XNC + qcol:hp * XNC + qcol + 128],
                            kT[sub:sub + 64,
                               hp * KC + kpos:hp * KC + kpos + 192],
                            start=True, stop=True),
                            reads=[xnT, kT], writes=[Sp[j]])
                    if ASTOP == "s0":
                        continue
                    pr.emit("dve", [
                        lambda v, r=r, j=j: v.tensor_tensor(
                            out=Sm[r][:, j * 192:(j + 1) * 192],
                            in0=Sp[j][:, :], in1=band[:, :], op=ALU.add)
                        for j in (0, 1)],
                        reads=[Sp[0], Sp[1], band], writes=[Sm[r]])
                    if ASTOP == "s":
                        continue
                    pr.emit("dve", lambda v, r=r: v.reduce_max(
                        out=astat[:, 2 * r:2 * r + 2],
                        in_=Sm[r][:, :].rearrange("p (h w) -> p h w", h=2),
                        axis=AX.X, negate=True),
                        reads=[Sm[r]], writes=[astat])
                    if ASTOP == "m":
                        continue
                    pr.emit("act", [
                        lambda a, r=r, j=j: a.activation(
                            out=Eb[r][:, j * 192:(j + 1) * 192],
                            in_=Sm[r][:, j * 192:(j + 1) * 192], func=AF.Exp,
                            bias=astat[:, 2 * r + j:2 * r + j + 1])
                        for j in (0, 1)],
                        reads=[Sm[r], astat], writes=[Eb[r]])
                    pr.emit("dve", lambda v, r=r: v.reduce_sum(
                        out=astat[:, 4 + 2 * r:6 + 2 * r],
                        in_=Eb[r][:, :].rearrange("p (h w) -> p h w", h=2),
                        axis=AX.X), reads=[Eb[r]], writes=[astat])
                    pr.emit("dve", lambda v, r=r: v.reciprocal(
                        out=astat[:, 2 * r:2 * r + 2],
                        in_=astat[:, 4 + 2 * r:6 + 2 * r]),
                        reads=[astat], writes=[astat])
                    import os
                    if not os.environ.get("SKIP_SCALE"):
                        pr.emit("dve", [
                            lambda v, r=r, j=j: v.tensor_scalar(
                                out=Eb[r][:, j * 192:(j + 1) * 192],
                                in0=Eb[r][:, j * 192:(j + 1) * 192],
                                scalar1=astat[:, 2 * r + j:2 * r + j + 1],
                                scalar2=None, op0=ALU.mult)
                            for j in (0, 1)],
                            reads=[Eb[r], astat], writes=[Eb[r]])
                    if ASTOP == "e":
                        continue
                    pr.emit("pe", [f for j in (0, 1) for f in (
                        lambda T, Epp=Epp, r=r, j=j: T.transpose(
                            Epp[:, j * 256:j * 256 + 128],
                            Eb[r][:, j * 192:j * 192 + 128], identb[:, :]),
                        lambda T, Epp=Epp, r=r, j=j: T.transpose(
                            Epp[0:64, j * 256 + 128:j * 256 + 256],
                            Eb[r][:, j * 192 + 128:j * 192 + 192],
                            identb[:, :]))],
                        reads=[Eb[r], identb], writes=[Epp])
                    pr.emit("dve", [
                        lambda v, Epp=Epp, r=r, j=j: v.tensor_copy(
                            out=ETb[r][:, j * 256:j * 256 + 128],
                            in_=Epp[:, j * 256:j * 256 + 128])
                        for j in (0, 1)] + [
                        lambda v, Epp=Epp, r=r, j=j: v.tensor_copy(
                            out=ETb[r][0:64, j * 256 + 128:j * 256 + 256],
                            in_=Epp[0:64, j * 256 + 128:j * 256 + 256])
                        for j in (0, 1)],
                        reads=[Epp], writes=[ETb[r]])
                    if ASTOP == "t":
                        continue
                    pvf = []
                    for j in (0, 1):
                        h = 2 * hp + j
                        sub = j * 64
                        pvf.append(lambda T, App=App, r=r, t=t, h=h, sub=sub,
                                   j=j: T.matmul(
                            App[sub:sub + 64, :],
                            v_sb[:, t * 1024 + h * 64:t * 1024 + h * 64 + 64],
                            ETb[r][:, j * 256:j * 256 + 128],
                            start=True, stop=False))
                        pvf.append(lambda T, App=App, r=r, t=t, h=h, sub=sub,
                                   j=j: T.matmul(
                            App[sub:sub + 64, :],
                            v_sb[0:64, (t + 1) * 1024 + h * 64:
                                 (t + 1) * 1024 + h * 64 + 64],
                            ETb[r][0:64, j * 256 + 128:j * 256 + 256],
                            start=False, stop=True))
                    pr.emit("pe", pvf, reads=[v_sb, ETb[r]], writes=[App])
                    pr.emit("dve", lambda v, App=App, hp=hp: v.tensor_copy(
                        out=aT_sb[:, hp * 128:hp * 128 + 128],
                        in_=App[:, :]), reads=[App], writes=[aT_sb])
                fns = []
                for fc in range(8):
                    for hf in range(2):
                        fns.append(lambda T, fc=fc, hf=hf: T.matmul(
                            Op[:, hf * 512:(hf + 1) * 512],
                            aT_sb[:, fc * 128:fc * 128 + 128],
                            wattn[:, fc * 1024 + hf * 512:
                                  fc * 1024 + hf * 512 + 512],
                            start=(fc == 0), stop=False))
                for hf in range(2):
                    fns.append(lambda T, hf=hf: T.matmul(
                        Op[:, hf * 512:(hf + 1) * 512], ones_bf[:, :],
                        ob_bf[:, hf * 512:(hf + 1) * 512],
                        start=False, stop=True))
                pr.emit("pe", fns, reads=[aT_sb, wattn, ones_bf, ob_bf],
                        writes=[Op])
                slot = t + 1
                x_t = x_sb[:, slot * 1024:(slot + 1) * 1024]
                pr.emit("dve", lambda v, x_t=x_t: v.tensor_tensor(
                    out=x_t, in0=x_t, in1=Op[:, :], op=ALU.add),
                    reads=[x_sb, Op], writes=[x_sb])

    def store_and_carry(c):
        for slot in range(1, SLOTS - 1):
            row0 = (c * CT + slot - 1) * 128
            dst = d_y[row0:row0 + 128, :]
            src = x_sb[:, slot * 1024:(slot + 1) * 1024]
            pr.emit("act", lambda a, dst=dst, src=src: a.dma_start(
                out=dst, in_=src), reads=[x_sb], writes=[d_y], dma="Q_act")
        if c < cfg.NCHUNK - 1:
            pr.emit("dve", lambda v: v.tensor_copy(
                out=x_sb[:, 0:1024],
                in_=x_sb[:, (SLOTS - 2) * 1024:(SLOTS - 1) * 1024]),
                reads=[x_sb], writes=[x_sb])
            pr.emit("dve", lambda v: v.tensor_copy(
                out=x_sb[:, 1024:2048],
                in_=x_sb[:, (SLOTS - 1) * 1024:SLOTS * 1024]),
                reads=[x_sb], writes=[x_sb])

    def stop(name):
        return stop_after == name

    for c in range(cfg.NCHUNK):
        ffn1_slots = list(range(SLOTS)) if c == 0 else list(range(2, SLOTS))
        ffn2_slots = list(range(1, SLOTS - 1))

        if stop("conv"):
            break
        if c > 0:
            pr.phase(f"chunk{c}")
            load_x_slots(c, ffn1_slots)
        pr.claim(WB_ALL)
        if c == 0:
            w_prefetch(0, 0, 0)
        if stop("ln1load"):
            break
        pr.claim(XN_ALL)
        init_toff()
        for i, slot in enumerate(ffn1_slots):
            ln_tile(slot, do_gate=True, tag=f"a_{c}_{slot}", par=i % 2)
        if stop("ln1"):
            break

        pr.claim(WB_ALL)
        ffn_pass(ffn1_slots, lambda: w_prefetch(0, 0, 0), tag=f"f1_{c}")
        if stop("ffn1"):
            break

        pr.claim(WB_ALL)
        pr.claim(XN_ALL)
        pr.emit("sp", lambda _: _.dma_start(out=vb_bf[:, :], in_=c_vbo[0:1, :]),
                reads=[c_vbo], writes=[vb_bf], dma="Q_sp")
        pr.emit("sp", lambda _: _.dma_start(out=ob_bf[:, :], in_=c_vbo[1:2, :]),
                reads=[c_vbo], writes=[ob_bf], dma="Q_sp")
        chunk_masks(c)
        qkv_phase(c)
        if stop("qkv"):
            break

        attn_phase(c)
        if stop("attn"):
            break

        pr.claim(WB_ALL)
        pr.claim(XN_ALL)
        init_toff()
        for i, slot in enumerate(ffn2_slots):
            ln_tile(slot, do_gate=True, tag=f"b_{c}_{slot}", par=i % 2)
        if stop("ln3"):
            break

        pr.claim(WB_ALL)
        nxt = (lambda: w_prefetch(0, 0, 0)) if c < cfg.NCHUNK - 1 else None
        ffn_pass(ffn2_slots, nxt, tag=f"f2_{c}")
        if stop("ffn2"):
            break

        store_and_carry(c)
        if debug_stage == c:
            for slot in range(SLOTS):
                dst = d_dbg[slot * 128:(slot + 1) * 128, :]
                src = x_sb[:, slot * 1024:(slot + 1) * 1024]
                pr.emit("act", lambda a, dst=dst, src=src: a.dma_start(
                    out=dst, in_=src), reads=[x_sb], writes=["dbg"],
                    dma="Q_act")
            break

    pr.phase("fin")
    pr.finish()
    return nc



def _shard(x, cfg, ncores=8):
    B, S, _ = x.shape
    own = cfg.OT * 128
    nq = S // own
    halo = 128
    slabs = []
    for core in range(ncores):
        b, q = divmod(core, nq)
        lo = q * own - halo
        hi = q * own + own + halo
        slab = np.zeros((own + 2 * halo, D), np.float32)
        m = np.zeros((own + 2 * halo,), ml_dtypes.bfloat16)
        s0, s1 = max(lo, 0), min(hi, S)
        slab[s0 - lo:s1 - lo] = x[b, s0:s1]
        m[s0 - lo:s1 - lo] = 1.0
        slabs.append((np.ascontiguousarray(slab), m))
    return slabs


def _weight_map(L0_attn_q_w, L0_attn_q_b, L0_attn_k_w, L0_attn_k_b,
                L0_attn_v_w, L0_attn_v_b, L0_attn_o_w, L0_attn_o_b,
                L0_ffn_W1, L0_ffn_b1, L0_ffn_W2, L0_ffn_b2, L0_ffn_Wg,
                L0_ffn_bg):
    f = lambda a: np.ascontiguousarray(np.asarray(a, np.float32))
    return {
        "q_w": f(L0_attn_q_w), "q_b": f(L0_attn_q_b),
        "k_w": f(L0_attn_k_w), "k_b": f(L0_attn_k_b),
        "v_w": f(L0_attn_v_w), "v_b": f(L0_attn_v_b),
        "o_w": f(L0_attn_o_w), "o_b": f(L0_attn_o_b),
        "W1": f(L0_ffn_W1), "b1": f(L0_ffn_b1),
        "W2": f(L0_ffn_W2), "b2": f(L0_ffn_b2),
        "Wg": f(L0_ffn_Wg), "bg": f(L0_ffn_bg),
    }


def kernel(L0_attn_hidden_states, **w):
    cfg = Cfg(own_tiles=32, chunk_tiles=8)
    nc = build_program(cfg)
    x = np.asarray(L0_attn_hidden_states, np.float32)
    slabs = _shard(x, cfg)
    weights = _weight_map(**w)
    in_maps = []
    for core in range(8):
        slab, m = slabs[core]
        d = dict(weights)
        d["x_slab"] = slab
        d["kvmask"] = m
        in_maps.append(d)
    res = run_bass_kernel_spmd(nc, in_maps, core_ids=list(range(8)))
    out = np.empty((2, 16384, 1024), np.float32)
    own = cfg.OT * 128
    for core in range(8):
        b, q = divmod(core, 4)
        out[b, q * own:(q + 1) * own] = res.results[core]["y"]
    return out
